# version 03
# baseline (speedup 1.0000x reference)
import numpy as np
from contextlib import ExitStack

import concourse.bass as bass
import concourse.mybir as mybir
from concourse.bass_utils import run_bass_kernel_spmd

F32 = mybir.dt.float32
BF16 = mybir.dt.bfloat16
U8 = mybir.dt.uint8
AF = mybir.ActivationFunctionType
ALU = mybir.AluOpType
AX = mybir.AxisListType.X

NCORE = 8
D = 1024
KT = 8
GAMMA = 0.96875
LN_EPS = 1e-5
RMS_EPS = float(np.finfo(np.float32).eps)
E_MOE = 16
DFF = 512
KF = 4
NEXP = 18
DHID = 4096
KH = 32
SEG = 4096


def _ap_range(ap):
    sz = mybir.dt.size(ap.dtype)
    dims = list(ap.ap)
    is_dram = "DRam" in type(ap.tensor).__name__
    off = int(ap.offset)
    if not is_dram:
        row = 1
        for d in list(ap.tensor.shape)[1:]:
            row *= int(d)
        off = off % row
    use = dims if is_dram else dims[1:]
    ext = 0
    for step, count in use:
        ext += abs(int(step)) * (int(count) - 1)
    lo = off * sz
    return ap.tensor.name, lo, lo + (ext + 1) * sz


class Prog:
    ENGS = ("sync", "gpsimd", "vector", "scalar", "tensor")
    NQ = 8
    SEMS = tuple(["gpsimd", "vector", "scalar", "tensor"]
                 + [f"sy{i}" for i in range(8)] + [f"gd{i}" for i in range(8)])

    def __init__(self):
        self.items = []
        self.cnt = dict.fromkeys(self.SEMS, 0)
        self.dmai = {"sync": 0, "gpsimd": 0}
        self.wseg = {}
        self.rseg = {}

    def _segs(self, lo, hi):
        return range(lo // SEG, (hi - 1) // SEG + 1)

    def op(self, eng, fn, reads=(), writes=(), deps=(), inc=1, rank=False,
           noself_tensor=False, sem=None):
        sem = sem or (eng if eng in self.cnt else "sy0")
        dep = set(deps)
        for ap in reads:
            name, lo, hi = _ap_range(ap)
            wm = self.wseg.get(name)
            if wm:
                for s in self._segs(lo, hi):
                    t = wm.get(s)
                    if t:
                        dep.add(t)
        for ap in writes:
            name, lo, hi = _ap_range(ap)
            wm = self.wseg.get(name)
            rm = self.rseg.get(name)
            for s in self._segs(lo, hi):
                if wm and (t := wm.get(s)):
                    dep.add(t)
                if rm:
                    dep.update(rm.get(s, ()))
        if noself_tensor:
            dep = {d for d in dep if d[0] != "tensor"}
        self.cnt[sem] += inc
        tok = (sem, self.cnt[sem])
        self.items.append((eng, sem, fn, sorted(dep), inc, rank))
        for ap in reads:
            name, lo, hi = _ap_range(ap)
            rm = self.rseg.setdefault(name, {})
            for s in self._segs(lo, hi):
                rm.setdefault(s, set()).add(tok)
        for ap in writes:
            name, lo, hi = _ap_range(ap)
            wm = self.wseg.setdefault(name, {})
            rm = self.rseg.setdefault(name, {})
            for s in self._segs(lo, hi):
                wm[s] = tok
                rm[s] = set()
        return tok

    def dma(self, eng, fn, reads=(), writes=(), deps=()):
        idx = self.dmai[eng]
        self.dmai[eng] = idx + 1
        pre = "sy" if eng == "sync" else "gd"
        sem = f"pre{idx % self.NQ}".replace("pre", pre)
        deps = list(deps)
        if idx >= self.NQ:
            deps.append((sem, 16 * (idx // self.NQ)))
        return self.op(eng, fn, reads, writes, deps, inc=16, sem=sem)

    def queue_total_deps(self, eng):
        pre = "sy" if eng == "sync" else "gd"
        return [(f"{pre}{i}", self.cnt[f"{pre}{i}"]) for i in range(self.NQ)]

    def emit(self, nc, ctx):
        sems = {e: ctx.enter_context(nc.semaphore(f"pg_{e}")) for e in self.SEMS}
        block = ctx.enter_context(nc.Block())
        items = self.items

        def mk(eng_name):
            def body(eng):
                waited = {}
                for (e, sm, fn, deps, inc, rank) in items:
                    if e != eng_name:
                        continue
                    for (de, dc) in deps:
                        if waited.get(de, 0) >= dc:
                            continue
                        eng.wait_ge(sems[de], dc)
                        waited[de] = dc
                    if rank:
                        for arm in eng.switch_core_id(n=128):
                            insts = fn(eng, arm.logical % NCORE)
                            if not isinstance(insts, (list, tuple)):
                                insts = [insts]
                            assert len(insts) == inc
                            for i in insts:
                                i.then_inc(sems[sm], 1)
                    elif inc == 0:
                        fn(eng)
                    else:
                        fn(eng).then_inc(sems[sm], inc)
            return body

        for e in self.ENGS:
            getattr(block, e)(mk(e))


class Pool:

    def __init__(self, nc, ctx, name, nbytes):
        assert nbytes % 4 == 0
        self.t = ctx.enter_context(nc.sbuf_tensor(name, [128, nbytes // 4], F32))

    def view(self, off, dtype, shape):
        assert off % 4 == 0
        n = int(np.prod(shape))
        if dtype == F32:
            ap = self.t[:, off // 4:off // 4 + n]
        elif dtype == BF16:
            assert n % 2 == 0
            ap = self.t[:, off // 4:off // 4 + n // 2].bitcast(BF16)
        else:
            raise ValueError(dtype)
        if len(shape) == 1:
            return ap
        pat = "p (" + " ".join(f"d{i}" for i in range(len(shape))) + ") -> p " + \
              " ".join(f"d{i}" for i in range(len(shape)))
        return ap.rearrange(pat, **{f"d{i}": s for i, s in enumerate(shape)})


def _layer_norm(P, xr, xrb, sqb, ones_bf, scs, PSA, PSB, mu_i=4, rs_i=5):
    P.op("scalar", lambda s: s.copy(xrb[:], xr[:]), reads=[xr[:]],
         writes=[xrb[:]])
    P.op("scalar", lambda s: s.activation(sqb[:], xrb[:], AF.Square),
         reads=[xrb[:]], writes=[sqb[:]])
    for kt in range(KT):
        P.op("tensor", lambda t, kt=kt: t.matmul(
            PSA[:, 3, :], ones_bf[:], xrb[:, kt, :],
            start=(kt == 0), stop=(kt == KT - 1)),
            reads=[ones_bf[:], xrb[:, kt, :]], writes=[PSA[:, 3, :]],
            noself_tensor=True)
        P.op("tensor", lambda t, kt=kt: t.matmul(
            PSB[:, 2, :], ones_bf[:], sqb[:, kt, :],
            start=(kt == 0), stop=(kt == KT - 1)),
            reads=[ones_bf[:], sqb[:, kt, :]], writes=[PSB[:, 2, :]],
            noself_tensor=True)
    P.op("vector", lambda v: v.tensor_scalar(
        scs[:, mu_i, :], PSA[:, 3, :], 1.0 / D, None, ALU.mult),
        reads=[PSA[:, 3, :]], writes=[scs[:, mu_i, :]])
    P.op("scalar", lambda s: s.activation(
        scs[:, rs_i, :], scs[:, mu_i, :], AF.Square),
        reads=[scs[:, mu_i, :]], writes=[scs[:, rs_i, :]])
    P.op("vector", lambda v: v.scalar_tensor_tensor(
        scs[:, rs_i, :], PSB[:, 2, :], 1.0 / D, scs[:, rs_i, :],
        ALU.mult, ALU.subtract),
        reads=[PSB[:, 2, :], scs[:, rs_i, :]], writes=[scs[:, rs_i, :]])
    P.op("vector", lambda v: v.tensor_scalar_add(
        scs[:, rs_i, :], scs[:, rs_i, :], LN_EPS),
        reads=[scs[:, rs_i, :]], writes=[scs[:, rs_i, :]])
    P.op("scalar", lambda s: s.activation(
        scs[:, rs_i, :], scs[:, rs_i, :], AF.Sqrt),
        reads=[scs[:, rs_i, :]], writes=[scs[:, rs_i, :]])
    P.op("vector", lambda v: v.reciprocal(scs[:, rs_i, :], scs[:, rs_i, :]),
         reads=[scs[:, rs_i, :]], writes=[scs[:, rs_i, :]])
    for kt in range(KT):
        P.op("vector", lambda v, kt=kt: v.tensor_tensor(
            xr[:, kt, :], xr[:, kt, :], scs[:, mu_i, :], ALU.subtract),
            reads=[xr[:, kt, :], scs[:, mu_i, :]], writes=[xr[:, kt, :]])
        P.op("vector", lambda v, kt=kt: v.tensor_tensor(
            xr[:, kt, :], xr[:, kt, :], scs[:, rs_i, :], ALU.mult),
            reads=[xr[:, kt, :], scs[:, rs_i, :]], writes=[xr[:, kt, :]])


def build_launch1(T, debug=False, upto=3):
    TFULL = 8 * T
    assert TFULL // D == 16
    TB = 512
    NTB = T // TB
    NCH = TFULL // T
    JC = T // 16
    NTT = T // 128

    nc = bass.Bass("TRN2", target_bir_lowering=False)
    P = Prog()
    dt = nc.dram_tensor
    xfull = dt("xfull", [128, NCH, KT, T], F32, kind="ExternalInput")
    xloc = dt("xloc", [128, NTB, KT, TB], F32, kind="ExternalInput")
    wqP = dt("wqP", [128, KT, D], F32, kind="ExternalInput")
    wkP = dt("wkP", [128, KT, D], F32, kind="ExternalInput")
    wvP = dt("wvP", [128, KT, D], F32, kind="ExternalInput")
    gateP = dt("gateP", [128, KT, E_MOE], F32, kind="ExternalInput")
    ebias = dt("ebias", [128, E_MOE], F32, kind="ExternalInput")
    we1P = dt("we1P", [NEXP, 128, KT, DFF], F32, kind="ExternalInput")
    we3P = dt("we3P", [NEXP, 128, KT, DFF], F32, kind="ExternalInput")
    we2P = dt("we2P", [NEXP, 128, KF, D], F32, kind="ExternalInput")
    b1p = dt("b1p", [128, NEXP * KF], F32, kind="ExternalInput")
    b3p = dt("b3p", [128, NEXP * KF], F32, kind="ExternalInput")
    b2xT = dt("b2xT", [17, D], F32, kind="ExternalInput")
    wrP = dt("wrP", [128, KT, D], F32, kind="ExternalInput")
    wk2P = dt("wk2P", [128, KT, D], F32, kind="ExternalInput")
    wv2P = dt("wv2P", [128, KT, D], F32, kind="ExternalInput")
    wwP = dt("wwP", [128, KT, D], F32, kind="ExternalInput")

    x2_out = dt("x2_out", [128, NTB, KT, TB], F32, kind="ExternalOutput")
    r_out = dt("r_out", [128, NTB, KT, TB], F32, kind="ExternalOutput")
    w_out = dt("w_out", [128, NTB, KT, TB], F32, kind="ExternalOutput")
    z_out = dt("z_out", [128, NTB, KT, TB], F32, kind="ExternalOutput")
    ex_out = dt("ex_out", [128, 16], F32, kind="ExternalOutput")
    if debug:
        x1_dram = dt("x1_dram", [128, NTB, KT, TB], F32, kind="ExternalOutput")
        psel_out = dt("psel_out", [128, KT, T], BF16, kind="ExternalOutput")
    else:
        x1_dram = dt("x1_dram", [128, NTB, KT, TB], F32)
    x2b_dram = dt("x2b_dram", [128, NTB, KT, TB], BF16)
    g_dram = dt("g_dram", [NEXP, T], F32)

    with ExitStack() as ctx:
        PB = Pool(nc, ctx, "PB", 176 * 1024)
        R_AP, R_SF, R_XB, R_PS = 0, 64 * 1024, 128 * 1024, 144 * 1024

        sb = lambda n, s, d: ctx.enter_context(nc.sbuf_tensor(n, s, d))
        ident = sb("ident", [128, 128], F32)
        ones_bf = sb("ones_bf", [128, 128], BF16)
        bdiag = sb("bdiag", [128, 128], BF16)
        gamma1 = sb("gamma1", [128, 1], F32)
        onesrow = sb("onesrow", [1, 512], BF16)
        wvs = sb("wvs", [128, KT], F32)
        wcol = sb("wcol", [128, KT, 128], BF16)
        carryA = sb("carryA", [128, KT], F32)
        scs = sb("scs", [128, 6, TB], F32)
        b1s = sb("b1s", [128, NEXP * KF], F32)
        b3s = sb("b3s", [128, NEXP * KF], F32)
        ebias_s = sb("ebias_s", [128, E_MOE], F32)
        glog = sb("glog", [128, NTT, E_MOE], F32)
        gsig = sb("gsig", [128, NTT, E_MOE], F32)
        gbia = sb("gbia", [128, NTT, E_MOE], F32)
        gmk1 = sb("gmk1", [128, NTT, E_MOE], F32)
        gmk2 = sb("gmk2", [128, NTT, E_MOE], F32)
        gwv = sb("gwv", [128, NTT, E_MOE], F32)
        gm = sb("gm", [128, NTT, 4], F32)
        b2xs = sb("b2xs", [17, D], BF16)
        gb = sb("gb", [128, 1, TB], F32)
        hbuf = sb("hbuf", [128, 1, KF, TB], BF16)
        carryU = sb("carryU", [128, KT], F32)
        carryP = sb("carryP", [128, KT], F32)
        ex1 = sb("ex1", [128, 16], F32)

        PSA = ctx.enter_context(nc.psum_tensor("psa", [128, 4, 512], F32))
        PSB = ctx.enter_context(nc.psum_tensor("psb", [128, 4, 512], F32))
        PST = PSB[:, 3, 0:128]

        P.op("gpsimd", lambda g: g.memset(ident[:], 0.0), writes=[ident[:]])
        P.op("gpsimd", lambda g: g.affine_select(
            out=ident[:], in_=ident[:], compare_op=ALU.not_equal, fill=1.0,
            base=0, pattern=[[-1, 128]], channel_multiplier=1),
            reads=[ident[:]], writes=[ident[:]])
        P.op("gpsimd", lambda g: g.memset(ones_bf[:], 1.0), writes=[ones_bf[:]])
        P.op("gpsimd", lambda g: g.memset(bdiag[:], 0.0), writes=[bdiag[:]])
        P.op("gpsimd", lambda g: g.memset(bdiag[0:64, 0:64], 1.0),
             reads=[bdiag[:]], writes=[bdiag[0:64, 0:64]])
        P.op("gpsimd", lambda g: g.memset(bdiag[64:128, 64:128], 1.0),
             reads=[bdiag[:]], writes=[bdiag[64:128, 64:128]])
        P.op("gpsimd", lambda g: g.memset(gamma1[:], GAMMA), writes=[gamma1[:]])
        P.op("gpsimd", lambda g: g.memset(onesrow[:], 1.0), writes=[onesrow[:]])
        P.dma("sync", lambda s: s.dma_start(out=b1s[:], in_=b1p.ap()),
              writes=[b1s[:]])
        P.dma("sync", lambda s: s.dma_start(out=b3s[:], in_=b3p.ap()),
              writes=[b3s[:]])
        P.dma("sync", lambda s: s.dma_start(out=ebias_s[:], in_=ebias.ap()),
              writes=[ebias_s[:]])
        P.dma("gpsimd", lambda g: g.dma_start(out=b2xs[:], in_=b2xT.ap()),
              writes=[b2xs[:]])

        wv_slab = PB.view(R_AP, F32, (KT, D))
        for kt in range(KT):
            P.dma("sync", lambda s, kt=kt: s.dma_start(
                out=wv_slab[:, kt, :], in_=wvP.ap()[:, kt, :]),
                writes=[wv_slab[:, kt, :]])
            P.op("vector", lambda v, kt=kt: v.reduce_sum(
                wvs[:, kt:kt + 1], wv_slab[:, kt, :], axis=AX),
                reads=[wv_slab[:, kt, :]], writes=[wvs[:, kt:kt + 1]])
        for kt in range(KT):
            P.op("vector", lambda v, kt=kt: v.tensor_scalar(
                wcol[:, kt, :], ones_bf[:], wvs[:, kt:kt + 1], None, ALU.mult),
                reads=[ones_bf[:], wvs[:, kt:kt + 1]], writes=[wcol[:, kt, :]])

        xb_ch = PB.view(R_SF, BF16, (KT, T))
        APbuf = PB.view(R_AP, F32, (KT, T))
        psel = PB.view(R_PS, BF16, (KT, T))
        ap4 = APbuf.rearrange("p k (j a) -> p k j a", a=16)

        for ct in range(NCH):
            P.dma("gpsimd", lambda g, ct=ct: g.dma_start(
                out=xb_ch, in_=xfull.ap()[:, ct]), writes=[xb_ch])
            for tb in range(NTB):
                sl = slice(tb * TB, (tb + 1) * TB)
                for kt in range(KT):
                    P.op("tensor", lambda t, kt=kt, sl=sl: t.matmul(
                        PSA[:, 0, :], wcol[:, kt, :], xb_ch[:, kt, sl],
                        start=(kt == 0), stop=(kt == KT - 1)),
                        reads=[wcol[:, kt, :], xb_ch[:, kt, sl]],
                        writes=[PSA[:, 0, :]], noself_tensor=True)
                for kt in range(KT):
                    P.op("vector", lambda v, kt=kt, sl=sl: v.tensor_tensor(
                        APbuf[:, kt, sl], xb_ch[:, kt, sl], PSA[:, 0, :],
                        ALU.mult),
                        reads=[xb_ch[:, kt, sl], PSA[:, 0, :]],
                        writes=[APbuf[:, kt, sl]])
            if ct == 0:
                P.op("vector", lambda v: v.memset(APbuf[:, :, 0:1], 0.0),
                     writes=[APbuf[:, :, 0:1]])
            for kt in range(KT):
                init = 0.0 if ct == 0 else carryA[:, kt:kt + 1]
                rd = [gamma1[:], APbuf[:, kt, :]]
                if ct > 0:
                    rd.append(carryA[:, kt:kt + 1])
                P.op("vector", lambda v, kt=kt, init=init: v.tensor_tensor_scan(
                    APbuf[:, kt, :], gamma1[:].broadcast_to([128, T]),
                    APbuf[:, kt, :], init,
                    ALU.mult, ALU.add), reads=rd, writes=[APbuf[:, kt, :]])
                P.op("vector", lambda v, kt=kt: v.tensor_copy(
                    carryA[:, kt:kt + 1], APbuf[:, kt, T - 1:T]),
                    reads=[APbuf[:, kt, T - 1:T]], writes=[carryA[:, kt:kt + 1]])
            if ct == 0:
                P.op("vector", lambda v: v.tensor_copy(
                    APbuf[:, :, 0:1], APbuf[:, :, 1:2]),
                    reads=[APbuf[:, :, 1:2]], writes=[APbuf[:, :, 0:1]])

            def ext(vv, r, ct=ct):
                i0 = vv.tensor_copy(psel[:, :, ct * JC:(ct + 1) * JC],
                                    ap4[:, :, :, 2 * r])
                i1 = vv.tensor_copy(
                    psel[:, :, T // 2 + ct * JC:T // 2 + (ct + 1) * JC],
                    ap4[:, :, :, 2 * r + 1])
                return [i0, i1]
            P.op("vector", ext, reads=[APbuf[:, :, :]],
                 writes=[psel[:, :, ct * JC:(ct + 1) * JC],
                         psel[:, :, T // 2 + ct * JC:T // 2 + (ct + 1) * JC]],
                 rank=True, inc=2)

        if debug and upto == 0:
            P.dma("sync", lambda s: s.dma_start(out=psel_out.ap(), in_=psel),
                  reads=[psel])
            P.op("sync", lambda s: s.nop(), deps=P.queue_total_deps("sync"),
                 inc=0)
            P.emit(nc, ctx)
            return nc

        wk_sb = PB.view(R_AP, BF16, (KT, D))
        wq_sb = PB.view(R_AP + 16 * 1024, BF16, (KT, D))
        xb_tb = PB.view(R_AP + 32 * 1024, BF16, (KT, TB))
        xrb = PB.view(R_AP + 40 * 1024, BF16, (KT, TB))
        sqb = PB.view(R_AP + 48 * 1024, BF16, (KT, TB))
        sstage = PB.view(R_AP + 56 * 1024, F32, (TB,))
        sfin = PB.view(R_SF, F32, (KT, T))
        xloc_f = PB.view(R_XB, F32, (KT, TB))
        x1b = PB.view(R_PS, BF16, (KT, T))

        P.dma("gpsimd", lambda g: g.dma_start(out=wk_sb, in_=wkP.ap()),
              writes=[wk_sb])
        P.dma("gpsimd", lambda g: g.dma_start(out=wq_sb, in_=wqP.ap()),
              writes=[wq_sb])

        for bkt in range(KT):
            for sbk in range(T // TB):
                pb = PSA[:, sbk % 2, :]
                for kin in range(KT):
                    P.op("tensor", lambda t, bkt=bkt, sbk=sbk, kin=kin, pb=pb:
                         t.matmul(pb, wk_sb[:, kin, bkt * 128:(bkt + 1) * 128],
                                  psel[:, kin, sbk * TB:(sbk + 1) * TB],
                                  start=(kin == 0), stop=(kin == KT - 1)),
                         reads=[wk_sb[:, kin, bkt * 128:(bkt + 1) * 128],
                                psel[:, kin, sbk * TB:(sbk + 1) * TB]],
                         writes=[pb], noself_tensor=True)
                P.op("scalar", lambda s, pb=pb: s.copy(sstage[:], pb),
                     reads=[pb], writes=[sstage[:]])
                for j in range(TB // 128):
                    s_abs = sbk * TB + j * 128
                    a_loc = s_abs // (T // 2)
                    ikt = (s_abs % (T // 2)) // 128
                    col = a_loc * (T // 2) + bkt * 128
                    P.op("tensor", lambda t, j=j: t.transpose(
                        PST, sstage[:, j * 128:(j + 1) * 128], ident[:]),
                        reads=[sstage[:, j * 128:(j + 1) * 128], ident[:]],
                        writes=[PST])
                    P.op("vector", lambda v, ikt=ikt, col=col: v.tensor_copy(
                        sfin[:, ikt, col:col + 128], PST),
                        reads=[PST], writes=[sfin[:, ikt, col:col + 128]])

        for tb in range(NTB):
            sl = slice(tb * TB, (tb + 1) * TB)
            P.dma("sync", lambda s, tb=tb: s.dma_start(
                out=xloc_f, in_=xloc.ap()[:, tb]), writes=[xloc_f])
            P.dma("gpsimd", lambda g, tb=tb: g.dma_start(
                out=xb_tb, in_=xloc.ap()[:, tb]), writes=[xb_tb])
            xr = xloc_f
            for kto in range(KT):
                for kin in range(KT):
                    P.op("tensor", lambda t, kto=kto, kin=kin: t.matmul(
                        PSA[:, 2, :], wq_sb[:, kin, kto * 128:(kto + 1) * 128],
                        xb_tb[:, kin, :], start=(kin == 0),
                        stop=(kin == KT - 1)),
                        reads=[wq_sb[:, kin, kto * 128:(kto + 1) * 128],
                               xb_tb[:, kin, :]],
                        writes=[PSA[:, 2, :]], noself_tensor=True)
                P.op("vector", lambda v, kto=kto, sl=sl: v.scalar_tensor_tensor(
                    scs[:, 0, :], PSA[:, 2, :], 1.0, sfin[:, kto, sl],
                    ALU.mult, ALU.mult),
                    reads=[PSA[:, 2, :], sfin[:, kto, sl]],
                    writes=[scs[:, 0, :]])
                P.op("vector", lambda v, kto=kto: v.tensor_add(
                    xr[:, kto, :], xr[:, kto, :], scs[:, 0, :]),
                    reads=[xr[:, kto, :], scs[:, 0, :]], writes=[xr[:, kto, :]])
            _layer_norm(P, xr, xrb, sqb, ones_bf, scs, PSA, PSB)
            P.dma("sync", lambda s, tb=tb: s.dma_start(
                out=x1_dram.ap()[:, tb], in_=xr[:]), reads=[xr[:]])
            P.op("scalar", lambda s, sl=sl: s.copy(x1b[:, :, sl], xr[:]),
                 reads=[xr[:]], writes=[x1b[:, :, sl]])

        if upto == 1:
            P.op("sync", lambda s: s.nop(), deps=P.queue_total_deps("sync"),
                 inc=0)
            P.emit(nc, ctx)
            return nc

        gT_sb = PB.view(R_XB, BF16, (KT, E_MOE))
        gxs = PB.t[0:17, (R_XB + 4096) // 4:(R_XB + 8192) // 4].bitcast(BF16)
        gxf = PB.t[0:16, (R_XB + 8192) // 4:(R_XB + 16384) // 4]
        P.dma("gpsimd", lambda g: g.dma_start(out=gT_sb, in_=gateP.ap()),
              writes=[gT_sb])
        for tt in range(NTT):
            bank = tt % 2
            for kin in range(KT):
                P.op("tensor", lambda t, tt=tt, kin=kin, bank=bank: t.matmul(
                    PSA[:, bank, 0:16],
                    x1b[:, kin, tt * 128:(tt + 1) * 128], gT_sb[:, kin, :],
                    start=(kin == 0), stop=(kin == KT - 1)),
                    reads=[x1b[:, kin, tt * 128:(tt + 1) * 128],
                           gT_sb[:, kin, :]],
                    writes=[PSA[:, bank, 0:16]],
                    noself_tensor=True)
            P.op("vector", lambda v, tt=tt, bank=bank: v.tensor_copy(
                glog[:, tt, :], PSA[:, bank, 0:16]),
                reads=[PSA[:, bank, 0:16]], writes=[glog[:, tt, :]])
        P.op("scalar", lambda s: s.activation(gsig[:], glog[:], AF.Sigmoid),
             reads=[glog[:]], writes=[gsig[:]])
        for tt in range(NTT):
            P.op("vector", lambda v, tt=tt: v.tensor_add(
                gbia[:, tt, :], glog[:, tt, :], ebias_s[:]),
                reads=[glog[:, tt, :], ebias_s[:]], writes=[gbia[:, tt, :]])
            P.op("vector", lambda v, tt=tt: v.reduce_max(
                gm[:, tt, 0:1], gbia[:, tt, :], axis=AX),
                reads=[gbia[:, tt, :]], writes=[gm[:, tt, 0:1]])
            P.op("vector", lambda v, tt=tt: v.tensor_scalar(
                gmk1[:, tt, :], gbia[:, tt, :], gm[:, tt, 0:1], None,
                ALU.is_equal),
                reads=[gbia[:, tt, :], gm[:, tt, 0:1]], writes=[gmk1[:, tt, :]])
            P.op("vector", lambda v, tt=tt: v.scalar_tensor_tensor(
                gmk2[:, tt, :], gmk1[:, tt, :], -1e30, gbia[:, tt, :],
                ALU.mult, ALU.add),
                reads=[gmk1[:, tt, :], gbia[:, tt, :]], writes=[gmk2[:, tt, :]])
            P.op("vector", lambda v, tt=tt: v.reduce_max(
                gm[:, tt, 1:2], gmk2[:, tt, :], axis=AX),
                reads=[gmk2[:, tt, :]], writes=[gm[:, tt, 1:2]])
            P.op("vector", lambda v, tt=tt: v.tensor_scalar(
                gmk2[:, tt, :], gmk2[:, tt, :], gm[:, tt, 1:2], None,
                ALU.is_equal),
                reads=[gmk2[:, tt, :], gm[:, tt, 1:2]], writes=[gmk2[:, tt, :]])
            P.op("vector", lambda v, tt=tt: v.tensor_add(
                gmk1[:, tt, :], gmk1[:, tt, :], gmk2[:, tt, :]),
                reads=[gmk1[:, tt, :], gmk2[:, tt, :]], writes=[gmk1[:, tt, :]])
            P.op("vector", lambda v, tt=tt: v.tensor_mul(
                gwv[:, tt, :], gsig[:, tt, :], gmk1[:, tt, :]),
                reads=[gsig[:, tt, :], gmk1[:, tt, :]], writes=[gwv[:, tt, :]])
            P.op("vector", lambda v, tt=tt: v.reduce_sum(
                gm[:, tt, 2:3], gwv[:, tt, :], axis=AX),
                reads=[gwv[:, tt, :]], writes=[gm[:, tt, 2:3]])
            P.op("vector", lambda v, tt=tt: v.tensor_scalar_add(
                gm[:, tt, 2:3], gm[:, tt, 2:3], 1e-10),
                reads=[gm[:, tt, 2:3]], writes=[gm[:, tt, 2:3]])
            P.op("vector", lambda v, tt=tt: v.reciprocal(
                gm[:, tt, 3:4], gm[:, tt, 2:3]),
                reads=[gm[:, tt, 2:3]], writes=[gm[:, tt, 3:4]])
            P.op("vector", lambda v, tt=tt: v.tensor_scalar(
                gwv[:, tt, :], gwv[:, tt, :], gm[:, tt, 3:4], None, ALU.mult),
                reads=[gwv[:, tt, :], gm[:, tt, 3:4]], writes=[gwv[:, tt, :]])
        gtr = scs[:, 0, 0:128]
        for blkI in range(NTT // 8):
            gblk = gwv.ap().rearrange("p a e -> p (a e)")[
                :, blkI * 128:(blkI + 1) * 128]
            P.op("tensor", lambda t, gblk=gblk: t.transpose(
                PST, gblk, ident[:]),
                reads=[gblk, ident[:]], writes=[PST])
            P.op("vector", lambda v: v.tensor_copy(gtr, PST),
                 reads=[PST], writes=[gtr])
            dap = g_dram.ap()[0:16, blkI * 1024:(blkI + 1) * 1024].rearrange(
                "e (q p) -> q e p", p=128)
            P.dma("sync", lambda s, dap=dap: s.dma_start(out=dap, in_=gtr),
                  reads=[gtr], writes=[g_dram.ap()[0:16,
                                                   blkI * 1024:(blkI + 1) * 1024]])
        for rr in (16, 17):
            for q in range(T // 512):
                P.dma("gpsimd", lambda g, rr=rr, q=q: g.dma_start(
                    out=g_dram.ap()[rr:rr + 1, q * 512:(q + 1) * 512],
                    in_=onesrow[:]),
                    reads=[onesrow[:]],
                    writes=[g_dram.ap()[rr:rr + 1, q * 512:(q + 1) * 512]])
        P.dma("gpsimd", lambda g: g.dma_start(
            out=gxs[:], in_=g_dram.ap()[0:17, :]),
            reads=[g_dram.ap()[0:17, :]], writes=[gxs[:]])

        y_sb = PB.view(R_AP, F32, (KT, T))
        for tb in range(NTB):
            sl = slice(tb * TB, (tb + 1) * TB)
            for half in range(2):
                for k4 in range(4):
                    dk = half * 4 + k4
                    P.op("tensor", lambda t, dk=dk, k4=k4, sl=sl: t.matmul(
                        PSB[:, k4, :], b2xs[:, dk * 128:(dk + 1) * 128],
                        gxs[:, sl], start=True, stop=True),
                        reads=[b2xs[:, dk * 128:(dk + 1) * 128], gxs[:, sl]],
                        writes=[PSB[:, k4, :]], noself_tensor=True)
                P.op("vector", lambda v, half=half, sl=sl: v.tensor_copy(
                    y_sb[:, half * 4:half * 4 + 4, sl], PSB[:]),
                    reads=[PSB[:]], writes=[y_sb[:, half * 4:half * 4 + 4, sl]])

        slab1 = [PB.view(R_SF + i * 24 * 1024, BF16, (KT, DFF)) for i in range(2)]
        slab3 = [PB.view(R_SF + i * 24 * 1024 + 8 * 1024, BF16, (KT, DFF))
                 for i in range(2)]
        slab2 = [PB.view(R_SF + i * 24 * 1024 + 16 * 1024, BF16, (KF, D))
                 for i in range(2)]
        for e in range(NEXP):
            pe = e % 2
            P.dma("gpsimd", lambda g, e=e, pe=pe: g.dma_start(
                out=slab1[pe], in_=we1P.ap()[e]), writes=[slab1[pe]])
            P.dma("gpsimd", lambda g, e=e, pe=pe: g.dma_start(
                out=slab3[pe], in_=we3P.ap()[e]), writes=[slab3[pe]])
            P.dma("gpsimd", lambda g, e=e, pe=pe: g.dma_start(
                out=slab2[pe], in_=we2P.ap()[e]), writes=[slab2[pe]])
            for tb in range(NTB):
                sl = slice(tb * TB, (tb + 1) * TB)
                ph = tb % 2
                P.dma("sync", lambda s, e=e, sl=sl: s.dma_start(
                    out=gb[:, 0:1, :],
                    in_=g_dram.ap()[e:e + 1, sl].partition_broadcast(128)),
                    reads=[g_dram.ap()[e:e + 1, sl]], writes=[gb[:, 0, :]])
                for kf in range(KF):
                    for kin in range(KT):
                        P.op("tensor", lambda t, pe=pe, kf=kf, kin=kin, sl=sl:
                             t.matmul(PSA[:, kf, :],
                                      slab1[pe][:, kin, kf * 128:(kf + 1) * 128],
                                      x1b[:, kin, sl], start=(kin == 0),
                                      stop=(kin == KT - 1)),
                             reads=[slab1[pe][:, kin, kf * 128:(kf + 1) * 128],
                                    x1b[:, kin, sl]],
                             writes=[PSA[:, kf, :]], noself_tensor=True)
                for kf in range(KF):
                    P.op("scalar", lambda s, e=e, kf=kf: s.activation(
                        scs[:, kf, :], PSA[:, kf, :], AF.Silu,
                        bias=b1s[:, e * KF + kf:e * KF + kf + 1]),
                        reads=[PSA[:, kf, :],
                               b1s[:, e * KF + kf:e * KF + kf + 1]],
                        writes=[scs[:, kf, :]])
                for kf in range(KF):
                    for kin in range(KT):
                        P.op("tensor", lambda t, pe=pe, kf=kf, kin=kin, sl=sl:
                             t.matmul(PSB[:, kf, :],
                                      slab3[pe][:, kin, kf * 128:(kf + 1) * 128],
                                      x1b[:, kin, sl], start=(kin == 0),
                                      stop=(kin == KT - 1)),
                             reads=[slab3[pe][:, kin, kf * 128:(kf + 1) * 128],
                                    x1b[:, kin, sl]],
                             writes=[PSB[:, kf, :]], noself_tensor=True)
                for kf in range(KF):
                    P.op("vector", lambda v, e=e, kf=kf, ph=ph:
                         v.scalar_tensor_tensor(
                             scs[:, 4 + (kf % 2), :], PSB[:, kf, :],
                             b3s[:, e * KF + kf:e * KF + kf + 1], gb[:, 0, :],
                             ALU.add, ALU.mult),
                         reads=[PSB[:, kf, :], gb[:, 0, :],
                                b3s[:, e * KF + kf:e * KF + kf + 1]],
                         writes=[scs[:, 4 + (kf % 2), :]])
                    P.op("vector", lambda v, kf=kf, ph=ph: v.tensor_mul(
                        hbuf[:, 0, kf, :], scs[:, kf, :],
                        scs[:, 4 + (kf % 2), :]),
                        reads=[scs[:, kf, :], scs[:, 4 + (kf % 2), :]],
                        writes=[hbuf[:, 0, kf, :]])
                for half in range(2):
                    for k4 in range(4):
                        dk = half * 4 + k4
                        for kf in range(KF):
                            P.op("tensor", lambda t, pe=pe, dk=dk, k4=k4, kf=kf,
                                 ph=ph: t.matmul(
                                     PSB[:, k4, :],
                                     slab2[pe][:, kf, dk * 128:(dk + 1) * 128],
                                     hbuf[:, 0, kf, :], start=(kf == 0),
                                     stop=(kf == KF - 1)),
                                 reads=[slab2[pe][:, kf, dk * 128:(dk + 1) * 128],
                                        hbuf[:, 0, kf, :]],
                                 writes=[PSB[:, k4, :]], noself_tensor=True)
                    P.op("vector", lambda v, half=half, sl=sl: v.tensor_add(
                        y_sb[:, half * 4:half * 4 + 4, sl],
                        y_sb[:, half * 4:half * 4 + 4, sl], PSB[:]),
                        reads=[y_sb[:, half * 4:half * 4 + 4, sl], PSB[:]],
                        writes=[y_sb[:, half * 4:half * 4 + 4, sl]])

        x1st = PB.view(R_XB, F32, (KT, TB))
        x2bst = PB.view(R_PS, BF16, (KT, TB))
        xrb2 = PB.view(R_SF, BF16, (KT, TB))
        sqb2 = PB.view(R_SF + 8 * 1024, BF16, (KT, TB))
        for tb in range(NTB):
            sl = slice(tb * TB, (tb + 1) * TB)
            P.dma("sync", lambda s, tb=tb: s.dma_start(
                out=x1st, in_=x1_dram.ap()[:, tb]),
                reads=[x1_dram.ap()[:, tb]], writes=[x1st])
            ysl = y_sb[:, :, sl]
            P.op("vector", lambda v, ysl=ysl: v.tensor_add(ysl, ysl, x1st[:]),
                 reads=[ysl, x1st[:]], writes=[ysl])
            _layer_norm(P, ysl, xrb2, sqb2, ones_bf, scs, PSA, PSB)
            P.dma("sync", lambda s, tb=tb, ysl=ysl: s.dma_start(
                out=x2_out.ap()[:, tb], in_=ysl), reads=[ysl])
            P.op("scalar", lambda s, ysl=ysl: s.copy(x2bst[:], ysl),
                 reads=[ysl], writes=[x2bst[:]])
            P.dma("sync", lambda s, tb=tb: s.dma_start(
                out=x2b_dram.ap()[:, tb], in_=x2bst[:]),
                reads=[x2bst[:]], writes=[x2b_dram.ap()[:, tb]])

        if upto == 2:
            P.op("sync", lambda s: s.nop(), deps=P.queue_total_deps("sync"),
                 inc=0)
            P.emit(nc, ctx)
            return nc

        wl = PB.view(R_SF, BF16, (4, KT, D))
        for i, wsrc in enumerate((wrP, wk2P, wv2P, wwP)):
            P.dma("gpsimd", lambda g, i=i, wsrc=wsrc: g.dma_start(
                out=wl[:, i], in_=wsrc.ap()), writes=[wl[:, i]])
        x2b_ch = [PB.view(R_XB + i * 8 * 1024, BF16, (KT, TB)) for i in range(2)]
        r_ch = PB.view(R_AP, F32, (KT, TB))
        w_ch = PB.view(R_AP + 16 * 1024, F32, (KT, TB))
        z_ch = PB.view(R_AP + 32 * 1024, F32, (KT, TB))
        u_scr = PB.view(R_AP + 48 * 1024, F32, (KT, TB))
        k_bf = PB.view(R_PS, BF16, (KT, TB))

        for cc in range(NTB):
            ph = cc % 2
            P.dma("sync", lambda s, cc=cc, ph=ph: s.dma_start(
                out=x2b_ch[ph], in_=x2b_dram.ap()[:, cc]),
                reads=[x2b_dram.ap()[:, cc]], writes=[x2b_ch[ph]])

            def proj(widx, outbuf, use_a, act=None, out_bf=False, ph=ph):
                PSx = PSA if use_a else PSB
                for kto in range(KT):
                    bank = kto % 4
                    for kin in range(KT):
                        P.op("tensor", lambda t, widx=widx, kto=kto, kin=kin,
                             bank=bank, PSx=PSx: t.matmul(
                                 PSx[:, bank, :],
                                 wl[:, widx, kin, kto * 128:(kto + 1) * 128],
                                 x2b_ch[ph][:, kin, :], start=(kin == 0),
                                 stop=(kin == KT - 1)),
                             reads=[wl[:, widx, kin, kto * 128:(kto + 1) * 128],
                                    x2b_ch[ph][:, kin, :]],
                             writes=[PSx[:, bank, :]], noself_tensor=True)
                    if act is not None:
                        P.op("scalar", lambda s, kto=kto, bank=bank, PSx=PSx:
                             s.activation(outbuf[:, kto, :], PSx[:, bank, :],
                                          act),
                             reads=[PSx[:, bank, :]], writes=[outbuf[:, kto, :]])
                    elif out_bf:
                        P.op("scalar", lambda s, kto=kto, bank=bank, PSx=PSx:
                             s.copy(outbuf[:, kto, :], PSx[:, bank, :]),
                             reads=[PSx[:, bank, :]], writes=[outbuf[:, kto, :]])
                    else:
                        P.op("vector", lambda v, kto=kto, bank=bank, PSx=PSx:
                             v.tensor_copy(outbuf[:, kto, :], PSx[:, bank, :]),
                             reads=[PSx[:, bank, :]], writes=[outbuf[:, kto, :]])

            proj(0, r_ch, True, act=AF.Sigmoid)
            P.dma("sync", lambda s, cc=cc: s.dma_start(
                out=r_out.ap()[:, cc], in_=r_ch[:]), reads=[r_ch[:]])
            proj(3, w_ch, False, act=AF.Sigmoid)
            for kt in range(KT):
                P.op("vector", lambda v, kt=kt: v.tensor_scalar_mul(
                    w_ch[:, kt, :], w_ch[:, kt, :], 0.99),
                    reads=[w_ch[:, kt, :]], writes=[w_ch[:, kt, :]])
            P.dma("sync", lambda s, cc=cc: s.dma_start(
                out=w_out.ap()[:, cc], in_=w_ch[:]), reads=[w_ch[:]])
            proj(1, k_bf, True, out_bf=True)
            proj(2, z_ch, False)
            for kt in range(KT):
                P.op("tensor", lambda t, kt=kt: t.matmul(
                    PSA[:, kt % 4, :], bdiag[:], k_bf[:, kt, :],
                    start=True, stop=True),
                    reads=[bdiag[:], k_bf[:, kt, :]],
                    writes=[PSA[:, kt % 4, :]])
                P.op("vector", lambda v, kt=kt: v.tensor_mul(
                    z_ch[:, kt, :], z_ch[:, kt, :], PSA[:, kt % 4, :]),
                    reads=[z_ch[:, kt, :], PSA[:, kt % 4, :]],
                    writes=[z_ch[:, kt, :]])
            P.dma("sync", lambda s, cc=cc: s.dma_start(
                out=z_out.ap()[:, cc], in_=z_ch[:]), reads=[z_ch[:]])
            for kt in range(KT):
                init_u = 0.0 if cc == 0 else carryU[:, kt:kt + 1]
                rd = [w_ch[:, kt, :], z_ch[:, kt, :]]
                if cc > 0:
                    rd.append(carryU[:, kt:kt + 1])
                P.op("vector", lambda v, kt=kt, init_u=init_u:
                     v.tensor_tensor_scan(
                         u_scr[:, kt, :], w_ch[:, kt, :], z_ch[:, kt, :],
                         init_u, ALU.mult, ALU.add),
                     reads=rd, writes=[u_scr[:, kt, :]])
                P.op("vector", lambda v, kt=kt: v.tensor_copy(
                    carryU[:, kt:kt + 1], u_scr[:, kt, TB - 1:TB]),
                    reads=[u_scr[:, kt, TB - 1:TB]],
                    writes=[carryU[:, kt:kt + 1]])
                init_p = 1.0 if cc == 0 else carryP[:, kt:kt + 1]
                rdp = [w_ch[:, kt, :]]
                if cc > 0:
                    rdp.append(carryP[:, kt:kt + 1])
                P.op("vector", lambda v, kt=kt, init_p=init_p:
                     v.tensor_tensor_scan(
                         u_scr[:, kt, :], w_ch[:, kt, :], w_ch[:, kt, :],
                         init_p, ALU.mult, ALU.bypass),
                     reads=rdp, writes=[u_scr[:, kt, :]])
                P.op("vector", lambda v, kt=kt: v.tensor_copy(
                    carryP[:, kt:kt + 1], u_scr[:, kt, TB - 1:TB]),
                    reads=[u_scr[:, kt, TB - 1:TB]],
                    writes=[carryP[:, kt:kt + 1]])
        P.op("vector", lambda v: v.tensor_copy(ex1[:, 0:8], carryU[:]),
             reads=[carryU[:]], writes=[ex1[:, 0:8]])
        P.op("vector", lambda v: v.tensor_copy(ex1[:, 8:16], carryP[:]),
             reads=[carryP[:]], writes=[ex1[:, 8:16]])
        tst = P.dma("sync", lambda s: s.dma_start(
            out=ex_out.ap(), in_=ex1[:]), reads=[ex1[:]])
        P.op("sync", lambda s: s.nop(), deps=P.queue_total_deps("sync"),
             inc=0)

        P.emit(nc, ctx)
    return nc


def build_launch2(T):
    TB = 512
    NTB = T // TB
    TH = 2 * TB

    nc = bass.Bass("TRN2", target_bir_lowering=False)
    P = Prog()
    dt = nc.dram_tensor
    x2sp = dt("x2sp", [128, NTB, KT, TB], F32, kind="ExternalInput")
    rsp = dt("rsp", [128, NTB, KT, TB], F32, kind="ExternalInput")
    wsp = dt("wsp", [128, NTB, KT, TB], F32, kind="ExternalInput")
    zsp = dt("zsp", [128, NTB, KT, TB], F32, kind="ExternalInput")
    exall = dt("exall", [128, 8, 16], F32, kind="ExternalInput")
    cmsk = dt("cmsk", [128, 8], F32, kind="ExternalInput")
    cm1m = dt("cm1m", [128, 8], F32, kind="ExternalInput")
    woP = dt("woP", [128, KT, D], F32, kind="ExternalInput")
    tw1P = dt("tw1P", [128, 8, KT, DFF], F32, kind="ExternalInput")
    tw2P = dt("tw2P", [128, 8, KT, DFF], F32, kind="ExternalInput")
    th1P = dt("th1P", [128, 16, KH, 256], F32, kind="ExternalInput")
    th2P = dt("th2P", [128, 16, KH, 256], F32, kind="ExternalInput")
    tw3P = dt("tw3P", [128, 4, KH, 256], F32, kind="ExternalInput")
    s1p = dt("s1p", [128, KH], F32, kind="ExternalInput")
    s2p = dt("s2p", [128, KH], F32, kind="ExternalInput")
    sh1p = dt("sh1p", [128, KH], F32, kind="ExternalInput")
    sh2p = dt("sh2p", [128, KH], F32, kind="ExternalInput")
    s3p = dt("s3p", [128, KT], F32, kind="ExternalInput")
    x4_out = dt("x4_out", [128, NTB, KT, TB], F32, kind="ExternalOutput")
    x3_dram = dt("x3_dram", [128, NTB, KT, TB], F32)

    with ExitStack() as ctx:
        PB = Pool(nc, ctx, "PB", 176 * 1024)
        R1, R2, R3 = 0, 64 * 1024, 128 * 1024

        sb = lambda n, s, d: ctx.enter_context(nc.sbuf_tensor(n, s, d))
        ones_bf = sb("ones_bf", [128, 128], BF16)
        scs = sb("scs", [128, 2, TB], F32)
        exs = sb("exs", [128, 8, 16], F32)
        cms = sb("cms", [128, 8], F32)
        cm1 = sb("cm1", [128, 8], F32)
        acc = sb("acc", [128, 8], F32)
        tmp8 = sb("tmp8", [128, 2, 8], F32)
        carryD = sb("carryD", [128, KT], F32)
        wo_sb = sb("wo_sb", [128, KT, D], BF16)
        scps = sb("scps", [128, 5, KH], F32)
        s1h = sb("s1h", [128, 2, TB], BF16)
        t2f = sb("t2f", [128, 2, TB], F32)

        PSA = ctx.enter_context(nc.psum_tensor("psa", [128, 4, 512], F32))
        PSB = ctx.enter_context(nc.psum_tensor("psb", [128, 4, 512], F32))
        nrm = PB.view(R2, BF16, (KT, TB))
        sqbD = PB.view(R2 + 8 * 1024, BF16, (KT, TB))

        P.op("gpsimd", lambda g: g.memset(ones_bf[:], 1.0), writes=[ones_bf[:]])
        P.dma("sync", lambda s: s.dma_start(out=exs[:], in_=exall.ap()),
              writes=[exs[:]])
        P.dma("sync", lambda s: s.dma_start(out=cms[:], in_=cmsk.ap()),
              writes=[cms[:]])
        P.dma("sync", lambda s: s.dma_start(out=cm1[:], in_=cm1m.ap()),
              writes=[cm1[:]])
        P.dma("gpsimd", lambda g: g.dma_start(out=wo_sb[:], in_=woP.ap()),
              writes=[wo_sb[:]])
        for i, (src, dst) in enumerate(
                ((s1p, 0), (s2p, 1), (sh1p, 2), (sh2p, 3))):
            P.dma("sync", lambda s, src=src, dst=dst: s.dma_start(
                out=scps[:, dst, :], in_=src.ap()), writes=[scps[:, dst, :]])
        P.dma("sync", lambda s: s.dma_start(
            out=scps[:, 4, 0:KT], in_=s3p.ap()), writes=[scps[:, 4, 0:KT]])

        P.op("vector", lambda v: v.memset(acc[:], 0.0), writes=[acc[:]])
        for c in range(NCORE):
            P.op("vector", lambda v, c=c: v.tensor_scalar(
                tmp8[:, 0, :], exs[:, c, 8:16], cms[:, c:c + 1],
                cm1[:, c:c + 1], ALU.mult, ALU.add),
                reads=[exs[:, c, 8:16], cms[:, c:c + 1], cm1[:, c:c + 1]],
                writes=[tmp8[:, 0, :]])
            P.op("vector", lambda v, c=c: v.tensor_scalar(
                tmp8[:, 1, :], exs[:, c, 0:8], cms[:, c:c + 1], None, ALU.mult),
                reads=[exs[:, c, 0:8], cms[:, c:c + 1]], writes=[tmp8[:, 1, :]])
            P.op("vector", lambda v: v.tensor_mul(
                acc[:], acc[:], tmp8[:, 0, :]),
                reads=[acc[:], tmp8[:, 0, :]], writes=[acc[:]])
            P.op("vector", lambda v: v.tensor_add(
                acc[:], acc[:], tmp8[:, 1, :]),
                reads=[acc[:], tmp8[:, 1, :]], writes=[acc[:]])

        w_ch = PB.view(R1, F32, (KT, TB))
        z_ch = PB.view(R1 + 16 * 1024, F32, (KT, TB))
        r_ch = PB.view(R1 + 32 * 1024, F32, (KT, TB))
        x2_ch = PB.view(R1 + 48 * 1024, F32, (KT, TB))
        for cc in range(NTB):
            P.dma("sync", lambda s, cc=cc: s.dma_start(
                out=w_ch[:], in_=wsp.ap()[:, cc]), writes=[w_ch[:]])
            P.dma("sync", lambda s, cc=cc: s.dma_start(
                out=z_ch[:], in_=zsp.ap()[:, cc]), writes=[z_ch[:]])
            P.dma("sync", lambda s, cc=cc: s.dma_start(
                out=r_ch[:], in_=rsp.ap()[:, cc]), writes=[r_ch[:]])
            P.dma("sync", lambda s, cc=cc: s.dma_start(
                out=x2_ch[:], in_=x2sp.ap()[:, cc]), writes=[x2_ch[:]])
            for kt in range(KT):
                init = acc[:, kt:kt + 1] if cc == 0 else carryD[:, kt:kt + 1]
                P.op("vector", lambda v, kt=kt, init=init: v.tensor_tensor_scan(
                    z_ch[:, kt, :], w_ch[:, kt, :], z_ch[:, kt, :], init,
                    ALU.mult, ALU.add),
                    reads=[w_ch[:, kt, :], z_ch[:, kt, :], init],
                    writes=[z_ch[:, kt, :]])
                P.op("vector", lambda v, kt=kt: v.tensor_copy(
                    carryD[:, kt:kt + 1], z_ch[:, kt, TB - 1:TB]),
                    reads=[z_ch[:, kt, TB - 1:TB]], writes=[carryD[:, kt:kt + 1]])
                P.op("vector", lambda v, kt=kt: v.tensor_mul(
                    w_ch[:, kt, :], r_ch[:, kt, :], z_ch[:, kt, :]),
                    reads=[r_ch[:, kt, :], z_ch[:, kt, :]],
                    writes=[w_ch[:, kt, :]])
            P.op("scalar", lambda s: s.activation(sqbD[:], w_ch[:], AF.Square),
                 reads=[w_ch[:]], writes=[sqbD[:]])
            for kt in range(KT):
                P.op("tensor", lambda t, kt=kt: t.matmul(
                    PSB[:, 2, :], ones_bf[:], sqbD[:, kt, :],
                    start=(kt == 0), stop=(kt == KT - 1)),
                    reads=[ones_bf[:], sqbD[:, kt, :]], writes=[PSB[:, 2, :]],
                    noself_tensor=True)
            P.op("vector", lambda v: v.tensor_scalar(
                scs[:, 1, :], PSB[:, 2, :], 1.0 / D, None, ALU.mult),
                reads=[PSB[:, 2, :]], writes=[scs[:, 1, :]])
            P.op("vector", lambda v: v.tensor_scalar_add(
                scs[:, 1, :], scs[:, 1, :], RMS_EPS),
                reads=[scs[:, 1, :]], writes=[scs[:, 1, :]])
            P.op("scalar", lambda s: s.activation(
                scs[:, 1, :], scs[:, 1, :], AF.Sqrt),
                reads=[scs[:, 1, :]], writes=[scs[:, 1, :]])
            P.op("vector", lambda v: v.reciprocal(scs[:, 1, :], scs[:, 1, :]),
                 reads=[scs[:, 1, :]], writes=[scs[:, 1, :]])
            for kt in range(KT):
                P.op("vector", lambda v, kt=kt: v.tensor_tensor(
                    nrm[:, kt, :], w_ch[:, kt, :], scs[:, 1, :], ALU.mult),
                    reads=[w_ch[:, kt, :], scs[:, 1, :]], writes=[nrm[:, kt, :]])
            for kto in range(KT):
                bank = kto % 2
                for kin in range(KT):
                    P.op("tensor", lambda t, kto=kto, kin=kin, bank=bank:
                         t.matmul(PSA[:, bank, :],
                                  wo_sb[:, kin, kto * 128:(kto + 1) * 128],
                                  nrm[:, kin, :], start=(kin == 0),
                                  stop=(kin == KT - 1)),
                         reads=[wo_sb[:, kin, kto * 128:(kto + 1) * 128],
                                nrm[:, kin, :]],
                         writes=[PSA[:, bank, :]], noself_tensor=True)
                P.op("vector", lambda v, kto=kto, bank=bank:
                     v.scalar_tensor_tensor(
                         x2_ch[:, kto, :], PSA[:, bank, :], 1.0,
                         x2_ch[:, kto, :], ALU.mult, ALU.add),
                     reads=[PSA[:, bank, :], x2_ch[:, kto, :]],
                     writes=[x2_ch[:, kto, :]])
            _layer_norm(P, x2_ch, nrm, sqbD, ones_bf, scs, PSA, PSB, mu_i=0, rs_i=1)
            P.dma("sync", lambda s, cc=cc: s.dma_start(
                out=x3_dram.ap()[:, cc], in_=x2_ch[:]),
                reads=[x2_ch[:]], writes=[x3_dram.ap()[:, cc]])

        hB = PB.view(R1, BF16, (KH, TH))
        gB = PB.view(R2, BF16, (KH, TH))
        x3b = PB.view(R3, BF16, (2, KT, TB))
        slabs = [PB.view(R3 + 16 * 1024 + i * 16 * 1024, BF16, (KH, 256))
                 for i in range(2)]
        slabw = [PB.view(R3 + 16 * 1024 + i * 8 * 1024, BF16, (KT, DFF))
                 for i in range(4)]
        xr4 = PB.view(R2 + 16 * 1024, F32, (2, KT, TB))

        x4_toks = []
        for hf in range(NTB // 2):
            P.dma("gpsimd", lambda g, hf=hf: g.dma_start(
                out=x3b[:], in_=x3_dram.ap()[:, 2 * hf:2 * hf + 2]),
                reads=[x3_dram.ap()[:, 2 * hf:2 * hf + 2]], writes=[x3b[:]])
            for dc in range(8):
                p1s = slabw[(dc % 2) * 2]
                p2s = slabw[(dc % 2) * 2 + 1]
                P.dma("gpsimd", lambda g, dc=dc, p1s=p1s: g.dma_start(
                    out=p1s, in_=tw1P.ap()[:, dc]), writes=[p1s])
                P.dma("gpsimd", lambda g, dc=dc, p2s=p2s: g.dma_start(
                    out=p2s, in_=tw2P.ap()[:, dc]), writes=[p2s])
                for k4 in range(KF):
                    dk = dc * 4 + k4
                    for tbh in range(2):
                        for kin in range(KT):
                            P.op("tensor", lambda t, p1s=p1s, k4=k4, kin=kin,
                                 tbh=tbh: t.matmul(
                                     PSA[:, k4 % 2 + 2 * tbh, :],
                                     p1s[:, kin, k4 * 128:(k4 + 1) * 128],
                                     x3b[:, tbh, kin, :], start=(kin == 0),
                                     stop=(kin == KT - 1)),
                                 reads=[p1s[:, kin, k4 * 128:(k4 + 1) * 128],
                                        x3b[:, tbh, kin, :]],
                                 writes=[PSA[:, k4 % 2 + 2 * tbh, :]],
                                 noself_tensor=True)
                        for kin in range(KT):
                            P.op("tensor", lambda t, p2s=p2s, k4=k4, kin=kin,
                                 tbh=tbh: t.matmul(
                                     PSB[:, k4 % 2 + 2 * tbh, :],
                                     p2s[:, kin, k4 * 128:(k4 + 1) * 128],
                                     x3b[:, tbh, kin, :], start=(kin == 0),
                                     stop=(kin == KT - 1)),
                                 reads=[p2s[:, kin, k4 * 128:(k4 + 1) * 128],
                                        x3b[:, tbh, kin, :]],
                                 writes=[PSB[:, k4 % 2 + 2 * tbh, :]],
                                 noself_tensor=True)
                        bnk = k4 % 2 + 2 * tbh
                        P.op("scalar", lambda s, dk=dk, bnk=bnk, tbh=tbh:
                             s.activation(s1h[:, tbh, :], PSA[:, bnk, :],
                                          AF.Silu,
                                          scale=scps[:, 0, dk:dk + 1]),
                             reads=[PSA[:, bnk, :], scps[:, 0, dk:dk + 1]],
                             writes=[s1h[:, tbh, :]])
                        P.op("vector", lambda v, dk=dk, bnk=bnk, tbh=tbh:
                             v.tensor_scalar(
                                 t2f[:, tbh, :], PSB[:, bnk, :],
                                 scps[:, 1, dk:dk + 1], None, ALU.mult),
                             reads=[PSB[:, bnk, :], scps[:, 1, dk:dk + 1]],
                             writes=[t2f[:, tbh, :]])
                        P.op("vector", lambda v, dk=dk, tbh=tbh: v.tensor_mul(
                            hB[:, dk, tbh * TB:(tbh + 1) * TB],
                            s1h[:, tbh, :], t2f[:, tbh, :]),
                            reads=[s1h[:, tbh, :], t2f[:, tbh, :]],
                            writes=[hB[:, dk, tbh * TB:(tbh + 1) * TB]])

            def hidden(P_, src, dst, wsrc, prow):
                for dc2 in range(16):
                    sl_ = slabs[dc2 % 2]
                    P_.dma("gpsimd", lambda g, dc2=dc2, sl_=sl_: g.dma_start(
                        out=sl_, in_=wsrc.ap()[:, dc2]), writes=[sl_])
                    for sub in range(2):
                        dk = dc2 * 2 + sub
                        for tbh in range(2):
                            bnk = sub + 2 * tbh
                            for kin in range(KH):
                                P_.op("tensor", lambda t, sl_=sl_, sub=sub,
                                      kin=kin, tbh=tbh, bnk=bnk: t.matmul(
                                          PSA[:, bnk, :],
                                          sl_[:, kin, sub * 128:(sub + 1) * 128],
                                          src[:, kin, tbh * TB:(tbh + 1) * TB],
                                          start=(kin == 0),
                                          stop=(kin == KH - 1)),
                                      reads=[sl_[:, kin,
                                                 sub * 128:(sub + 1) * 128],
                                             src[:, kin,
                                                 tbh * TB:(tbh + 1) * TB]],
                                      writes=[PSA[:, bnk, :]],
                                      noself_tensor=True)
                            P_.op("scalar", lambda s, dk=dk, bnk=bnk, tbh=tbh:
                                  s.activation(
                                      dst[:, dk, tbh * TB:(tbh + 1) * TB],
                                      PSA[:, bnk, :], AF.Gelu,
                                      scale=scps[:, prow, dk:dk + 1]),
                                  reads=[PSA[:, bnk, :],
                                         scps[:, prow, dk:dk + 1]],
                                  writes=[dst[:, dk,
                                              tbh * TB:(tbh + 1) * TB]])
            hidden(P, hB, gB, th1P, 2)
            hidden(P, gB, hB, th2P, 3)

            for tbh in range(2):
                P.dma("sync", lambda s, hf=hf, tbh=tbh: s.dma_start(
                    out=xr4[:, tbh], in_=x3_dram.ap()[:, 2 * hf + tbh]),
                    reads=[x3_dram.ap()[:, 2 * hf + tbh]],
                    writes=[xr4[:, tbh]])
            for dc3 in range(4):
                sl_ = slabs[dc3 % 2]
                P.dma("gpsimd", lambda g, dc3=dc3, sl_=sl_: g.dma_start(
                    out=sl_, in_=tw3P.ap()[:, dc3]), writes=[sl_])
                for sub in range(2):
                    dk = dc3 * 2 + sub
                    for tbh in range(2):
                        bnk = sub + 2 * tbh
                        for kin in range(KH):
                            P.op("tensor", lambda t, sl_=sl_, sub=sub, kin=kin,
                                 tbh=tbh, bnk=bnk: t.matmul(
                                     PSB[:, bnk, :],
                                     sl_[:, kin, sub * 128:(sub + 1) * 128],
                                     hB[:, kin, tbh * TB:(tbh + 1) * TB],
                                     start=(kin == 0), stop=(kin == KH - 1)),
                                 reads=[sl_[:, kin, sub * 128:(sub + 1) * 128],
                                        hB[:, kin, tbh * TB:(tbh + 1) * TB]],
                                 writes=[PSB[:, bnk, :]], noself_tensor=True)
                        P.op("vector", lambda v, dk=dk, bnk=bnk, tbh=tbh:
                             v.scalar_tensor_tensor(
                                 xr4[:, tbh, dk, :], PSB[:, bnk, :],
                                 scps[:, 4, dk:dk + 1], xr4[:, tbh, dk, :],
                                 ALU.mult, ALU.add),
                             reads=[PSB[:, bnk, :], scps[:, 4, dk:dk + 1],
                                    xr4[:, tbh, dk, :]],
                             writes=[xr4[:, tbh, dk, :]])
            for tbh in range(2):
                cc = 2 * hf + tbh
                _layer_norm(P, xr4[:, tbh], nrm, sqbD, ones_bf, scs, PSA, PSB, mu_i=0, rs_i=1)
                tk = P.dma("sync", lambda s, cc=cc, tbh=tbh: s.dma_start(
                    out=x4_out.ap()[:, cc], in_=xr4[:, tbh]),
                    reads=[xr4[:, tbh]])
                x4_toks.append(tk)
        P.op("sync", lambda s: s.nop(), deps=P.queue_total_deps("sync"),
             inc=0)

        P.emit(nc, ctx)
    return nc



def _img(w):
    K = w.shape[0] // 128
    return np.ascontiguousarray(
        w.reshape(K, 128, w.shape[1]).transpose(1, 0, 2))


def _img_chunks(w, nch):
    K = w.shape[0] // 128
    N = w.shape[1]
    return np.ascontiguousarray(
        w.reshape(K, 128, nch, N // nch).transpose(1, 2, 0, 3))


_NC_CACHE = {}


def kernel(L0_attn_x, L0_attn_Wq, L0_attn_Wk, L0_attn_Wv, L0_ffn_gate_w,
           L0_ffn_expert_bias, L0_ffn_w1, L0_ffn_b1, L0_ffn_w2, L0_ffn_b2,
           L0_ffn_w3, L0_ffn_b3, L0_ffn_sw1, L0_ffn_sb1, L0_ffn_sw2,
           L0_ffn_sb2, L0_ffn_sw3, L0_ffn_sb3, L1_attn_Wr, L1_attn_Wk,
           L1_attn_Wv, L1_attn_Ww, L1_attn_Wo, L1_ffn_w1_w, L1_ffn_w1_s,
           L1_ffn_w2_w, L1_ffn_w2_s, L1_ffn_h1_w, L1_ffn_h1_s, L1_ffn_h2_w,
           L1_ffn_h2_s, L1_ffn_w3_w, L1_ffn_w3_s):
    TFULL = L0_attn_x.shape[1]
    T = TFULL // NCORE
    TB = 512
    NTB = T // TB
    NCH = TFULL // T

    f = np.float32
    x_cm = np.ascontiguousarray(L0_attn_x[0].T.astype(f, copy=False))
    xfull_img = np.ascontiguousarray(
        x_cm.reshape(KT, 128, NCH, T).transpose(1, 2, 0, 3))

    wqP = _img(L0_attn_Wq.astype(f, copy=False))
    wkP = _img(L0_attn_Wk.astype(f, copy=False))
    wvP = _img(L0_attn_Wv.astype(f, copy=False))
    gateP = _img(np.ascontiguousarray(L0_ffn_gate_w.T.astype(f, copy=False)))
    ebias = np.ascontiguousarray(
        np.broadcast_to(L0_ffn_expert_bias.astype(f), (128, E_MOE)))

    w1T = L0_ffn_w1.transpose(0, 2, 1).astype(f, copy=False)
    w3T = L0_ffn_w3.transpose(0, 2, 1).astype(f, copy=False)
    w2T = L0_ffn_w2.transpose(0, 2, 1).astype(f, copy=False)
    sw1T = L0_ffn_sw1.T.astype(f, copy=False)
    sw3T = L0_ffn_sw3.T.astype(f, copy=False)
    sw2T = L0_ffn_sw2.T.astype(f, copy=False)
    we1T = np.concatenate(
        [w1T, sw1T[None, :, 0:DFF], sw1T[None, :, DFF:2 * DFF]], 0)
    we3T = np.concatenate(
        [w3T, sw3T[None, :, 0:DFF], sw3T[None, :, DFF:2 * DFF]], 0)
    we2T = np.concatenate(
        [w2T, sw2T[None, 0:DFF, :], sw2T[None, DFF:2 * DFF, :]], 0)
    we1P = np.ascontiguousarray(np.stack([_img(we1T[e]) for e in range(NEXP)]))
    we3P = np.ascontiguousarray(np.stack([_img(we3T[e]) for e in range(NEXP)]))
    we2P = np.ascontiguousarray(np.stack([_img(we2T[e]) for e in range(NEXP)]))

    def packb(b):
        n = b.shape[0] // 128
        return np.ascontiguousarray(b.reshape(n, 128).T.astype(f, copy=False))

    b1cat = np.concatenate([L0_ffn_b1.reshape(-1), L0_ffn_sb1], 0)
    b3cat = np.concatenate([L0_ffn_b3.reshape(-1), L0_ffn_sb3], 0)
    b1p = packb(b1cat)
    b3p = packb(b3cat)
    b2xT = np.ascontiguousarray(
        np.concatenate([L0_ffn_b2, L0_ffn_sb2[None]], 0).astype(f, copy=False))

    wrP = _img(np.ascontiguousarray(L1_attn_Wr.T.astype(f, copy=False)))
    wk2P = _img(np.ascontiguousarray(L1_attn_Wk.T.astype(f, copy=False)))
    wv2P = _img(np.ascontiguousarray(L1_attn_Wv.T.astype(f, copy=False)))
    wwP = _img(np.ascontiguousarray(L1_attn_Ww.T.astype(f, copy=False)))
    woP = _img(np.ascontiguousarray(L1_attn_Wo.T.astype(f, copy=False)))
    tw1P = _img_chunks(np.ascontiguousarray(L1_ffn_w1_w.T.astype(f, copy=False)), 8)
    tw2P = _img_chunks(np.ascontiguousarray(L1_ffn_w2_w.T.astype(f, copy=False)), 8)
    th1P = _img_chunks(np.ascontiguousarray(L1_ffn_h1_w.T.astype(f, copy=False)), 16)
    th2P = _img_chunks(np.ascontiguousarray(L1_ffn_h2_w.T.astype(f, copy=False)), 16)
    tw3P = _img_chunks(np.ascontiguousarray(L1_ffn_w3_w.T.astype(f, copy=False)), 4)
    s1p = packb(L1_ffn_w1_s)
    s2p = packb(L1_ffn_w2_s)
    sh1p = packb(L1_ffn_h1_s)
    sh2p = packb(L1_ffn_h2_s)
    s3p = packb(L1_ffn_w3_s)

    key = ("l1", T)
    if key not in _NC_CACHE:
        _NC_CACHE[key] = build_launch1(T)
    nc1 = _NC_CACHE[key]

    shared1 = dict(wqP=wqP, wkP=wkP, wvP=wvP, gateP=gateP, ebias=ebias,
                   we1P=we1P, we3P=we3P, we2P=we2P, b1p=b1p, b3p=b3p,
                   b2xT=b2xT, wrP=wrP, wk2P=wk2P, wv2P=wv2P, wwP=wwP,
                   xfull=xfull_img)
    in_maps1 = []
    for c in range(NCORE):
        xl = np.ascontiguousarray(
            x_cm[:, c * T:(c + 1) * T].reshape(KT, 128, NTB, TB)
            .transpose(1, 2, 0, 3))
        m = dict(shared1)
        m["xloc"] = xl
        in_maps1.append(m)
    res1 = run_bass_kernel_spmd(nc1, in_maps1, core_ids=list(range(NCORE)))

    exall = np.ascontiguousarray(
        np.stack([res1.results[c]["ex_out"] for c in range(NCORE)], axis=1))

    key2 = ("l2", T)
    if key2 not in _NC_CACHE:
        _NC_CACHE[key2] = build_launch2(T)
    nc2 = _NC_CACHE[key2]

    shared2 = dict(exall=exall, woP=woP, tw1P=tw1P, tw2P=tw2P, th1P=th1P,
                   th2P=th2P, tw3P=tw3P, s1p=s1p, s2p=s2p, sh1p=sh1p,
                   sh2p=sh2p, s3p=s3p)
    in_maps2 = []
    for c in range(NCORE):
        cm = np.zeros((128, 8), f)
        cm[:, :c] = 1.0
        m = dict(shared2)
        m["x2sp"] = res1.results[c]["x2_out"]
        m["rsp"] = res1.results[c]["r_out"]
        m["wsp"] = res1.results[c]["w_out"]
        m["zsp"] = res1.results[c]["z_out"]
        m["cmsk"] = cm
        m["cm1m"] = np.ascontiguousarray(1.0 - cm)
        in_maps2.append(m)
    res2 = run_bass_kernel_spmd(nc2, in_maps2, core_ids=list(range(NCORE)))

    out = np.empty((1, TFULL, D), np.float32)
    for c in range(NCORE):
        img = res2.results[c]["x4_out"]
        xc = img.transpose(2, 0, 1, 3).reshape(D, T)
        out[0, c * T:(c + 1) * T, :] = xc.T
    return out


# version 05
# speedup vs baseline: 1.0107x; 1.0107x over previous
import numpy as np
from contextlib import ExitStack

import concourse.bass as bass
import concourse.mybir as mybir
from concourse.bass_utils import run_bass_kernel_spmd

F32 = mybir.dt.float32
BF16 = mybir.dt.bfloat16
U8 = mybir.dt.uint8
AF = mybir.ActivationFunctionType
ALU = mybir.AluOpType
AX = mybir.AxisListType.X

NCORE = 8
D = 1024
KT = 8
GAMMA = 0.96875
LN_EPS = 1e-5
RMS_EPS = float(np.finfo(np.float32).eps)
E_MOE = 16
DFF = 512
KF = 4
NEXP = 18
DHID = 4096
KH = 32
SEG = 4096


def _ap_range(ap):
    sz = mybir.dt.size(ap.dtype)
    dims = list(ap.ap)
    is_dram = "DRam" in type(ap.tensor).__name__
    off = int(ap.offset)
    if not is_dram:
        row = 1
        for d in list(ap.tensor.shape)[1:]:
            row *= int(d)
        off = off % row
    use = dims if is_dram else dims[1:]
    ext = 0
    for step, count in use:
        ext += abs(int(step)) * (int(count) - 1)
    lo = off * sz
    return ap.tensor.name, lo, lo + (ext + 1) * sz


class Prog:
    ENGS = ("sync", "gpsimd", "vector", "scalar", "tensor")
    NQ = 8
    SEMS = tuple(["gpsimd", "vector", "scalar", "tensor"]
                 + [f"sy{i}" for i in range(8)] + [f"gd{i}" for i in range(8)])

    def __init__(self):
        self.items = []
        self.cnt = dict.fromkeys(self.SEMS, 0)
        self.dmai = {"sync": 0, "gpsimd": 0}
        self.wseg = {}
        self.rseg = {}

    def _segs(self, lo, hi):
        return range(lo // SEG, (hi - 1) // SEG + 1)

    def op(self, eng, fn, reads=(), writes=(), deps=(), inc=1, rank=False,
           noself_tensor=False, sem=None):
        sem = sem or (eng if eng in self.cnt else "sy0")
        dep = set(deps)
        for ap in reads:
            name, lo, hi = _ap_range(ap)
            wm = self.wseg.get(name)
            if wm:
                for s in self._segs(lo, hi):
                    t = wm.get(s)
                    if t:
                        dep.add(t)
        for ap in writes:
            name, lo, hi = _ap_range(ap)
            wm = self.wseg.get(name)
            rm = self.rseg.get(name)
            for s in self._segs(lo, hi):
                if wm and (t := wm.get(s)):
                    dep.add(t)
                if rm:
                    dep.update(rm.get(s, ()))
        if noself_tensor:
            dep = {d for d in dep if d[0] != "tensor"}
        self.cnt[sem] += inc
        tok = (sem, self.cnt[sem])
        self.items.append((eng, sem, fn, sorted(dep), inc, rank))
        for ap in reads:
            name, lo, hi = _ap_range(ap)
            rm = self.rseg.setdefault(name, {})
            for s in self._segs(lo, hi):
                rm.setdefault(s, set()).add(tok)
        for ap in writes:
            name, lo, hi = _ap_range(ap)
            wm = self.wseg.setdefault(name, {})
            rm = self.rseg.setdefault(name, {})
            for s in self._segs(lo, hi):
                wm[s] = tok
                rm[s] = set()
        return tok

    def dma(self, eng, fn, reads=(), writes=(), deps=()):
        idx = self.dmai[eng]
        self.dmai[eng] = idx + 1
        pre = "sy" if eng == "sync" else "gd"
        sem = f"pre{idx % self.NQ}".replace("pre", pre)
        deps = list(deps)
        if idx >= self.NQ:
            deps.append((sem, 16 * (idx // self.NQ)))
        return self.op(eng, fn, reads, writes, deps, inc=16, sem=sem)

    def queue_total_deps(self, eng):
        pre = "sy" if eng == "sync" else "gd"
        return [(f"{pre}{i}", self.cnt[f"{pre}{i}"]) for i in range(self.NQ)]

    def emit(self, nc, ctx):
        sems = {e: ctx.enter_context(nc.semaphore(f"pg_{e}")) for e in self.SEMS}
        block = ctx.enter_context(nc.Block())
        items = self.items

        def mk(eng_name):
            def body(eng):
                waited = {}
                for (e, sm, fn, deps, inc, rank) in items:
                    if e != eng_name:
                        continue
                    for (de, dc) in deps:
                        if waited.get(de, 0) >= dc:
                            continue
                        eng.wait_ge(sems[de], dc)
                        waited[de] = dc
                    if rank:
                        for arm in eng.switch_core_id(n=128):
                            insts = fn(eng, arm.logical % NCORE)
                            if not isinstance(insts, (list, tuple)):
                                insts = [insts]
                            assert len(insts) == inc
                            for i in insts:
                                i.then_inc(sems[sm], 1)
                    elif inc == 0:
                        fn(eng)
                    else:
                        insts = fn(eng)
                        if isinstance(insts, (list, tuple)):
                            insts[-1].then_inc(sems[sm], inc)
                        else:
                            insts.then_inc(sems[sm], inc)
            return body

        for e in self.ENGS:
            getattr(block, e)(mk(e))


class Pool:

    def __init__(self, nc, ctx, name, nbytes):
        assert nbytes % 4 == 0
        self.t = ctx.enter_context(nc.sbuf_tensor(name, [128, nbytes // 4], F32))

    def view(self, off, dtype, shape):
        assert off % 4 == 0
        n = int(np.prod(shape))
        if dtype == F32:
            ap = self.t[:, off // 4:off // 4 + n]
        elif dtype == BF16:
            assert n % 2 == 0
            ap = self.t[:, off // 4:off // 4 + n // 2].bitcast(BF16)
        else:
            raise ValueError(dtype)
        if len(shape) == 1:
            return ap
        pat = "p (" + " ".join(f"d{i}" for i in range(len(shape))) + ") -> p " + \
              " ".join(f"d{i}" for i in range(len(shape)))
        return ap.rearrange(pat, **{f"d{i}": s for i, s in enumerate(shape)})


def _layer_norm(P, xr, xrb, sqb, ones_bf, scs, PSA, PSB, mu_i=4, rs_i=5):
    P.op("scalar", lambda s: s.copy(xrb[:], xr[:]), reads=[xr[:]],
         writes=[xrb[:]])
    P.op("scalar", lambda s: s.activation(sqb[:], xrb[:], AF.Square),
         reads=[xrb[:]], writes=[sqb[:]])
    P.op("tensor", lambda t: [t.matmul(
        PSA[:, 3, :], ones_bf[:], xrb[:, kt, :],
        start=(kt == 0), stop=(kt == KT - 1)) for kt in range(KT)],
        reads=[ones_bf[:], xrb[:]], writes=[PSA[:, 3, :]],
        noself_tensor=True)
    P.op("tensor", lambda t: [t.matmul(
        PSB[:, 2, :], ones_bf[:], sqb[:, kt, :],
        start=(kt == 0), stop=(kt == KT - 1)) for kt in range(KT)],
        reads=[ones_bf[:], sqb[:]], writes=[PSB[:, 2, :]],
        noself_tensor=True)
    P.op("vector", lambda v: v.tensor_scalar(
        scs[:, mu_i, :], PSA[:, 3, :], 1.0 / D, None, ALU.mult),
        reads=[PSA[:, 3, :]], writes=[scs[:, mu_i, :]])
    P.op("scalar", lambda s: s.activation(
        scs[:, rs_i, :], scs[:, mu_i, :], AF.Square),
        reads=[scs[:, mu_i, :]], writes=[scs[:, rs_i, :]])
    P.op("vector", lambda v: v.scalar_tensor_tensor(
        scs[:, rs_i, :], PSB[:, 2, :], 1.0 / D, scs[:, rs_i, :],
        ALU.mult, ALU.subtract),
        reads=[PSB[:, 2, :], scs[:, rs_i, :]], writes=[scs[:, rs_i, :]])
    P.op("vector", lambda v: v.tensor_scalar_add(
        scs[:, rs_i, :], scs[:, rs_i, :], LN_EPS),
        reads=[scs[:, rs_i, :]], writes=[scs[:, rs_i, :]])
    P.op("scalar", lambda s: s.activation(
        scs[:, rs_i, :], scs[:, rs_i, :], AF.Sqrt),
        reads=[scs[:, rs_i, :]], writes=[scs[:, rs_i, :]])
    P.op("vector", lambda v: v.reciprocal(scs[:, rs_i, :], scs[:, rs_i, :]),
         reads=[scs[:, rs_i, :]], writes=[scs[:, rs_i, :]])
    for kt in range(KT):
        P.op("vector", lambda v, kt=kt: v.tensor_tensor(
            xr[:, kt, :], xr[:, kt, :], scs[:, mu_i, :], ALU.subtract),
            reads=[xr[:, kt, :], scs[:, mu_i, :]], writes=[xr[:, kt, :]])
        P.op("vector", lambda v, kt=kt: v.tensor_tensor(
            xr[:, kt, :], xr[:, kt, :], scs[:, rs_i, :], ALU.mult),
            reads=[xr[:, kt, :], scs[:, rs_i, :]], writes=[xr[:, kt, :]])


def build_launch1(T, debug=False, upto=3):
    TFULL = 8 * T
    assert TFULL // D == 16
    TB = 512
    NTB = T // TB
    NCH = TFULL // T
    JC = T // 16
    NTT = T // 128

    nc = bass.Bass("TRN2", target_bir_lowering=False)
    P = Prog()
    dt = nc.dram_tensor
    xfull = dt("xfull", [128, NCH, KT, T], F32, kind="ExternalInput")
    xloc = dt("xloc", [128, NTB, KT, TB], F32, kind="ExternalInput")
    wqP = dt("wqP", [128, KT, D], F32, kind="ExternalInput")
    wkP = dt("wkP", [128, KT, D], F32, kind="ExternalInput")
    wvP = dt("wvP", [128, KT, D], F32, kind="ExternalInput")
    gateP = dt("gateP", [128, KT, E_MOE], F32, kind="ExternalInput")
    ebias = dt("ebias", [128, E_MOE], F32, kind="ExternalInput")
    weP = dt("weP", [NEXP, 128, 3 * KT * DFF], F32, kind="ExternalInput")
    b1p = dt("b1p", [128, NEXP * KF], F32, kind="ExternalInput")
    b3p = dt("b3p", [128, NEXP * KF], F32, kind="ExternalInput")
    b2xT = dt("b2xT", [17, D], F32, kind="ExternalInput")
    wrP = dt("wrP", [128, KT, D], F32, kind="ExternalInput")
    wk2P = dt("wk2P", [128, KT, D], F32, kind="ExternalInput")
    wv2P = dt("wv2P", [128, KT, D], F32, kind="ExternalInput")
    wwP = dt("wwP", [128, KT, D], F32, kind="ExternalInput")

    x2_out = dt("x2_out", [128, NTB, KT, TB], F32, kind="ExternalOutput")
    r_out = dt("r_out", [128, NTB, KT, TB], F32, kind="ExternalOutput")
    w_out = dt("w_out", [128, NTB, KT, TB], F32, kind="ExternalOutput")
    z_out = dt("z_out", [128, NTB, KT, TB], F32, kind="ExternalOutput")
    ex_out = dt("ex_out", [128, 16], F32, kind="ExternalOutput")
    if debug:
        x1_dram = dt("x1_dram", [128, NTB, KT, TB], F32, kind="ExternalOutput")
        psel_out = dt("psel_out", [128, KT, T], BF16, kind="ExternalOutput")
    else:
        x1_dram = dt("x1_dram", [128, NTB, KT, TB], F32)
    x2b_dram = dt("x2b_dram", [128, NTB, KT, TB], BF16)
    g_dram = dt("g_dram", [NEXP, T], F32)

    with ExitStack() as ctx:
        PB = Pool(nc, ctx, "PB", 176 * 1024)
        R_AP, R_SF, R_XB, R_PS = 0, 64 * 1024, 128 * 1024, 144 * 1024

        sb = lambda n, s, d: ctx.enter_context(nc.sbuf_tensor(n, s, d))
        ident = sb("ident", [128, 128], F32)
        ones_bf = sb("ones_bf", [128, 128], BF16)
        bdiag = sb("bdiag", [128, 128], BF16)
        gamma1 = sb("gamma1", [128, 1], F32)
        onesrow = sb("onesrow", [1, 512], BF16)
        wvs = sb("wvs", [128, KT], F32)
        wcol = sb("wcol", [128, KT, 128], BF16)
        carryA = sb("carryA", [128, KT], F32)
        scs = sb("scs", [128, 6, TB], F32)
        b1s = sb("b1s", [128, NEXP * KF], F32)
        b3s = sb("b3s", [128, NEXP * KF], F32)
        ebias_s = sb("ebias_s", [128, E_MOE], F32)
        glog = sb("glog", [128, NTT, E_MOE], F32)
        gsig = sb("gsig", [128, NTT, E_MOE], F32)
        gbia = sb("gbia", [128, NTT, E_MOE], F32)
        gmk1 = sb("gmk1", [128, NTT, E_MOE], F32)
        gmk2 = sb("gmk2", [128, NTT, E_MOE], F32)
        gwv = sb("gwv", [128, NTT, E_MOE], F32)
        gm = sb("gm", [128, NTT, 4], F32)
        b2xs = sb("b2xs", [17, D], BF16)
        gb = sb("gb", [128, 1, TB], F32)
        hbuf = sb("hbuf", [128, 1, KF, TB], BF16)
        carryU = sb("carryU", [128, KT], F32)
        carryP = sb("carryP", [128, KT], F32)
        ex1 = sb("ex1", [128, 16], F32)

        PSA = ctx.enter_context(nc.psum_tensor("psa", [128, 4, 512], F32))
        PSB = ctx.enter_context(nc.psum_tensor("psb", [128, 4, 512], F32))
        PST = PSB[:, 3, 0:128]

        P.op("gpsimd", lambda g: g.memset(ident[:], 0.0), writes=[ident[:]])
        P.op("gpsimd", lambda g: g.affine_select(
            out=ident[:], in_=ident[:], compare_op=ALU.not_equal, fill=1.0,
            base=0, pattern=[[-1, 128]], channel_multiplier=1),
            reads=[ident[:]], writes=[ident[:]])
        P.op("gpsimd", lambda g: g.memset(ones_bf[:], 1.0), writes=[ones_bf[:]])
        P.op("gpsimd", lambda g: g.memset(bdiag[:], 0.0), writes=[bdiag[:]])
        P.op("gpsimd", lambda g: g.memset(bdiag[0:64, 0:64], 1.0),
             reads=[bdiag[:]], writes=[bdiag[0:64, 0:64]])
        P.op("gpsimd", lambda g: g.memset(bdiag[64:128, 64:128], 1.0),
             reads=[bdiag[:]], writes=[bdiag[64:128, 64:128]])
        P.op("gpsimd", lambda g: g.memset(gamma1[:], GAMMA), writes=[gamma1[:]])
        P.op("gpsimd", lambda g: g.memset(onesrow[:], 1.0), writes=[onesrow[:]])
        P.dma("sync", lambda s: s.dma_start(out=b1s[:], in_=b1p.ap()),
              writes=[b1s[:]])
        P.dma("sync", lambda s: s.dma_start(out=b3s[:], in_=b3p.ap()),
              writes=[b3s[:]])
        P.dma("sync", lambda s: s.dma_start(out=ebias_s[:], in_=ebias.ap()),
              writes=[ebias_s[:]])
        P.dma("gpsimd", lambda g: g.dma_start(out=b2xs[:], in_=b2xT.ap()),
              writes=[b2xs[:]])

        wv_slab = PB.view(R_AP, F32, (KT, D))
        for kt in range(KT):
            P.dma("sync", lambda s, kt=kt: s.dma_start(
                out=wv_slab[:, kt, :], in_=wvP.ap()[:, kt, :]),
                writes=[wv_slab[:, kt, :]])
            P.op("vector", lambda v, kt=kt: v.reduce_sum(
                wvs[:, kt:kt + 1], wv_slab[:, kt, :], axis=AX),
                reads=[wv_slab[:, kt, :]], writes=[wvs[:, kt:kt + 1]])
        for kt in range(KT):
            P.op("vector", lambda v, kt=kt: v.tensor_scalar(
                wcol[:, kt, :], ones_bf[:], wvs[:, kt:kt + 1], None, ALU.mult),
                reads=[ones_bf[:], wvs[:, kt:kt + 1]], writes=[wcol[:, kt, :]])

        xb_ch = PB.view(R_SF, BF16, (KT, T))
        APbuf = PB.view(R_AP, F32, (KT, T))
        psel = PB.view(R_PS, BF16, (KT, T))
        ap4 = APbuf.rearrange("p k (j a) -> p k j a", a=16)

        for ct in range(NCH):
            P.dma("gpsimd", lambda g, ct=ct: g.dma_start(
                out=xb_ch, in_=xfull.ap()[:, ct]), writes=[xb_ch])
            for tb in range(NTB):
                sl = slice(tb * TB, (tb + 1) * TB)
                P.op("tensor", lambda t, sl=sl: [t.matmul(
                    PSA[:, 0, :], wcol[:, kt, :], xb_ch[:, kt, sl],
                    start=(kt == 0), stop=(kt == KT - 1)) for kt in range(KT)],
                    reads=[wcol[:], xb_ch[:, :, sl]],
                    writes=[PSA[:, 0, :]], noself_tensor=True)
                for kt in range(KT):
                    P.op("vector", lambda v, kt=kt, sl=sl: v.tensor_tensor(
                        APbuf[:, kt, sl], xb_ch[:, kt, sl], PSA[:, 0, :],
                        ALU.mult),
                        reads=[xb_ch[:, kt, sl], PSA[:, 0, :]],
                        writes=[APbuf[:, kt, sl]])
            if ct == 0:
                P.op("vector", lambda v: v.memset(APbuf[:, :, 0:1], 0.0),
                     writes=[APbuf[:, :, 0:1]])
            for kt in range(KT):
                init = 0.0 if ct == 0 else carryA[:, kt:kt + 1]
                rd = [gamma1[:], APbuf[:, kt, :]]
                if ct > 0:
                    rd.append(carryA[:, kt:kt + 1])
                P.op("vector", lambda v, kt=kt, init=init: v.tensor_tensor_scan(
                    APbuf[:, kt, :], gamma1[:].broadcast_to([128, T]),
                    APbuf[:, kt, :], init,
                    ALU.mult, ALU.add), reads=rd, writes=[APbuf[:, kt, :]])
                P.op("vector", lambda v, kt=kt: v.tensor_copy(
                    carryA[:, kt:kt + 1], APbuf[:, kt, T - 1:T]),
                    reads=[APbuf[:, kt, T - 1:T]], writes=[carryA[:, kt:kt + 1]])
            if ct == 0:
                P.op("vector", lambda v: v.tensor_copy(
                    APbuf[:, :, 0:1], APbuf[:, :, 1:2]),
                    reads=[APbuf[:, :, 1:2]], writes=[APbuf[:, :, 0:1]])

            def ext(vv, r, ct=ct):
                i0 = vv.tensor_copy(psel[:, :, ct * JC:(ct + 1) * JC],
                                    ap4[:, :, :, 2 * r])
                i1 = vv.tensor_copy(
                    psel[:, :, T // 2 + ct * JC:T // 2 + (ct + 1) * JC],
                    ap4[:, :, :, 2 * r + 1])
                return [i0, i1]
            P.op("vector", ext, reads=[APbuf[:, :, :]],
                 writes=[psel[:, :, ct * JC:(ct + 1) * JC],
                         psel[:, :, T // 2 + ct * JC:T // 2 + (ct + 1) * JC]],
                 rank=True, inc=2)

        if debug and upto == 0:
            P.dma("sync", lambda s: s.dma_start(out=psel_out.ap(), in_=psel),
                  reads=[psel])
            P.op("sync", lambda s: s.nop(), deps=P.queue_total_deps("sync"),
                 inc=0)
            P.emit(nc, ctx)
            return nc

        wk_sb = PB.view(R_AP, BF16, (KT, D))
        wq_sb = PB.view(R_AP + 16 * 1024, BF16, (KT, D))
        xb_tb = PB.view(R_AP + 32 * 1024, BF16, (KT, TB))
        xrb = PB.view(R_AP + 40 * 1024, BF16, (KT, TB))
        sqb = PB.view(R_AP + 48 * 1024, BF16, (KT, TB))
        sstage = PB.view(R_AP + 56 * 1024, F32, (TB,))
        sfin = PB.view(R_SF, F32, (KT, T))
        xloc_f = PB.view(R_XB, F32, (KT, TB))
        x1b = PB.view(R_PS, BF16, (KT, T))

        P.dma("gpsimd", lambda g: g.dma_start(out=wk_sb, in_=wkP.ap()),
              writes=[wk_sb])
        P.dma("gpsimd", lambda g: g.dma_start(out=wq_sb, in_=wqP.ap()),
              writes=[wq_sb])

        for bkt in range(KT):
            for sbk in range(T // TB):
                pb = PSA[:, sbk % 2, :]
                P.op("tensor", lambda t, bkt=bkt, sbk=sbk, pb=pb:
                     [t.matmul(pb, wk_sb[:, kin, bkt * 128:(bkt + 1) * 128],
                               psel[:, kin, sbk * TB:(sbk + 1) * TB],
                               start=(kin == 0), stop=(kin == KT - 1))
                      for kin in range(KT)],
                     reads=[wk_sb[:], psel[:, :, sbk * TB:(sbk + 1) * TB]],
                     writes=[pb], noself_tensor=True)
                P.op("scalar", lambda s, pb=pb: s.copy(sstage[:], pb),
                     reads=[pb], writes=[sstage[:]])
                for j in range(TB // 128):
                    s_abs = sbk * TB + j * 128
                    a_loc = s_abs // (T // 2)
                    ikt = (s_abs % (T // 2)) // 128
                    col = a_loc * (T // 2) + bkt * 128
                    P.op("tensor", lambda t, j=j: t.transpose(
                        PST, sstage[:, j * 128:(j + 1) * 128], ident[:]),
                        reads=[sstage[:, j * 128:(j + 1) * 128], ident[:]],
                        writes=[PST])
                    P.op("vector", lambda v, ikt=ikt, col=col: v.tensor_copy(
                        sfin[:, ikt, col:col + 128], PST),
                        reads=[PST], writes=[sfin[:, ikt, col:col + 128]])

        for tb in range(NTB):
            sl = slice(tb * TB, (tb + 1) * TB)
            P.dma("sync", lambda s, tb=tb: s.dma_start(
                out=xloc_f, in_=xloc.ap()[:, tb]), writes=[xloc_f])
            P.dma("gpsimd", lambda g, tb=tb: g.dma_start(
                out=xb_tb, in_=xloc.ap()[:, tb]), writes=[xb_tb])
            xr = xloc_f
            for kto in range(KT):
                P.op("tensor", lambda t, kto=kto: [t.matmul(
                    PSA[:, 2, :], wq_sb[:, kin, kto * 128:(kto + 1) * 128],
                    xb_tb[:, kin, :], start=(kin == 0),
                    stop=(kin == KT - 1)) for kin in range(KT)],
                    reads=[wq_sb[:], xb_tb[:]],
                    writes=[PSA[:, 2, :]], noself_tensor=True)
                P.op("vector", lambda v, kto=kto, sl=sl: v.scalar_tensor_tensor(
                    scs[:, 0, :], PSA[:, 2, :], 1.0, sfin[:, kto, sl],
                    ALU.mult, ALU.mult),
                    reads=[PSA[:, 2, :], sfin[:, kto, sl]],
                    writes=[scs[:, 0, :]])
                P.op("vector", lambda v, kto=kto: v.tensor_add(
                    xr[:, kto, :], xr[:, kto, :], scs[:, 0, :]),
                    reads=[xr[:, kto, :], scs[:, 0, :]], writes=[xr[:, kto, :]])
            _layer_norm(P, xr, xrb, sqb, ones_bf, scs, PSA, PSB)
            P.dma("sync", lambda s, tb=tb: s.dma_start(
                out=x1_dram.ap()[:, tb], in_=xr[:]), reads=[xr[:]])
            P.op("scalar", lambda s, sl=sl: s.copy(x1b[:, :, sl], xr[:]),
                 reads=[xr[:]], writes=[x1b[:, :, sl]])

        if upto == 1:
            P.op("sync", lambda s: s.nop(), deps=P.queue_total_deps("sync"),
                 inc=0)
            P.emit(nc, ctx)
            return nc

        gT_sb = PB.view(R_XB, BF16, (KT, E_MOE))
        gxs = PB.t[0:17, (R_XB + 4096) // 4:(R_XB + 8192) // 4].bitcast(BF16)
        gxf = PB.t[0:16, (R_XB + 8192) // 4:(R_XB + 16384) // 4]
        P.dma("gpsimd", lambda g: g.dma_start(out=gT_sb, in_=gateP.ap()),
              writes=[gT_sb])
        for tt in range(NTT):
            bank = tt % 2
            P.op("tensor", lambda t, tt=tt, bank=bank: [t.matmul(
                PSA[:, bank, 0:16],
                x1b[:, kin, tt * 128:(tt + 1) * 128], gT_sb[:, kin, :],
                start=(kin == 0), stop=(kin == KT - 1)) for kin in range(KT)],
                reads=[x1b[:, :, tt * 128:(tt + 1) * 128], gT_sb[:]],
                writes=[PSA[:, bank, 0:16]],
                noself_tensor=True)
            P.op("vector", lambda v, tt=tt, bank=bank: v.tensor_copy(
                glog[:, tt, :], PSA[:, bank, 0:16]),
                reads=[PSA[:, bank, 0:16]], writes=[glog[:, tt, :]])
        P.op("scalar", lambda s: s.activation(gsig[:], glog[:], AF.Sigmoid),
             reads=[glog[:]], writes=[gsig[:]])
        for tt in range(NTT):
            P.op("vector", lambda v, tt=tt: v.tensor_add(
                gbia[:, tt, :], glog[:, tt, :], ebias_s[:]),
                reads=[glog[:, tt, :], ebias_s[:]], writes=[gbia[:, tt, :]])
            P.op("vector", lambda v, tt=tt: v.reduce_max(
                gm[:, tt, 0:1], gbia[:, tt, :], axis=AX),
                reads=[gbia[:, tt, :]], writes=[gm[:, tt, 0:1]])
            P.op("vector", lambda v, tt=tt: v.tensor_scalar(
                gmk1[:, tt, :], gbia[:, tt, :], gm[:, tt, 0:1], None,
                ALU.is_equal),
                reads=[gbia[:, tt, :], gm[:, tt, 0:1]], writes=[gmk1[:, tt, :]])
            P.op("vector", lambda v, tt=tt: v.scalar_tensor_tensor(
                gmk2[:, tt, :], gmk1[:, tt, :], -1e30, gbia[:, tt, :],
                ALU.mult, ALU.add),
                reads=[gmk1[:, tt, :], gbia[:, tt, :]], writes=[gmk2[:, tt, :]])
            P.op("vector", lambda v, tt=tt: v.reduce_max(
                gm[:, tt, 1:2], gmk2[:, tt, :], axis=AX),
                reads=[gmk2[:, tt, :]], writes=[gm[:, tt, 1:2]])
            P.op("vector", lambda v, tt=tt: v.tensor_scalar(
                gmk2[:, tt, :], gmk2[:, tt, :], gm[:, tt, 1:2], None,
                ALU.is_equal),
                reads=[gmk2[:, tt, :], gm[:, tt, 1:2]], writes=[gmk2[:, tt, :]])
            P.op("vector", lambda v, tt=tt: v.tensor_add(
                gmk1[:, tt, :], gmk1[:, tt, :], gmk2[:, tt, :]),
                reads=[gmk1[:, tt, :], gmk2[:, tt, :]], writes=[gmk1[:, tt, :]])
            P.op("vector", lambda v, tt=tt: v.tensor_mul(
                gwv[:, tt, :], gsig[:, tt, :], gmk1[:, tt, :]),
                reads=[gsig[:, tt, :], gmk1[:, tt, :]], writes=[gwv[:, tt, :]])
            P.op("vector", lambda v, tt=tt: v.reduce_sum(
                gm[:, tt, 2:3], gwv[:, tt, :], axis=AX),
                reads=[gwv[:, tt, :]], writes=[gm[:, tt, 2:3]])
            P.op("vector", lambda v, tt=tt: v.tensor_scalar_add(
                gm[:, tt, 2:3], gm[:, tt, 2:3], 1e-10),
                reads=[gm[:, tt, 2:3]], writes=[gm[:, tt, 2:3]])
            P.op("vector", lambda v, tt=tt: v.reciprocal(
                gm[:, tt, 3:4], gm[:, tt, 2:3]),
                reads=[gm[:, tt, 2:3]], writes=[gm[:, tt, 3:4]])
            P.op("vector", lambda v, tt=tt: v.tensor_scalar(
                gwv[:, tt, :], gwv[:, tt, :], gm[:, tt, 3:4], None, ALU.mult),
                reads=[gwv[:, tt, :], gm[:, tt, 3:4]], writes=[gwv[:, tt, :]])
        gtr = scs[:, 0, 0:128]
        for blkI in range(NTT // 8):
            gblk = gwv.ap().rearrange("p a e -> p (a e)")[
                :, blkI * 128:(blkI + 1) * 128]
            P.op("tensor", lambda t, gblk=gblk: t.transpose(
                PST, gblk, ident[:]),
                reads=[gblk, ident[:]], writes=[PST])
            P.op("vector", lambda v: v.tensor_copy(gtr, PST),
                 reads=[PST], writes=[gtr])
            dap = g_dram.ap()[0:16, blkI * 1024:(blkI + 1) * 1024].rearrange(
                "e (q p) -> q e p", p=128)
            P.dma("sync", lambda s, dap=dap: s.dma_start(out=dap, in_=gtr),
                  reads=[gtr], writes=[g_dram.ap()[0:16,
                                                   blkI * 1024:(blkI + 1) * 1024]])
        for rr in (16, 17):
            for q in range(T // 512):
                P.dma("gpsimd", lambda g, rr=rr, q=q: g.dma_start(
                    out=g_dram.ap()[rr:rr + 1, q * 512:(q + 1) * 512],
                    in_=onesrow[:]),
                    reads=[onesrow[:]],
                    writes=[g_dram.ap()[rr:rr + 1, q * 512:(q + 1) * 512]])
        P.dma("gpsimd", lambda g: g.dma_start(
            out=gxs[:], in_=g_dram.ap()[0:17, :]),
            reads=[g_dram.ap()[0:17, :]], writes=[gxs[:]])

        y_sb = PB.view(R_AP, F32, (KT, T))
        for tb in range(NTB):
            sl = slice(tb * TB, (tb + 1) * TB)
            for half in range(2):
                P.op("tensor", lambda t, half=half, sl=sl: [t.matmul(
                    PSB[:, k4, :], b2xs[:, (half * 4 + k4) * 128:
                                        (half * 4 + k4 + 1) * 128],
                    gxs[:, sl], start=True, stop=True) for k4 in range(4)],
                    reads=[b2xs[:], gxs[:, sl]],
                    writes=[PSB[:]], noself_tensor=True)
                P.op("vector", lambda v, half=half, sl=sl: v.tensor_copy(
                    y_sb[:, half * 4:half * 4 + 4, sl], PSB[:]),
                    reads=[PSB[:]], writes=[y_sb[:, half * 4:half * 4 + 4, sl]])

        slab1 = [PB.view(R_SF + i * 24 * 1024, BF16, (KT, DFF)) for i in range(2)]
        slab3 = [PB.view(R_SF + i * 24 * 1024 + 8 * 1024, BF16, (KT, DFF))
                 for i in range(2)]
        slab2 = [PB.view(R_SF + i * 24 * 1024 + 16 * 1024, BF16, (KF, D))
                 for i in range(2)]
        slabA = [PB.view(R_SF + i * 24 * 1024, BF16, (3 * KT * DFF,))
                 for i in range(2)]
        for e in range(NEXP):
            pe = e % 2
            P.dma("gpsimd", lambda g, e=e, pe=pe: g.dma_start(
                out=slabA[pe], in_=weP.ap()[e]), writes=[slabA[pe]])
            for tb in range(NTB):
                sl = slice(tb * TB, (tb + 1) * TB)
                ph = tb % 2
                P.dma("sync", lambda s, e=e, sl=sl: s.dma_start(
                    out=gb[:, 0:1, :],
                    in_=g_dram.ap()[e:e + 1, sl].partition_broadcast(128)),
                    reads=[g_dram.ap()[e:e + 1, sl]], writes=[gb[:, 0, :]])
                for kf in range(KF):
                    P.op("tensor", lambda t, pe=pe, sl=sl, kf=kf: [t.matmul(
                        PSA[:, kf, :],
                        slab1[pe][:, kin, kf * 128:(kf + 1) * 128],
                        x1b[:, kin, sl], start=(kin == 0),
                        stop=(kin == KT - 1)) for kin in range(KT)],
                        reads=[slab1[pe][:, :, kf * 128:(kf + 1) * 128],
                               x1b[:, :, sl]],
                        writes=[PSA[:, kf, :]], noself_tensor=True)
                for kf in range(KF):
                    P.op("scalar", lambda s, e=e, kf=kf: s.activation(
                        scs[:, kf, :], PSA[:, kf, :], AF.Silu,
                        bias=b1s[:, e * KF + kf:e * KF + kf + 1]),
                        reads=[PSA[:, kf, :],
                               b1s[:, e * KF + kf:e * KF + kf + 1]],
                        writes=[scs[:, kf, :]])
                for kf in range(KF):
                    P.op("tensor", lambda t, pe=pe, sl=sl, kf=kf: [t.matmul(
                        PSB[:, kf, :],
                        slab3[pe][:, kin, kf * 128:(kf + 1) * 128],
                        x1b[:, kin, sl], start=(kin == 0),
                        stop=(kin == KT - 1)) for kin in range(KT)],
                        reads=[slab3[pe][:, :, kf * 128:(kf + 1) * 128],
                               x1b[:, :, sl]],
                        writes=[PSB[:, kf, :]], noself_tensor=True)
                for kf in range(KF):
                    P.op("vector", lambda v, e=e, kf=kf, ph=ph:
                         v.scalar_tensor_tensor(
                             scs[:, 4 + (kf % 2), :], PSB[:, kf, :],
                             b3s[:, e * KF + kf:e * KF + kf + 1], gb[:, 0, :],
                             ALU.add, ALU.mult),
                         reads=[PSB[:, kf, :], gb[:, 0, :],
                                b3s[:, e * KF + kf:e * KF + kf + 1]],
                         writes=[scs[:, 4 + (kf % 2), :]])
                    P.op("vector", lambda v, kf=kf, ph=ph: v.tensor_mul(
                        hbuf[:, 0, kf, :], scs[:, kf, :],
                        scs[:, 4 + (kf % 2), :]),
                        reads=[scs[:, kf, :], scs[:, 4 + (kf % 2), :]],
                        writes=[hbuf[:, 0, kf, :]])
                for half in range(2):
                    P.op("tensor", lambda t, pe=pe, half=half: [t.matmul(
                        PSB[:, k4, :],
                        slab2[pe][:, kf, (half * 4 + k4) * 128:
                                  (half * 4 + k4 + 1) * 128],
                        hbuf[:, 0, kf, :], start=(kf == 0),
                        stop=(kf == KF - 1))
                        for k4 in range(4) for kf in range(KF)],
                        reads=[slab2[pe][:], hbuf[:]],
                        writes=[PSB[:]], noself_tensor=True)
                    P.op("vector", lambda v, half=half, sl=sl: v.tensor_add(
                        y_sb[:, half * 4:half * 4 + 4, sl],
                        y_sb[:, half * 4:half * 4 + 4, sl], PSB[:]),
                        reads=[y_sb[:, half * 4:half * 4 + 4, sl], PSB[:]],
                        writes=[y_sb[:, half * 4:half * 4 + 4, sl]])

        x1st = PB.view(R_XB, F32, (KT, TB))
        x2bst = PB.view(R_PS, BF16, (KT, TB))
        xrb2 = PB.view(R_SF, BF16, (KT, TB))
        sqb2 = PB.view(R_SF + 8 * 1024, BF16, (KT, TB))
        for tb in range(NTB):
            sl = slice(tb * TB, (tb + 1) * TB)
            P.dma("sync", lambda s, tb=tb: s.dma_start(
                out=x1st, in_=x1_dram.ap()[:, tb]),
                reads=[x1_dram.ap()[:, tb]], writes=[x1st])
            ysl = y_sb[:, :, sl]
            P.op("vector", lambda v, ysl=ysl: v.tensor_add(ysl, ysl, x1st[:]),
                 reads=[ysl, x1st[:]], writes=[ysl])
            _layer_norm(P, ysl, xrb2, sqb2, ones_bf, scs, PSA, PSB)
            P.dma("sync", lambda s, tb=tb, ysl=ysl: s.dma_start(
                out=x2_out.ap()[:, tb], in_=ysl), reads=[ysl])
            P.op("scalar", lambda s, ysl=ysl: s.copy(x2bst[:], ysl),
                 reads=[ysl], writes=[x2bst[:]])
            P.dma("sync", lambda s, tb=tb: s.dma_start(
                out=x2b_dram.ap()[:, tb], in_=x2bst[:]),
                reads=[x2bst[:]], writes=[x2b_dram.ap()[:, tb]])

        if upto == 2:
            P.op("sync", lambda s: s.nop(), deps=P.queue_total_deps("sync"),
                 inc=0)
            P.emit(nc, ctx)
            return nc

        wl = PB.view(R_SF, BF16, (4, KT, D))
        for i, wsrc in enumerate((wrP, wk2P, wv2P, wwP)):
            P.dma("gpsimd", lambda g, i=i, wsrc=wsrc: g.dma_start(
                out=wl[:, i], in_=wsrc.ap()), writes=[wl[:, i]])
        x2b_ch = [PB.view(R_XB + i * 8 * 1024, BF16, (KT, TB)) for i in range(2)]
        r_ch = PB.view(R_AP, F32, (KT, TB))
        w_ch = PB.view(R_AP + 16 * 1024, F32, (KT, TB))
        z_ch = PB.view(R_AP + 32 * 1024, F32, (KT, TB))
        u_scr = PB.view(R_AP + 48 * 1024, F32, (KT, TB))
        k_bf = PB.view(R_PS, BF16, (KT, TB))

        for cc in range(NTB):
            ph = cc % 2
            P.dma("sync", lambda s, cc=cc, ph=ph: s.dma_start(
                out=x2b_ch[ph], in_=x2b_dram.ap()[:, cc]),
                reads=[x2b_dram.ap()[:, cc]], writes=[x2b_ch[ph]])

            def proj(widx, outbuf, use_a, act=None, out_bf=False, ph=ph):
                PSx = PSA if use_a else PSB
                for kto in range(KT):
                    bank = kto % 4
                    P.op("tensor", lambda t, widx=widx, kto=kto,
                         bank=bank, PSx=PSx: [t.matmul(
                             PSx[:, bank, :],
                             wl[:, widx, kin, kto * 128:(kto + 1) * 128],
                             x2b_ch[ph][:, kin, :], start=(kin == 0),
                             stop=(kin == KT - 1)) for kin in range(KT)],
                         reads=[wl[:, widx], x2b_ch[ph][:]],
                         writes=[PSx[:, bank, :]], noself_tensor=True)
                    if act is not None:
                        P.op("scalar", lambda s, kto=kto, bank=bank, PSx=PSx:
                             s.activation(outbuf[:, kto, :], PSx[:, bank, :],
                                          act),
                             reads=[PSx[:, bank, :]], writes=[outbuf[:, kto, :]])
                    elif out_bf:
                        P.op("scalar", lambda s, kto=kto, bank=bank, PSx=PSx:
                             s.copy(outbuf[:, kto, :], PSx[:, bank, :]),
                             reads=[PSx[:, bank, :]], writes=[outbuf[:, kto, :]])
                    else:
                        P.op("vector", lambda v, kto=kto, bank=bank, PSx=PSx:
                             v.tensor_copy(outbuf[:, kto, :], PSx[:, bank, :]),
                             reads=[PSx[:, bank, :]], writes=[outbuf[:, kto, :]])

            proj(0, r_ch, True, act=AF.Sigmoid)
            P.dma("sync", lambda s, cc=cc: s.dma_start(
                out=r_out.ap()[:, cc], in_=r_ch[:]), reads=[r_ch[:]])
            proj(3, w_ch, False, act=AF.Sigmoid)
            for kt in range(KT):
                P.op("vector", lambda v, kt=kt: v.tensor_scalar_mul(
                    w_ch[:, kt, :], w_ch[:, kt, :], 0.99),
                    reads=[w_ch[:, kt, :]], writes=[w_ch[:, kt, :]])
            P.dma("sync", lambda s, cc=cc: s.dma_start(
                out=w_out.ap()[:, cc], in_=w_ch[:]), reads=[w_ch[:]])
            proj(1, k_bf, True, out_bf=True)
            proj(2, z_ch, False)
            for kt in range(KT):
                P.op("tensor", lambda t, kt=kt: t.matmul(
                    PSA[:, kt % 4, :], bdiag[:], k_bf[:, kt, :],
                    start=True, stop=True),
                    reads=[bdiag[:], k_bf[:, kt, :]],
                    writes=[PSA[:, kt % 4, :]])
                P.op("vector", lambda v, kt=kt: v.tensor_mul(
                    z_ch[:, kt, :], z_ch[:, kt, :], PSA[:, kt % 4, :]),
                    reads=[z_ch[:, kt, :], PSA[:, kt % 4, :]],
                    writes=[z_ch[:, kt, :]])
            P.dma("sync", lambda s, cc=cc: s.dma_start(
                out=z_out.ap()[:, cc], in_=z_ch[:]), reads=[z_ch[:]])
            for kt in range(KT):
                init_u = 0.0 if cc == 0 else carryU[:, kt:kt + 1]
                rd = [w_ch[:, kt, :], z_ch[:, kt, :]]
                if cc > 0:
                    rd.append(carryU[:, kt:kt + 1])
                P.op("vector", lambda v, kt=kt, init_u=init_u:
                     v.tensor_tensor_scan(
                         u_scr[:, kt, :], w_ch[:, kt, :], z_ch[:, kt, :],
                         init_u, ALU.mult, ALU.add),
                     reads=rd, writes=[u_scr[:, kt, :]])
                P.op("vector", lambda v, kt=kt: v.tensor_copy(
                    carryU[:, kt:kt + 1], u_scr[:, kt, TB - 1:TB]),
                    reads=[u_scr[:, kt, TB - 1:TB]],
                    writes=[carryU[:, kt:kt + 1]])
                init_p = 1.0 if cc == 0 else carryP[:, kt:kt + 1]
                rdp = [w_ch[:, kt, :]]
                if cc > 0:
                    rdp.append(carryP[:, kt:kt + 1])
                P.op("vector", lambda v, kt=kt, init_p=init_p:
                     v.tensor_tensor_scan(
                         u_scr[:, kt, :], w_ch[:, kt, :], w_ch[:, kt, :],
                         init_p, ALU.mult, ALU.bypass),
                     reads=rdp, writes=[u_scr[:, kt, :]])
                P.op("vector", lambda v, kt=kt: v.tensor_copy(
                    carryP[:, kt:kt + 1], u_scr[:, kt, TB - 1:TB]),
                    reads=[u_scr[:, kt, TB - 1:TB]],
                    writes=[carryP[:, kt:kt + 1]])
        P.op("vector", lambda v: v.tensor_copy(ex1[:, 0:8], carryU[:]),
             reads=[carryU[:]], writes=[ex1[:, 0:8]])
        P.op("vector", lambda v: v.tensor_copy(ex1[:, 8:16], carryP[:]),
             reads=[carryP[:]], writes=[ex1[:, 8:16]])
        tst = P.dma("sync", lambda s: s.dma_start(
            out=ex_out.ap(), in_=ex1[:]), reads=[ex1[:]])
        P.op("sync", lambda s: s.nop(), deps=P.queue_total_deps("sync"),
             inc=0)

        P.emit(nc, ctx)
    return nc


def build_launch2(T):
    TB = 512
    NTB = T // TB
    TH = 2 * TB

    nc = bass.Bass("TRN2", target_bir_lowering=False)
    P = Prog()
    dt = nc.dram_tensor
    x2sp = dt("x2sp", [128, NTB, KT, TB], F32, kind="ExternalInput")
    rsp = dt("rsp", [128, NTB, KT, TB], F32, kind="ExternalInput")
    wsp = dt("wsp", [128, NTB, KT, TB], F32, kind="ExternalInput")
    zsp = dt("zsp", [128, NTB, KT, TB], F32, kind="ExternalInput")
    exall = dt("exall", [128, 8, 16], F32, kind="ExternalInput")
    cmsk = dt("cmsk", [128, 8], F32, kind="ExternalInput")
    cm1m = dt("cm1m", [128, 8], F32, kind="ExternalInput")
    woP = dt("woP", [128, KT, D], F32, kind="ExternalInput")
    tw12P = dt("tw12P", [128, 8, 2 * KT * DFF], F32, kind="ExternalInput")
    th1P = dt("th1P", [128, 16, KH, 256], F32, kind="ExternalInput")
    th2P = dt("th2P", [128, 16, KH, 256], F32, kind="ExternalInput")
    tw3P = dt("tw3P", [128, 4, KH, 256], F32, kind="ExternalInput")
    s1p = dt("s1p", [128, KH], F32, kind="ExternalInput")
    s2p = dt("s2p", [128, KH], F32, kind="ExternalInput")
    sh1p = dt("sh1p", [128, KH], F32, kind="ExternalInput")
    sh2p = dt("sh2p", [128, KH], F32, kind="ExternalInput")
    s3p = dt("s3p", [128, KT], F32, kind="ExternalInput")
    x4_out = dt("x4_out", [128, NTB, KT, TB], F32, kind="ExternalOutput")
    x3_dram = dt("x3_dram", [128, NTB, KT, TB], F32)

    with ExitStack() as ctx:
        PB = Pool(nc, ctx, "PB", 176 * 1024)
        R1, R2, R3 = 0, 64 * 1024, 128 * 1024

        sb = lambda n, s, d: ctx.enter_context(nc.sbuf_tensor(n, s, d))
        ones_bf = sb("ones_bf", [128, 128], BF16)
        scs = sb("scs", [128, 2, TB], F32)
        exs = sb("exs", [128, 8, 16], F32)
        cms = sb("cms", [128, 8], F32)
        cm1 = sb("cm1", [128, 8], F32)
        acc = sb("acc", [128, 8], F32)
        tmp8 = sb("tmp8", [128, 2, 8], F32)
        carryD = sb("carryD", [128, KT], F32)
        wo_sb = sb("wo_sb", [128, KT, D], BF16)
        scps = sb("scps", [128, 5, KH], F32)
        s1h = sb("s1h", [128, 2, TB], BF16)
        t2f = sb("t2f", [128, 2, TB], F32)

        PSA = ctx.enter_context(nc.psum_tensor("psa", [128, 4, 512], F32))
        PSB = ctx.enter_context(nc.psum_tensor("psb", [128, 4, 512], F32))
        nrm = PB.view(R2, BF16, (KT, TB))
        sqbD = PB.view(R2 + 8 * 1024, BF16, (KT, TB))

        P.op("gpsimd", lambda g: g.memset(ones_bf[:], 1.0), writes=[ones_bf[:]])
        P.dma("sync", lambda s: s.dma_start(out=exs[:], in_=exall.ap()),
              writes=[exs[:]])
        P.dma("sync", lambda s: s.dma_start(out=cms[:], in_=cmsk.ap()),
              writes=[cms[:]])
        P.dma("sync", lambda s: s.dma_start(out=cm1[:], in_=cm1m.ap()),
              writes=[cm1[:]])
        P.dma("gpsimd", lambda g: g.dma_start(out=wo_sb[:], in_=woP.ap()),
              writes=[wo_sb[:]])
        for i, (src, dst) in enumerate(
                ((s1p, 0), (s2p, 1), (sh1p, 2), (sh2p, 3))):
            P.dma("sync", lambda s, src=src, dst=dst: s.dma_start(
                out=scps[:, dst, :], in_=src.ap()), writes=[scps[:, dst, :]])
        P.dma("sync", lambda s: s.dma_start(
            out=scps[:, 4, 0:KT], in_=s3p.ap()), writes=[scps[:, 4, 0:KT]])

        P.op("vector", lambda v: v.memset(acc[:], 0.0), writes=[acc[:]])
        for c in range(NCORE):
            P.op("vector", lambda v, c=c: v.tensor_scalar(
                tmp8[:, 0, :], exs[:, c, 8:16], cms[:, c:c + 1],
                cm1[:, c:c + 1], ALU.mult, ALU.add),
                reads=[exs[:, c, 8:16], cms[:, c:c + 1], cm1[:, c:c + 1]],
                writes=[tmp8[:, 0, :]])
            P.op("vector", lambda v, c=c: v.tensor_scalar(
                tmp8[:, 1, :], exs[:, c, 0:8], cms[:, c:c + 1], None, ALU.mult),
                reads=[exs[:, c, 0:8], cms[:, c:c + 1]], writes=[tmp8[:, 1, :]])
            P.op("vector", lambda v: v.tensor_mul(
                acc[:], acc[:], tmp8[:, 0, :]),
                reads=[acc[:], tmp8[:, 0, :]], writes=[acc[:]])
            P.op("vector", lambda v: v.tensor_add(
                acc[:], acc[:], tmp8[:, 1, :]),
                reads=[acc[:], tmp8[:, 1, :]], writes=[acc[:]])

        w_ch = PB.view(R1, F32, (KT, TB))
        z_ch = PB.view(R1 + 16 * 1024, F32, (KT, TB))
        r_ch = PB.view(R1 + 32 * 1024, F32, (KT, TB))
        x2_ch = PB.view(R1 + 48 * 1024, F32, (KT, TB))
        for cc in range(NTB):
            P.dma("sync", lambda s, cc=cc: s.dma_start(
                out=w_ch[:], in_=wsp.ap()[:, cc]), writes=[w_ch[:]])
            P.dma("sync", lambda s, cc=cc: s.dma_start(
                out=z_ch[:], in_=zsp.ap()[:, cc]), writes=[z_ch[:]])
            P.dma("sync", lambda s, cc=cc: s.dma_start(
                out=r_ch[:], in_=rsp.ap()[:, cc]), writes=[r_ch[:]])
            P.dma("sync", lambda s, cc=cc: s.dma_start(
                out=x2_ch[:], in_=x2sp.ap()[:, cc]), writes=[x2_ch[:]])
            for kt in range(KT):
                init = acc[:, kt:kt + 1] if cc == 0 else carryD[:, kt:kt + 1]
                P.op("vector", lambda v, kt=kt, init=init: v.tensor_tensor_scan(
                    z_ch[:, kt, :], w_ch[:, kt, :], z_ch[:, kt, :], init,
                    ALU.mult, ALU.add),
                    reads=[w_ch[:, kt, :], z_ch[:, kt, :], init],
                    writes=[z_ch[:, kt, :]])
                P.op("vector", lambda v, kt=kt: v.tensor_copy(
                    carryD[:, kt:kt + 1], z_ch[:, kt, TB - 1:TB]),
                    reads=[z_ch[:, kt, TB - 1:TB]], writes=[carryD[:, kt:kt + 1]])
                P.op("vector", lambda v, kt=kt: v.tensor_mul(
                    w_ch[:, kt, :], r_ch[:, kt, :], z_ch[:, kt, :]),
                    reads=[r_ch[:, kt, :], z_ch[:, kt, :]],
                    writes=[w_ch[:, kt, :]])
            P.op("scalar", lambda s: s.activation(sqbD[:], w_ch[:], AF.Square),
                 reads=[w_ch[:]], writes=[sqbD[:]])
            P.op("tensor", lambda t: [t.matmul(
                PSB[:, 2, :], ones_bf[:], sqbD[:, kt, :],
                start=(kt == 0), stop=(kt == KT - 1)) for kt in range(KT)],
                reads=[ones_bf[:], sqbD[:]], writes=[PSB[:, 2, :]],
                noself_tensor=True)
            P.op("vector", lambda v: v.tensor_scalar(
                scs[:, 1, :], PSB[:, 2, :], 1.0 / D, None, ALU.mult),
                reads=[PSB[:, 2, :]], writes=[scs[:, 1, :]])
            P.op("vector", lambda v: v.tensor_scalar_add(
                scs[:, 1, :], scs[:, 1, :], RMS_EPS),
                reads=[scs[:, 1, :]], writes=[scs[:, 1, :]])
            P.op("scalar", lambda s: s.activation(
                scs[:, 1, :], scs[:, 1, :], AF.Sqrt),
                reads=[scs[:, 1, :]], writes=[scs[:, 1, :]])
            P.op("vector", lambda v: v.reciprocal(scs[:, 1, :], scs[:, 1, :]),
                 reads=[scs[:, 1, :]], writes=[scs[:, 1, :]])
            for kt in range(KT):
                P.op("vector", lambda v, kt=kt: v.tensor_tensor(
                    nrm[:, kt, :], w_ch[:, kt, :], scs[:, 1, :], ALU.mult),
                    reads=[w_ch[:, kt, :], scs[:, 1, :]], writes=[nrm[:, kt, :]])
            for kto in range(KT):
                bank = kto % 2
                P.op("tensor", lambda t, kto=kto, bank=bank: [t.matmul(
                    PSA[:, bank, :], wo_sb[:, kin, kto * 128:(kto + 1) * 128],
                    nrm[:, kin, :], start=(kin == 0),
                    stop=(kin == KT - 1)) for kin in range(KT)],
                    reads=[wo_sb[:], nrm[:]],
                    writes=[PSA[:, bank, :]], noself_tensor=True)
                P.op("vector", lambda v, kto=kto, bank=bank:
                     v.scalar_tensor_tensor(
                         x2_ch[:, kto, :], PSA[:, bank, :], 1.0,
                         x2_ch[:, kto, :], ALU.mult, ALU.add),
                     reads=[PSA[:, bank, :], x2_ch[:, kto, :]],
                     writes=[x2_ch[:, kto, :]])
            _layer_norm(P, x2_ch, nrm, sqbD, ones_bf, scs, PSA, PSB, mu_i=0, rs_i=1)
            P.dma("sync", lambda s, cc=cc: s.dma_start(
                out=x3_dram.ap()[:, cc], in_=x2_ch[:]),
                reads=[x2_ch[:]], writes=[x3_dram.ap()[:, cc]])

        hB = PB.view(R1, BF16, (KH, TH))
        gB = PB.view(R2, BF16, (KH, TH))
        x3b = PB.view(R3, BF16, (2, KT, TB))
        slabs = [PB.view(R3 + 16 * 1024 + i * 16 * 1024, BF16, (KH, 256))
                 for i in range(2)]
        slabw = [PB.view(R3 + 16 * 1024 + i * 8 * 1024, BF16, (KT, DFF))
                 for i in range(4)]
        xr4 = PB.view(R2 + 16 * 1024, F32, (2, KT, TB))

        x4_toks = []
        for hf in range(NTB // 2):
            P.dma("gpsimd", lambda g, hf=hf: g.dma_start(
                out=x3b[:], in_=x3_dram.ap()[:, 2 * hf:2 * hf + 2]),
                reads=[x3_dram.ap()[:, 2 * hf:2 * hf + 2]], writes=[x3b[:]])
            slabw12 = [PB.view(R3 + 16 * 1024 + i * 16 * 1024, BF16,
                               (2 * KT * DFF,)) for i in range(2)]
            for dc in range(8):
                p1s = slabw[(dc % 2) * 2]
                p2s = slabw[(dc % 2) * 2 + 1]
                P.dma("gpsimd", lambda g, dc=dc, w12=slabw12[dc % 2]:
                      g.dma_start(out=w12, in_=tw12P.ap()[:, dc]),
                      writes=[slabw12[dc % 2]])
                for k4 in range(KF):
                    dk = dc * 4 + k4
                    for tbh in range(2):
                        P.op("tensor", lambda t, p1s=p1s, k4=k4,
                             tbh=tbh: [t.matmul(
                                 PSA[:, k4 % 2 + 2 * tbh, :],
                                 p1s[:, kin, k4 * 128:(k4 + 1) * 128],
                                 x3b[:, tbh, kin, :], start=(kin == 0),
                                 stop=(kin == KT - 1)) for kin in range(KT)],
                             reads=[p1s[:], x3b[:, tbh]],
                             writes=[PSA[:, k4 % 2 + 2 * tbh, :]],
                             noself_tensor=True)
                        P.op("tensor", lambda t, p2s=p2s, k4=k4,
                             tbh=tbh: [t.matmul(
                                 PSB[:, k4 % 2 + 2 * tbh, :],
                                 p2s[:, kin, k4 * 128:(k4 + 1) * 128],
                                 x3b[:, tbh, kin, :], start=(kin == 0),
                                 stop=(kin == KT - 1)) for kin in range(KT)],
                             reads=[p2s[:], x3b[:, tbh]],
                             writes=[PSB[:, k4 % 2 + 2 * tbh, :]],
                             noself_tensor=True)
                        bnk = k4 % 2 + 2 * tbh
                        P.op("scalar", lambda s, dk=dk, bnk=bnk, tbh=tbh:
                             s.activation(s1h[:, tbh, :], PSA[:, bnk, :],
                                          AF.Silu,
                                          scale=scps[:, 0, dk:dk + 1]),
                             reads=[PSA[:, bnk, :], scps[:, 0, dk:dk + 1]],
                             writes=[s1h[:, tbh, :]])
                        P.op("vector", lambda v, dk=dk, bnk=bnk, tbh=tbh:
                             v.tensor_scalar(
                                 t2f[:, tbh, :], PSB[:, bnk, :],
                                 scps[:, 1, dk:dk + 1], None, ALU.mult),
                             reads=[PSB[:, bnk, :], scps[:, 1, dk:dk + 1]],
                             writes=[t2f[:, tbh, :]])
                        P.op("vector", lambda v, dk=dk, tbh=tbh: v.tensor_mul(
                            hB[:, dk, tbh * TB:(tbh + 1) * TB],
                            s1h[:, tbh, :], t2f[:, tbh, :]),
                            reads=[s1h[:, tbh, :], t2f[:, tbh, :]],
                            writes=[hB[:, dk, tbh * TB:(tbh + 1) * TB]])

            def hidden(P_, src, dst, wsrc, prow):
                for dc2 in range(16):
                    sl_ = slabs[dc2 % 2]
                    P_.dma("gpsimd", lambda g, dc2=dc2, sl_=sl_: g.dma_start(
                        out=sl_, in_=wsrc.ap()[:, dc2]), writes=[sl_])
                    for sub in range(2):
                        dk = dc2 * 2 + sub
                        for tbh in range(2):
                            bnk = sub + 2 * tbh
                            P_.op("tensor", lambda t, sl_=sl_, sub=sub,
                                  tbh=tbh, bnk=bnk: [t.matmul(
                                      PSA[:, bnk, :],
                                      sl_[:, kin, sub * 128:(sub + 1) * 128],
                                      src[:, kin, tbh * TB:(tbh + 1) * TB],
                                      start=(kin == 0),
                                      stop=(kin == KH - 1))
                                      for kin in range(KH)],
                                  reads=[sl_[:],
                                         src[:, :, tbh * TB:(tbh + 1) * TB]],
                                  writes=[PSA[:, bnk, :]],
                                  noself_tensor=True)
                            P_.op("scalar", lambda s, dk=dk, bnk=bnk, tbh=tbh:
                                  s.activation(
                                      dst[:, dk, tbh * TB:(tbh + 1) * TB],
                                      PSA[:, bnk, :], AF.Gelu,
                                      scale=scps[:, prow, dk:dk + 1]),
                                  reads=[PSA[:, bnk, :],
                                         scps[:, prow, dk:dk + 1]],
                                  writes=[dst[:, dk,
                                              tbh * TB:(tbh + 1) * TB]])
            hidden(P, hB, gB, th1P, 2)
            hidden(P, gB, hB, th2P, 3)

            for tbh in range(2):
                P.dma("sync", lambda s, hf=hf, tbh=tbh: s.dma_start(
                    out=xr4[:, tbh], in_=x3_dram.ap()[:, 2 * hf + tbh]),
                    reads=[x3_dram.ap()[:, 2 * hf + tbh]],
                    writes=[xr4[:, tbh]])
            for dc3 in range(4):
                sl_ = slabs[dc3 % 2]
                P.dma("gpsimd", lambda g, dc3=dc3, sl_=sl_: g.dma_start(
                    out=sl_, in_=tw3P.ap()[:, dc3]), writes=[sl_])
                for sub in range(2):
                    dk = dc3 * 2 + sub
                    for tbh in range(2):
                        bnk = sub + 2 * tbh
                        P.op("tensor", lambda t, sl_=sl_, sub=sub,
                             tbh=tbh, bnk=bnk: [t.matmul(
                                 PSB[:, bnk, :],
                                 sl_[:, kin, sub * 128:(sub + 1) * 128],
                                 hB[:, kin, tbh * TB:(tbh + 1) * TB],
                                 start=(kin == 0), stop=(kin == KH - 1))
                                 for kin in range(KH)],
                             reads=[sl_[:],
                                    hB[:, :, tbh * TB:(tbh + 1) * TB]],
                             writes=[PSB[:, bnk, :]], noself_tensor=True)
                        P.op("vector", lambda v, dk=dk, bnk=bnk, tbh=tbh:
                             v.scalar_tensor_tensor(
                                 xr4[:, tbh, dk, :], PSB[:, bnk, :],
                                 scps[:, 4, dk:dk + 1], xr4[:, tbh, dk, :],
                                 ALU.mult, ALU.add),
                             reads=[PSB[:, bnk, :], scps[:, 4, dk:dk + 1],
                                    xr4[:, tbh, dk, :]],
                             writes=[xr4[:, tbh, dk, :]])
            for tbh in range(2):
                cc = 2 * hf + tbh
                _layer_norm(P, xr4[:, tbh], nrm, sqbD, ones_bf, scs, PSA, PSB, mu_i=0, rs_i=1)
                tk = P.dma("sync", lambda s, cc=cc, tbh=tbh: s.dma_start(
                    out=x4_out.ap()[:, cc], in_=xr4[:, tbh]),
                    reads=[xr4[:, tbh]])
                x4_toks.append(tk)
        P.op("sync", lambda s: s.nop(), deps=P.queue_total_deps("sync"),
             inc=0)

        P.emit(nc, ctx)
    return nc



def _img(w):
    K = w.shape[0] // 128
    return np.ascontiguousarray(
        w.reshape(K, 128, w.shape[1]).transpose(1, 0, 2))


def _img_chunks(w, nch):
    K = w.shape[0] // 128
    N = w.shape[1]
    return np.ascontiguousarray(
        w.reshape(K, 128, nch, N // nch).transpose(1, 2, 0, 3))


_NC_CACHE = {}


def kernel(L0_attn_x, L0_attn_Wq, L0_attn_Wk, L0_attn_Wv, L0_ffn_gate_w,
           L0_ffn_expert_bias, L0_ffn_w1, L0_ffn_b1, L0_ffn_w2, L0_ffn_b2,
           L0_ffn_w3, L0_ffn_b3, L0_ffn_sw1, L0_ffn_sb1, L0_ffn_sw2,
           L0_ffn_sb2, L0_ffn_sw3, L0_ffn_sb3, L1_attn_Wr, L1_attn_Wk,
           L1_attn_Wv, L1_attn_Ww, L1_attn_Wo, L1_ffn_w1_w, L1_ffn_w1_s,
           L1_ffn_w2_w, L1_ffn_w2_s, L1_ffn_h1_w, L1_ffn_h1_s, L1_ffn_h2_w,
           L1_ffn_h2_s, L1_ffn_w3_w, L1_ffn_w3_s):
    TFULL = L0_attn_x.shape[1]
    T = TFULL // NCORE
    TB = 512
    NTB = T // TB
    NCH = TFULL // T

    f = np.float32
    x_cm = np.ascontiguousarray(L0_attn_x[0].T.astype(f, copy=False))
    xfull_img = np.ascontiguousarray(
        x_cm.reshape(KT, 128, NCH, T).transpose(1, 2, 0, 3))

    wqP = _img(L0_attn_Wq.astype(f, copy=False))
    wkP = _img(L0_attn_Wk.astype(f, copy=False))
    wvP = _img(L0_attn_Wv.astype(f, copy=False))
    gateP = _img(np.ascontiguousarray(L0_ffn_gate_w.T.astype(f, copy=False)))
    ebias = np.ascontiguousarray(
        np.broadcast_to(L0_ffn_expert_bias.astype(f), (128, E_MOE)))

    w1T = L0_ffn_w1.transpose(0, 2, 1).astype(f, copy=False)
    w3T = L0_ffn_w3.transpose(0, 2, 1).astype(f, copy=False)
    w2T = L0_ffn_w2.transpose(0, 2, 1).astype(f, copy=False)
    sw1T = L0_ffn_sw1.T.astype(f, copy=False)
    sw3T = L0_ffn_sw3.T.astype(f, copy=False)
    sw2T = L0_ffn_sw2.T.astype(f, copy=False)
    we1T = np.concatenate(
        [w1T, sw1T[None, :, 0:DFF], sw1T[None, :, DFF:2 * DFF]], 0)
    we3T = np.concatenate(
        [w3T, sw3T[None, :, 0:DFF], sw3T[None, :, DFF:2 * DFF]], 0)
    we2T = np.concatenate(
        [w2T, sw2T[None, 0:DFF, :], sw2T[None, DFF:2 * DFF, :]], 0)
    weP = np.ascontiguousarray(np.stack(
        [np.concatenate([_img(we1T[e]).reshape(128, -1),
                         _img(we3T[e]).reshape(128, -1),
                         _img(we2T[e]).reshape(128, -1)], axis=1)
         for e in range(NEXP)]))

    def packb(b):
        n = b.shape[0] // 128
        return np.ascontiguousarray(b.reshape(n, 128).T.astype(f, copy=False))

    b1cat = np.concatenate([L0_ffn_b1.reshape(-1), L0_ffn_sb1], 0)
    b3cat = np.concatenate([L0_ffn_b3.reshape(-1), L0_ffn_sb3], 0)
    b1p = packb(b1cat)
    b3p = packb(b3cat)
    b2xT = np.ascontiguousarray(
        np.concatenate([L0_ffn_b2, L0_ffn_sb2[None]], 0).astype(f, copy=False))

    wrP = _img(np.ascontiguousarray(L1_attn_Wr.T.astype(f, copy=False)))
    wk2P = _img(np.ascontiguousarray(L1_attn_Wk.T.astype(f, copy=False)))
    wv2P = _img(np.ascontiguousarray(L1_attn_Wv.T.astype(f, copy=False)))
    wwP = _img(np.ascontiguousarray(L1_attn_Ww.T.astype(f, copy=False)))
    woP = _img(np.ascontiguousarray(L1_attn_Wo.T.astype(f, copy=False)))
    _t1 = _img_chunks(np.ascontiguousarray(L1_ffn_w1_w.T.astype(f, copy=False)), 8)
    _t2 = _img_chunks(np.ascontiguousarray(L1_ffn_w2_w.T.astype(f, copy=False)), 8)
    tw12P = np.ascontiguousarray(np.concatenate(
        [_t1.reshape(128, 8, -1), _t2.reshape(128, 8, -1)], axis=2))
    th1P = _img_chunks(np.ascontiguousarray(L1_ffn_h1_w.T.astype(f, copy=False)), 16)
    th2P = _img_chunks(np.ascontiguousarray(L1_ffn_h2_w.T.astype(f, copy=False)), 16)
    tw3P = _img_chunks(np.ascontiguousarray(L1_ffn_w3_w.T.astype(f, copy=False)), 4)
    s1p = packb(L1_ffn_w1_s)
    s2p = packb(L1_ffn_w2_s)
    sh1p = packb(L1_ffn_h1_s)
    sh2p = packb(L1_ffn_h2_s)
    s3p = packb(L1_ffn_w3_s)

    key = ("l1", T)
    if key not in _NC_CACHE:
        _NC_CACHE[key] = build_launch1(T)
    nc1 = _NC_CACHE[key]

    shared1 = dict(wqP=wqP, wkP=wkP, wvP=wvP, gateP=gateP, ebias=ebias,
                   weP=weP, b1p=b1p, b3p=b3p,
                   b2xT=b2xT, wrP=wrP, wk2P=wk2P, wv2P=wv2P, wwP=wwP,
                   xfull=xfull_img)
    in_maps1 = []
    for c in range(NCORE):
        xl = np.ascontiguousarray(
            x_cm[:, c * T:(c + 1) * T].reshape(KT, 128, NTB, TB)
            .transpose(1, 2, 0, 3))
        m = dict(shared1)
        m["xloc"] = xl
        in_maps1.append(m)
    res1 = run_bass_kernel_spmd(nc1, in_maps1, core_ids=list(range(NCORE)))

    exall = np.ascontiguousarray(
        np.stack([res1.results[c]["ex_out"] for c in range(NCORE)], axis=1))

    key2 = ("l2", T)
    if key2 not in _NC_CACHE:
        _NC_CACHE[key2] = build_launch2(T)
    nc2 = _NC_CACHE[key2]

    shared2 = dict(exall=exall, woP=woP, tw12P=tw12P, th1P=th1P,
                   th2P=th2P, tw3P=tw3P, s1p=s1p, s2p=s2p, sh1p=sh1p,
                   sh2p=sh2p, s3p=s3p)
    in_maps2 = []
    for c in range(NCORE):
        cm = np.zeros((128, 8), f)
        cm[:, :c] = 1.0
        m = dict(shared2)
        m["x2sp"] = res1.results[c]["x2_out"]
        m["rsp"] = res1.results[c]["r_out"]
        m["wsp"] = res1.results[c]["w_out"]
        m["zsp"] = res1.results[c]["z_out"]
        m["cmsk"] = cm
        m["cm1m"] = np.ascontiguousarray(1.0 - cm)
        in_maps2.append(m)
    res2 = run_bass_kernel_spmd(nc2, in_maps2, core_ids=list(range(NCORE)))

    out = np.empty((1, TFULL, D), np.float32)
    for c in range(NCORE):
        img = res2.results[c]["x4_out"]
        xc = img.transpose(2, 0, 1, 3).reshape(D, T)
        out[0, c * T:(c + 1) * T, :] = xc.T
    return out


# version 06
# speedup vs baseline: 1.0132x; 1.0025x over previous
import numpy as np
from contextlib import ExitStack

import concourse.bass as bass
import concourse.mybir as mybir
from concourse.bass_utils import run_bass_kernel_spmd

F32 = mybir.dt.float32
BF16 = mybir.dt.bfloat16
U8 = mybir.dt.uint8
AF = mybir.ActivationFunctionType
ALU = mybir.AluOpType
AX = mybir.AxisListType.X

NCORE = 8
D = 1024
KT = 8
GAMMA = 0.96875
LN_EPS = 1e-5
RMS_EPS = float(np.finfo(np.float32).eps)
E_MOE = 16
DFF = 512
KF = 4
NEXP = 18
DHID = 4096
KH = 32
SEG = 4096


def _ap_range(ap):
    sz = mybir.dt.size(ap.dtype)
    dims = list(ap.ap)
    is_dram = "DRam" in type(ap.tensor).__name__
    off = int(ap.offset)
    if not is_dram:
        row = 1
        for d in list(ap.tensor.shape)[1:]:
            row *= int(d)
        off = off % row
    use = dims if is_dram else dims[1:]
    ext = 0
    for step, count in use:
        ext += abs(int(step)) * (int(count) - 1)
    lo = off * sz
    return ap.tensor.name, lo, lo + (ext + 1) * sz


class Prog:
    ENGS = ("sync", "gpsimd", "vector", "scalar", "tensor")
    NQ = 8
    SEMS = tuple(["gpsimd", "vector", "scalar", "tensor"]
                 + [f"sy{i}" for i in range(8)] + [f"gd{i}" for i in range(8)])

    def __init__(self):
        self.items = []
        self.cnt = dict.fromkeys(self.SEMS, 0)
        self.dmai = {"sync": 0, "gpsimd": 0}
        self.wseg = {}
        self.rseg = {}

    def _segs(self, lo, hi):
        return range(lo // SEG, (hi - 1) // SEG + 1)

    def op(self, eng, fn, reads=(), writes=(), deps=(), inc=1, rank=False,
           noself_tensor=False, sem=None):
        sem = sem or (eng if eng in self.cnt else "sy0")
        dep = set(deps)
        for ap in reads:
            name, lo, hi = _ap_range(ap)
            wm = self.wseg.get(name)
            if wm:
                for s in self._segs(lo, hi):
                    t = wm.get(s)
                    if t:
                        dep.add(t)
        for ap in writes:
            name, lo, hi = _ap_range(ap)
            wm = self.wseg.get(name)
            rm = self.rseg.get(name)
            for s in self._segs(lo, hi):
                if wm and (t := wm.get(s)):
                    dep.add(t)
                if rm:
                    dep.update(rm.get(s, ()))
        if noself_tensor:
            dep = {d for d in dep if d[0] != "tensor"}
        self.cnt[sem] += inc
        tok = (sem, self.cnt[sem])
        self.items.append((eng, sem, fn, sorted(dep), inc, rank))
        for ap in reads:
            name, lo, hi = _ap_range(ap)
            rm = self.rseg.setdefault(name, {})
            for s in self._segs(lo, hi):
                rm.setdefault(s, set()).add(tok)
        for ap in writes:
            name, lo, hi = _ap_range(ap)
            wm = self.wseg.setdefault(name, {})
            rm = self.rseg.setdefault(name, {})
            for s in self._segs(lo, hi):
                wm[s] = tok
                rm[s] = set()
        return tok

    def dma(self, eng, fn, reads=(), writes=(), deps=()):
        idx = self.dmai[eng]
        self.dmai[eng] = idx + 1
        pre = "sy" if eng == "sync" else "gd"
        sem = f"pre{idx % self.NQ}".replace("pre", pre)
        deps = list(deps)
        if idx >= self.NQ:
            deps.append((sem, 16 * (idx // self.NQ)))
        return self.op(eng, fn, reads, writes, deps, inc=16, sem=sem)

    def queue_total_deps(self, eng):
        pre = "sy" if eng == "sync" else "gd"
        return [(f"{pre}{i}", self.cnt[f"{pre}{i}"]) for i in range(self.NQ)]

    def emit(self, nc, ctx):
        sems = {e: ctx.enter_context(nc.semaphore(f"pg_{e}")) for e in self.SEMS}
        block = ctx.enter_context(nc.Block())
        items = self.items

        def mk(eng_name):
            def body(eng):
                waited = {}
                for (e, sm, fn, deps, inc, rank) in items:
                    if e != eng_name:
                        continue
                    for (de, dc) in deps:
                        if waited.get(de, 0) >= dc:
                            continue
                        eng.wait_ge(sems[de], dc)
                        waited[de] = dc
                    if rank:
                        for arm in eng.switch_core_id(n=128):
                            insts = fn(eng, arm.logical % NCORE)
                            if not isinstance(insts, (list, tuple)):
                                insts = [insts]
                            assert len(insts) == inc
                            for i in insts:
                                i.then_inc(sems[sm], 1)
                    elif inc == 0:
                        fn(eng)
                    else:
                        insts = fn(eng)
                        if isinstance(insts, (list, tuple)):
                            insts[-1].then_inc(sems[sm], inc)
                        else:
                            insts.then_inc(sems[sm], inc)
            return body

        for e in self.ENGS:
            getattr(block, e)(mk(e))


class Pool:

    def __init__(self, nc, ctx, name, nbytes):
        assert nbytes % 4 == 0
        self.t = ctx.enter_context(nc.sbuf_tensor(name, [128, nbytes // 4], F32))

    def view(self, off, dtype, shape):
        assert off % 4 == 0
        n = int(np.prod(shape))
        if dtype == F32:
            ap = self.t[:, off // 4:off // 4 + n]
        elif dtype == BF16:
            assert n % 2 == 0
            ap = self.t[:, off // 4:off // 4 + n // 2].bitcast(BF16)
        else:
            raise ValueError(dtype)
        if len(shape) == 1:
            return ap
        pat = "p (" + " ".join(f"d{i}" for i in range(len(shape))) + ") -> p " + \
              " ".join(f"d{i}" for i in range(len(shape)))
        return ap.rearrange(pat, **{f"d{i}": s for i, s in enumerate(shape)})


def _layer_norm(P, xr, xrb, sqb, ones_bf, scs, PSA, PSB, mu_i=4, rs_i=5):
    P.op("scalar", lambda s: s.copy(xrb[:], xr[:]), reads=[xr[:]],
         writes=[xrb[:]])
    P.op("scalar", lambda s: s.activation(sqb[:], xrb[:], AF.Square),
         reads=[xrb[:]], writes=[sqb[:]])
    P.op("tensor", lambda t: [t.matmul(
        PSA[:, 3, :], ones_bf[:], xrb[:, kt, :],
        start=(kt == 0), stop=(kt == KT - 1)) for kt in range(KT)],
        reads=[ones_bf[:], xrb[:]], writes=[PSA[:, 3, :]],
        noself_tensor=True)
    P.op("tensor", lambda t: [t.matmul(
        PSB[:, 2, :], ones_bf[:], sqb[:, kt, :],
        start=(kt == 0), stop=(kt == KT - 1)) for kt in range(KT)],
        reads=[ones_bf[:], sqb[:]], writes=[PSB[:, 2, :]],
        noself_tensor=True)
    P.op("vector", lambda v: v.tensor_scalar(
        scs[:, mu_i, :], PSA[:, 3, :], 1.0 / D, None, ALU.mult),
        reads=[PSA[:, 3, :]], writes=[scs[:, mu_i, :]])
    P.op("scalar", lambda s: s.activation(
        scs[:, rs_i, :], scs[:, mu_i, :], AF.Square),
        reads=[scs[:, mu_i, :]], writes=[scs[:, rs_i, :]])
    P.op("vector", lambda v: v.scalar_tensor_tensor(
        scs[:, rs_i, :], PSB[:, 2, :], 1.0 / D, scs[:, rs_i, :],
        ALU.mult, ALU.subtract),
        reads=[PSB[:, 2, :], scs[:, rs_i, :]], writes=[scs[:, rs_i, :]])
    P.op("vector", lambda v: v.tensor_scalar_add(
        scs[:, rs_i, :], scs[:, rs_i, :], LN_EPS),
        reads=[scs[:, rs_i, :]], writes=[scs[:, rs_i, :]])
    P.op("scalar", lambda s: s.activation(
        scs[:, rs_i, :], scs[:, rs_i, :], AF.Sqrt),
        reads=[scs[:, rs_i, :]], writes=[scs[:, rs_i, :]])
    P.op("vector", lambda v: v.reciprocal(scs[:, rs_i, :], scs[:, rs_i, :]),
         reads=[scs[:, rs_i, :]], writes=[scs[:, rs_i, :]])
    for kt in range(KT):
        P.op("vector", lambda v, kt=kt: v.tensor_tensor(
            xr[:, kt, :], xr[:, kt, :], scs[:, mu_i, :], ALU.subtract),
            reads=[xr[:, kt, :], scs[:, mu_i, :]], writes=[xr[:, kt, :]])
        P.op("vector", lambda v, kt=kt: v.tensor_tensor(
            xr[:, kt, :], xr[:, kt, :], scs[:, rs_i, :], ALU.mult),
            reads=[xr[:, kt, :], scs[:, rs_i, :]], writes=[xr[:, kt, :]])


def build_launch1(T, debug=False, upto=3):
    TFULL = 8 * T
    assert TFULL // D == 16
    TB = 512
    NTB = T // TB
    NCH = TFULL // T
    JC = T // 16
    NTT = T // 128

    nc = bass.Bass("TRN2", target_bir_lowering=False)
    P = Prog()
    dt = nc.dram_tensor
    xfull = dt("xfull", [128, NCH, KT, T], F32, kind="ExternalInput")
    xloc = dt("xloc", [128, NTB, KT, TB], F32, kind="ExternalInput")
    wqP = dt("wqP", [128, KT, D], F32, kind="ExternalInput")
    wkP = dt("wkP", [128, KT, D], F32, kind="ExternalInput")
    wvP = dt("wvP", [128, KT, D], F32, kind="ExternalInput")
    gateP = dt("gateP", [128, KT, E_MOE], F32, kind="ExternalInput")
    ebias = dt("ebias", [128, E_MOE], F32, kind="ExternalInput")
    weP = dt("weP", [NEXP, 128, 3 * KT * DFF], F32, kind="ExternalInput")
    b1p = dt("b1p", [128, NEXP * KF], F32, kind="ExternalInput")
    b3p = dt("b3p", [128, NEXP * KF], F32, kind="ExternalInput")
    b2xT = dt("b2xT", [17, D], F32, kind="ExternalInput")
    wrP = dt("wrP", [128, KT, D], F32, kind="ExternalInput")
    wk2P = dt("wk2P", [128, KT, D], F32, kind="ExternalInput")
    wv2P = dt("wv2P", [128, KT, D], F32, kind="ExternalInput")
    wwP = dt("wwP", [128, KT, D], F32, kind="ExternalInput")

    x2_out = dt("x2_out", [128, NTB, KT, TB], F32, kind="ExternalOutput")
    r_out = dt("r_out", [128, NTB, KT, TB], F32, kind="ExternalOutput")
    w_out = dt("w_out", [128, NTB, KT, TB], F32, kind="ExternalOutput")
    z_out = dt("z_out", [128, NTB, KT, TB], F32, kind="ExternalOutput")
    ex_out = dt("ex_out", [128, 16], F32, kind="ExternalOutput")
    if debug:
        x1_dram = dt("x1_dram", [128, NTB, KT, TB], F32, kind="ExternalOutput")
        psel_out = dt("psel_out", [128, KT, T], BF16, kind="ExternalOutput")
    else:
        x1_dram = dt("x1_dram", [128, NTB, KT, TB], F32)
    x2b_dram = dt("x2b_dram", [128, NTB, KT, TB], BF16)
    g_dram = dt("g_dram", [NEXP, T], F32)

    with ExitStack() as ctx:
        PB = Pool(nc, ctx, "PB", 176 * 1024)
        R_AP, R_SF, R_XB, R_PS = 0, 64 * 1024, 128 * 1024, 144 * 1024

        sb = lambda n, s, d: ctx.enter_context(nc.sbuf_tensor(n, s, d))
        ident = sb("ident", [128, 128], F32)
        ones_bf = sb("ones_bf", [128, 128], BF16)
        bdiag = sb("bdiag", [128, 128], BF16)
        gamma1 = sb("gamma1", [128, 1], F32)
        onesrow = sb("onesrow", [1, 512], BF16)
        wvs = sb("wvs", [128, KT], F32)
        wcol = sb("wcol", [128, KT, 128], BF16)
        carryA = sb("carryA", [128, KT], F32)
        scs = sb("scs", [128, 6, TB], F32)
        b1s = sb("b1s", [128, NEXP * KF], F32)
        b3s = sb("b3s", [128, NEXP * KF], F32)
        ebias_s = sb("ebias_s", [128, E_MOE], F32)
        glog = sb("glog", [128, NTT, E_MOE], F32)
        gsig = sb("gsig", [128, NTT, E_MOE], F32)
        gbia = sb("gbia", [128, NTT, E_MOE], F32)
        gmk1 = sb("gmk1", [128, NTT, E_MOE], F32)
        gmk2 = sb("gmk2", [128, NTT, E_MOE], F32)
        gwv = sb("gwv", [128, NTT, E_MOE], F32)
        gm = sb("gm", [128, NTT, 4], F32)
        b2xs = sb("b2xs", [17, D], BF16)
        carryU = sb("carryU", [128, KT], F32)
        carryP = sb("carryP", [128, KT], F32)
        ex1 = sb("ex1", [128, 16], F32)

        PSA = ctx.enter_context(nc.psum_tensor("psa", [128, 4, 512], F32))
        PSB = ctx.enter_context(nc.psum_tensor("psb", [128, 4, 512], F32))
        PST = PSB[:, 3, 0:128]

        P.op("gpsimd", lambda g: g.memset(ident[:], 0.0), writes=[ident[:]])
        P.op("gpsimd", lambda g: g.affine_select(
            out=ident[:], in_=ident[:], compare_op=ALU.not_equal, fill=1.0,
            base=0, pattern=[[-1, 128]], channel_multiplier=1),
            reads=[ident[:]], writes=[ident[:]])
        P.op("gpsimd", lambda g: g.memset(ones_bf[:], 1.0), writes=[ones_bf[:]])
        P.op("gpsimd", lambda g: g.memset(bdiag[:], 0.0), writes=[bdiag[:]])
        P.op("gpsimd", lambda g: g.memset(bdiag[0:64, 0:64], 1.0),
             reads=[bdiag[:]], writes=[bdiag[0:64, 0:64]])
        P.op("gpsimd", lambda g: g.memset(bdiag[64:128, 64:128], 1.0),
             reads=[bdiag[:]], writes=[bdiag[64:128, 64:128]])
        P.op("gpsimd", lambda g: g.memset(gamma1[:], GAMMA), writes=[gamma1[:]])
        P.op("gpsimd", lambda g: g.memset(onesrow[:], 1.0), writes=[onesrow[:]])
        P.dma("sync", lambda s: s.dma_start(out=b1s[:], in_=b1p.ap()),
              writes=[b1s[:]])
        P.dma("sync", lambda s: s.dma_start(out=b3s[:], in_=b3p.ap()),
              writes=[b3s[:]])
        P.dma("sync", lambda s: s.dma_start(out=ebias_s[:], in_=ebias.ap()),
              writes=[ebias_s[:]])
        P.dma("gpsimd", lambda g: g.dma_start(out=b2xs[:], in_=b2xT.ap()),
              writes=[b2xs[:]])

        wv_slab = PB.view(R_AP, F32, (KT, D))
        for kt in range(KT):
            P.dma("sync", lambda s, kt=kt: s.dma_start(
                out=wv_slab[:, kt, :], in_=wvP.ap()[:, kt, :]),
                writes=[wv_slab[:, kt, :]])
            P.op("vector", lambda v, kt=kt: v.reduce_sum(
                wvs[:, kt:kt + 1], wv_slab[:, kt, :], axis=AX),
                reads=[wv_slab[:, kt, :]], writes=[wvs[:, kt:kt + 1]])
        for kt in range(KT):
            P.op("vector", lambda v, kt=kt: v.tensor_scalar(
                wcol[:, kt, :], ones_bf[:], wvs[:, kt:kt + 1], None, ALU.mult),
                reads=[ones_bf[:], wvs[:, kt:kt + 1]], writes=[wcol[:, kt, :]])

        xb_ch = PB.view(R_SF, BF16, (KT, T))
        APbuf = PB.view(R_AP, F32, (KT, T))
        psel = PB.view(R_PS, BF16, (KT, T))
        ap4 = APbuf.rearrange("p k (j a) -> p k j a", a=16)

        for ct in range(NCH):
            P.dma("gpsimd", lambda g, ct=ct: g.dma_start(
                out=xb_ch, in_=xfull.ap()[:, ct]), writes=[xb_ch])
            for tb in range(NTB):
                sl = slice(tb * TB, (tb + 1) * TB)
                P.op("tensor", lambda t, sl=sl: [t.matmul(
                    PSA[:, 0, :], wcol[:, kt, :], xb_ch[:, kt, sl],
                    start=(kt == 0), stop=(kt == KT - 1)) for kt in range(KT)],
                    reads=[wcol[:], xb_ch[:, :, sl]],
                    writes=[PSA[:, 0, :]], noself_tensor=True)
                for kt in range(KT):
                    P.op("vector", lambda v, kt=kt, sl=sl: v.tensor_tensor(
                        APbuf[:, kt, sl], xb_ch[:, kt, sl], PSA[:, 0, :],
                        ALU.mult),
                        reads=[xb_ch[:, kt, sl], PSA[:, 0, :]],
                        writes=[APbuf[:, kt, sl]])
            if ct == 0:
                P.op("vector", lambda v: v.memset(APbuf[:, :, 0:1], 0.0),
                     writes=[APbuf[:, :, 0:1]])
            for kt in range(KT):
                init = 0.0 if ct == 0 else carryA[:, kt:kt + 1]
                rd = [gamma1[:], APbuf[:, kt, :]]
                if ct > 0:
                    rd.append(carryA[:, kt:kt + 1])
                P.op("vector", lambda v, kt=kt, init=init: v.tensor_tensor_scan(
                    APbuf[:, kt, :], gamma1[:].broadcast_to([128, T]),
                    APbuf[:, kt, :], init,
                    ALU.mult, ALU.add), reads=rd, writes=[APbuf[:, kt, :]])
                P.op("vector", lambda v, kt=kt: v.tensor_copy(
                    carryA[:, kt:kt + 1], APbuf[:, kt, T - 1:T]),
                    reads=[APbuf[:, kt, T - 1:T]], writes=[carryA[:, kt:kt + 1]])
            if ct == 0:
                P.op("vector", lambda v: v.tensor_copy(
                    APbuf[:, :, 0:1], APbuf[:, :, 1:2]),
                    reads=[APbuf[:, :, 1:2]], writes=[APbuf[:, :, 0:1]])

            def ext(vv, r, ct=ct):
                i0 = vv.tensor_copy(psel[:, :, ct * JC:(ct + 1) * JC],
                                    ap4[:, :, :, 2 * r])
                i1 = vv.tensor_copy(
                    psel[:, :, T // 2 + ct * JC:T // 2 + (ct + 1) * JC],
                    ap4[:, :, :, 2 * r + 1])
                return [i0, i1]
            P.op("vector", ext, reads=[APbuf[:, :, :]],
                 writes=[psel[:, :, ct * JC:(ct + 1) * JC],
                         psel[:, :, T // 2 + ct * JC:T // 2 + (ct + 1) * JC]],
                 rank=True, inc=2)

        if debug and upto == 0:
            P.dma("sync", lambda s: s.dma_start(out=psel_out.ap(), in_=psel),
                  reads=[psel])
            P.op("sync", lambda s: s.nop(), deps=P.queue_total_deps("sync"),
                 inc=0)
            P.emit(nc, ctx)
            return nc

        wk_sb = PB.view(R_AP, BF16, (KT, D))
        wq_sb = PB.view(R_AP + 16 * 1024, BF16, (KT, D))
        xb_tb = PB.view(R_AP + 32 * 1024, BF16, (KT, TB))
        xrb = PB.view(R_AP + 40 * 1024, BF16, (KT, TB))
        sqb = PB.view(R_AP + 48 * 1024, BF16, (KT, TB))
        sstage = PB.view(R_AP + 56 * 1024, F32, (TB,))
        sfin = PB.view(R_SF, F32, (KT, T))
        xloc_f = PB.view(R_XB, F32, (KT, TB))
        x1b = PB.view(R_PS, BF16, (KT, T))

        P.dma("gpsimd", lambda g: g.dma_start(out=wk_sb, in_=wkP.ap()),
              writes=[wk_sb])
        P.dma("gpsimd", lambda g: g.dma_start(out=wq_sb, in_=wqP.ap()),
              writes=[wq_sb])

        for bkt in range(KT):
            for sbk in range(T // TB):
                pb = PSA[:, sbk % 2, :]
                P.op("tensor", lambda t, bkt=bkt, sbk=sbk, pb=pb:
                     [t.matmul(pb, wk_sb[:, kin, bkt * 128:(bkt + 1) * 128],
                               psel[:, kin, sbk * TB:(sbk + 1) * TB],
                               start=(kin == 0), stop=(kin == KT - 1))
                      for kin in range(KT)],
                     reads=[wk_sb[:], psel[:, :, sbk * TB:(sbk + 1) * TB]],
                     writes=[pb], noself_tensor=True)
                P.op("scalar", lambda s, pb=pb: s.copy(sstage[:], pb),
                     reads=[pb], writes=[sstage[:]])
                for j in range(TB // 128):
                    s_abs = sbk * TB + j * 128
                    a_loc = s_abs // (T // 2)
                    ikt = (s_abs % (T // 2)) // 128
                    col = a_loc * (T // 2) + bkt * 128
                    P.op("tensor", lambda t, j=j: t.transpose(
                        PST, sstage[:, j * 128:(j + 1) * 128], ident[:]),
                        reads=[sstage[:, j * 128:(j + 1) * 128], ident[:]],
                        writes=[PST])
                    P.op("vector", lambda v, ikt=ikt, col=col: v.tensor_copy(
                        sfin[:, ikt, col:col + 128], PST),
                        reads=[PST], writes=[sfin[:, ikt, col:col + 128]])

        for tb in range(NTB):
            sl = slice(tb * TB, (tb + 1) * TB)
            P.dma("sync", lambda s, tb=tb: s.dma_start(
                out=xloc_f, in_=xloc.ap()[:, tb]), writes=[xloc_f])
            P.dma("gpsimd", lambda g, tb=tb: g.dma_start(
                out=xb_tb, in_=xloc.ap()[:, tb]), writes=[xb_tb])
            xr = xloc_f
            for kto in range(KT):
                P.op("tensor", lambda t, kto=kto: [t.matmul(
                    PSA[:, 2, :], wq_sb[:, kin, kto * 128:(kto + 1) * 128],
                    xb_tb[:, kin, :], start=(kin == 0),
                    stop=(kin == KT - 1)) for kin in range(KT)],
                    reads=[wq_sb[:], xb_tb[:]],
                    writes=[PSA[:, 2, :]], noself_tensor=True)
                P.op("vector", lambda v, kto=kto, sl=sl: v.scalar_tensor_tensor(
                    scs[:, 0, :], PSA[:, 2, :], 1.0, sfin[:, kto, sl],
                    ALU.mult, ALU.mult),
                    reads=[PSA[:, 2, :], sfin[:, kto, sl]],
                    writes=[scs[:, 0, :]])
                P.op("vector", lambda v, kto=kto: v.tensor_add(
                    xr[:, kto, :], xr[:, kto, :], scs[:, 0, :]),
                    reads=[xr[:, kto, :], scs[:, 0, :]], writes=[xr[:, kto, :]])
            _layer_norm(P, xr, xrb, sqb, ones_bf, scs, PSA, PSB)
            P.dma("sync", lambda s, tb=tb: s.dma_start(
                out=x1_dram.ap()[:, tb], in_=xr[:]), reads=[xr[:]])
            P.op("scalar", lambda s, sl=sl: s.copy(x1b[:, :, sl], xr[:]),
                 reads=[xr[:]], writes=[x1b[:, :, sl]])

        if upto == 1:
            P.op("sync", lambda s: s.nop(), deps=P.queue_total_deps("sync"),
                 inc=0)
            P.emit(nc, ctx)
            return nc

        gT_sb = PB.view(R_XB, BF16, (KT, E_MOE))
        gxs = PB.t[0:17, (R_XB + 4096) // 4:(R_XB + 8192) // 4].bitcast(BF16)
        gxf = PB.t[0:16, (R_XB + 8192) // 4:(R_XB + 16384) // 4]
        P.dma("gpsimd", lambda g: g.dma_start(out=gT_sb, in_=gateP.ap()),
              writes=[gT_sb])
        for tt in range(NTT):
            bank = tt % 2
            P.op("tensor", lambda t, tt=tt, bank=bank: [t.matmul(
                PSA[:, bank, 0:16],
                x1b[:, kin, tt * 128:(tt + 1) * 128], gT_sb[:, kin, :],
                start=(kin == 0), stop=(kin == KT - 1)) for kin in range(KT)],
                reads=[x1b[:, :, tt * 128:(tt + 1) * 128], gT_sb[:]],
                writes=[PSA[:, bank, 0:16]],
                noself_tensor=True)
            P.op("vector", lambda v, tt=tt, bank=bank: v.tensor_copy(
                glog[:, tt, :], PSA[:, bank, 0:16]),
                reads=[PSA[:, bank, 0:16]], writes=[glog[:, tt, :]])
        P.op("scalar", lambda s: s.activation(gsig[:], glog[:], AF.Sigmoid),
             reads=[glog[:]], writes=[gsig[:]])
        for tt in range(NTT):
            P.op("vector", lambda v, tt=tt: v.tensor_add(
                gbia[:, tt, :], glog[:, tt, :], ebias_s[:]),
                reads=[glog[:, tt, :], ebias_s[:]], writes=[gbia[:, tt, :]])
            P.op("vector", lambda v, tt=tt: v.reduce_max(
                gm[:, tt, 0:1], gbia[:, tt, :], axis=AX),
                reads=[gbia[:, tt, :]], writes=[gm[:, tt, 0:1]])
            P.op("vector", lambda v, tt=tt: v.tensor_scalar(
                gmk1[:, tt, :], gbia[:, tt, :], gm[:, tt, 0:1], None,
                ALU.is_equal),
                reads=[gbia[:, tt, :], gm[:, tt, 0:1]], writes=[gmk1[:, tt, :]])
            P.op("vector", lambda v, tt=tt: v.scalar_tensor_tensor(
                gmk2[:, tt, :], gmk1[:, tt, :], -1e30, gbia[:, tt, :],
                ALU.mult, ALU.add),
                reads=[gmk1[:, tt, :], gbia[:, tt, :]], writes=[gmk2[:, tt, :]])
            P.op("vector", lambda v, tt=tt: v.reduce_max(
                gm[:, tt, 1:2], gmk2[:, tt, :], axis=AX),
                reads=[gmk2[:, tt, :]], writes=[gm[:, tt, 1:2]])
            P.op("vector", lambda v, tt=tt: v.tensor_scalar(
                gmk2[:, tt, :], gmk2[:, tt, :], gm[:, tt, 1:2], None,
                ALU.is_equal),
                reads=[gmk2[:, tt, :], gm[:, tt, 1:2]], writes=[gmk2[:, tt, :]])
            P.op("vector", lambda v, tt=tt: v.tensor_add(
                gmk1[:, tt, :], gmk1[:, tt, :], gmk2[:, tt, :]),
                reads=[gmk1[:, tt, :], gmk2[:, tt, :]], writes=[gmk1[:, tt, :]])
            P.op("vector", lambda v, tt=tt: v.tensor_mul(
                gwv[:, tt, :], gsig[:, tt, :], gmk1[:, tt, :]),
                reads=[gsig[:, tt, :], gmk1[:, tt, :]], writes=[gwv[:, tt, :]])
            P.op("vector", lambda v, tt=tt: v.reduce_sum(
                gm[:, tt, 2:3], gwv[:, tt, :], axis=AX),
                reads=[gwv[:, tt, :]], writes=[gm[:, tt, 2:3]])
            P.op("vector", lambda v, tt=tt: v.tensor_scalar_add(
                gm[:, tt, 2:3], gm[:, tt, 2:3], 1e-10),
                reads=[gm[:, tt, 2:3]], writes=[gm[:, tt, 2:3]])
            P.op("vector", lambda v, tt=tt: v.reciprocal(
                gm[:, tt, 3:4], gm[:, tt, 2:3]),
                reads=[gm[:, tt, 2:3]], writes=[gm[:, tt, 3:4]])
            P.op("vector", lambda v, tt=tt: v.tensor_scalar(
                gwv[:, tt, :], gwv[:, tt, :], gm[:, tt, 3:4], None, ALU.mult),
                reads=[gwv[:, tt, :], gm[:, tt, 3:4]], writes=[gwv[:, tt, :]])
        gtr = scs[:, 0, 0:128]
        for blkI in range(NTT // 8):
            gblk = gwv.ap().rearrange("p a e -> p (a e)")[
                :, blkI * 128:(blkI + 1) * 128]
            P.op("tensor", lambda t, gblk=gblk: t.transpose(
                PST, gblk, ident[:]),
                reads=[gblk, ident[:]], writes=[PST])
            P.op("vector", lambda v: v.tensor_copy(gtr, PST),
                 reads=[PST], writes=[gtr])
            dap = g_dram.ap()[0:16, blkI * 1024:(blkI + 1) * 1024].rearrange(
                "e (q p) -> q e p", p=128)
            P.dma("sync", lambda s, dap=dap: s.dma_start(out=dap, in_=gtr),
                  reads=[gtr], writes=[g_dram.ap()[0:16,
                                                   blkI * 1024:(blkI + 1) * 1024]])
        for rr in (16, 17):
            for q in range(T // 512):
                P.dma("gpsimd", lambda g, rr=rr, q=q: g.dma_start(
                    out=g_dram.ap()[rr:rr + 1, q * 512:(q + 1) * 512],
                    in_=onesrow[:]),
                    reads=[onesrow[:]],
                    writes=[g_dram.ap()[rr:rr + 1, q * 512:(q + 1) * 512]])
        P.dma("gpsimd", lambda g: g.dma_start(
            out=gxs[:], in_=g_dram.ap()[0:17, :]),
            reads=[g_dram.ap()[0:17, :]], writes=[gxs[:]])

        y_sb = PB.view(R_AP, F32, (KT, T))
        hbuf = PB.view(R_XB + 4096, BF16, (2, KF, TB))
        gb = PB.view(R_XB + 12 * 1024, F32, (2, TB))
        for tb in range(NTB):
            sl = slice(tb * TB, (tb + 1) * TB)
            for half in range(2):
                P.op("tensor", lambda t, half=half, sl=sl: [t.matmul(
                    PSB[:, k4, :], b2xs[:, (half * 4 + k4) * 128:
                                        (half * 4 + k4 + 1) * 128],
                    gxs[:, sl], start=True, stop=True) for k4 in range(4)],
                    reads=[b2xs[:], gxs[:, sl]],
                    writes=[PSB[:]], noself_tensor=True)
                P.op("vector", lambda v, half=half, sl=sl: v.tensor_copy(
                    y_sb[:, half * 4:half * 4 + 4, sl], PSB[:]),
                    reads=[PSB[:]], writes=[y_sb[:, half * 4:half * 4 + 4, sl]])

        slab1 = [PB.view(R_SF + i * 24 * 1024, BF16, (KT, DFF)) for i in range(2)]
        slab3 = [PB.view(R_SF + i * 24 * 1024 + 8 * 1024, BF16, (KT, DFF))
                 for i in range(2)]
        slab2 = [PB.view(R_SF + i * 24 * 1024 + 16 * 1024, BF16, (KF, D))
                 for i in range(2)]
        slabA = [PB.view(R_SF + i * 24 * 1024, BF16, (3 * KT * DFF,))
                 for i in range(2)]
        for e in range(NEXP):
            pe = e % 2
            P.dma("gpsimd", lambda g, e=e, pe=pe: g.dma_start(
                out=slabA[pe], in_=weP.ap()[e]), writes=[slabA[pe]])
            for tb in range(NTB):
                sl = slice(tb * TB, (tb + 1) * TB)
                ph = tb % 2
                P.dma("sync", lambda s, e=e, sl=sl, ph=ph: s.dma_start(
                    out=gb[:, ph:ph + 1, :],
                    in_=g_dram.ap()[e:e + 1, sl].partition_broadcast(128)),
                    reads=[g_dram.ap()[e:e + 1, sl]], writes=[gb[:, ph, :]])
                for kf in range(KF):
                    P.op("tensor", lambda t, pe=pe, sl=sl, kf=kf: [t.matmul(
                        PSA[:, kf, :],
                        slab1[pe][:, kin, kf * 128:(kf + 1) * 128],
                        x1b[:, kin, sl], start=(kin == 0),
                        stop=(kin == KT - 1)) for kin in range(KT)],
                        reads=[slab1[pe][:, :, kf * 128:(kf + 1) * 128],
                               x1b[:, :, sl]],
                        writes=[PSA[:, kf, :]], noself_tensor=True)
                for kf in range(KF):
                    P.op("scalar", lambda s, e=e, kf=kf: s.activation(
                        scs[:, kf, :], PSA[:, kf, :], AF.Silu,
                        bias=b1s[:, e * KF + kf:e * KF + kf + 1]),
                        reads=[PSA[:, kf, :],
                               b1s[:, e * KF + kf:e * KF + kf + 1]],
                        writes=[scs[:, kf, :]])
                for kf in range(KF):
                    P.op("tensor", lambda t, pe=pe, sl=sl, kf=kf: [t.matmul(
                        PSB[:, kf, :],
                        slab3[pe][:, kin, kf * 128:(kf + 1) * 128],
                        x1b[:, kin, sl], start=(kin == 0),
                        stop=(kin == KT - 1)) for kin in range(KT)],
                        reads=[slab3[pe][:, :, kf * 128:(kf + 1) * 128],
                               x1b[:, :, sl]],
                        writes=[PSB[:, kf, :]], noself_tensor=True)
                for kf in range(KF):
                    P.op("vector", lambda v, e=e, kf=kf, ph=ph:
                         v.scalar_tensor_tensor(
                             scs[:, 4 + (kf % 2), :], PSB[:, kf, :],
                             b3s[:, e * KF + kf:e * KF + kf + 1], gb[:, ph, :],
                             ALU.add, ALU.mult),
                         reads=[PSB[:, kf, :], gb[:, ph, :],
                                b3s[:, e * KF + kf:e * KF + kf + 1]],
                         writes=[scs[:, 4 + (kf % 2), :]])
                    P.op("vector", lambda v, kf=kf, ph=ph: v.tensor_mul(
                        hbuf[:, ph, kf, :], scs[:, kf, :],
                        scs[:, 4 + (kf % 2), :]),
                        reads=[scs[:, kf, :], scs[:, 4 + (kf % 2), :]],
                        writes=[hbuf[:, ph, kf, :]])
                for half in range(2):
                    P.op("tensor", lambda t, pe=pe, half=half, ph=ph: [t.matmul(
                        PSB[:, k4, :],
                        slab2[pe][:, kf, (half * 4 + k4) * 128:
                                  (half * 4 + k4 + 1) * 128],
                        hbuf[:, ph, kf, :], start=(kf == 0),
                        stop=(kf == KF - 1))
                        for k4 in range(4) for kf in range(KF)],
                        reads=[slab2[pe][:], hbuf[:, ph]],
                        writes=[PSB[:]], noself_tensor=True)
                    P.op("vector", lambda v, half=half, sl=sl: v.tensor_add(
                        y_sb[:, half * 4:half * 4 + 4, sl],
                        y_sb[:, half * 4:half * 4 + 4, sl], PSB[:]),
                        reads=[y_sb[:, half * 4:half * 4 + 4, sl], PSB[:]],
                        writes=[y_sb[:, half * 4:half * 4 + 4, sl]])

        x1st = PB.view(R_SF + 16 * 1024, F32, (KT, TB))
        x2bst = PB.view(R_PS, BF16, (KT, TB))
        xrb2 = PB.view(R_SF, BF16, (KT, TB))
        sqb2 = PB.view(R_SF + 8 * 1024, BF16, (KT, TB))
        for tb in range(NTB):
            sl = slice(tb * TB, (tb + 1) * TB)
            P.dma("sync", lambda s, tb=tb: s.dma_start(
                out=x1st, in_=x1_dram.ap()[:, tb]),
                reads=[x1_dram.ap()[:, tb]], writes=[x1st])
            ysl = y_sb[:, :, sl]
            P.op("vector", lambda v, ysl=ysl: v.tensor_add(ysl, ysl, x1st[:]),
                 reads=[ysl, x1st[:]], writes=[ysl])
            _layer_norm(P, ysl, xrb2, sqb2, ones_bf, scs, PSA, PSB)
            P.dma("sync", lambda s, tb=tb, ysl=ysl: s.dma_start(
                out=x2_out.ap()[:, tb], in_=ysl), reads=[ysl])
            P.op("scalar", lambda s, ysl=ysl: s.copy(x2bst[:], ysl),
                 reads=[ysl], writes=[x2bst[:]])
            P.dma("sync", lambda s, tb=tb: s.dma_start(
                out=x2b_dram.ap()[:, tb], in_=x2bst[:]),
                reads=[x2bst[:]], writes=[x2b_dram.ap()[:, tb]])

        if upto == 2:
            P.op("sync", lambda s: s.nop(), deps=P.queue_total_deps("sync"),
                 inc=0)
            P.emit(nc, ctx)
            return nc

        wl = PB.view(R_SF, BF16, (4, KT, D))
        for i, wsrc in enumerate((wrP, wk2P, wv2P, wwP)):
            P.dma("gpsimd", lambda g, i=i, wsrc=wsrc: g.dma_start(
                out=wl[:, i], in_=wsrc.ap()), writes=[wl[:, i]])
        x2b_ch = [PB.view(R_XB + i * 8 * 1024, BF16, (KT, TB)) for i in range(2)]
        r_ch = PB.view(R_AP, F32, (KT, TB))
        w_ch = PB.view(R_AP + 16 * 1024, F32, (KT, TB))
        z_ch = PB.view(R_AP + 32 * 1024, F32, (KT, TB))
        u_scr = PB.view(R_AP + 48 * 1024, F32, (KT, TB))
        k_bf = PB.view(R_PS, BF16, (KT, TB))

        for cc in range(NTB):
            ph = cc % 2
            P.dma("sync", lambda s, cc=cc, ph=ph: s.dma_start(
                out=x2b_ch[ph], in_=x2b_dram.ap()[:, cc]),
                reads=[x2b_dram.ap()[:, cc]], writes=[x2b_ch[ph]])

            def proj(widx, outbuf, use_a, act=None, out_bf=False, ph=ph):
                PSx = PSA if use_a else PSB
                for kto in range(KT):
                    bank = kto % 4
                    P.op("tensor", lambda t, widx=widx, kto=kto,
                         bank=bank, PSx=PSx: [t.matmul(
                             PSx[:, bank, :],
                             wl[:, widx, kin, kto * 128:(kto + 1) * 128],
                             x2b_ch[ph][:, kin, :], start=(kin == 0),
                             stop=(kin == KT - 1)) for kin in range(KT)],
                         reads=[wl[:, widx], x2b_ch[ph][:]],
                         writes=[PSx[:, bank, :]], noself_tensor=True)
                    if act is not None:
                        P.op("scalar", lambda s, kto=kto, bank=bank, PSx=PSx:
                             s.activation(outbuf[:, kto, :], PSx[:, bank, :],
                                          act),
                             reads=[PSx[:, bank, :]], writes=[outbuf[:, kto, :]])
                    elif out_bf:
                        P.op("scalar", lambda s, kto=kto, bank=bank, PSx=PSx:
                             s.copy(outbuf[:, kto, :], PSx[:, bank, :]),
                             reads=[PSx[:, bank, :]], writes=[outbuf[:, kto, :]])
                    else:
                        P.op("vector", lambda v, kto=kto, bank=bank, PSx=PSx:
                             v.tensor_copy(outbuf[:, kto, :], PSx[:, bank, :]),
                             reads=[PSx[:, bank, :]], writes=[outbuf[:, kto, :]])

            proj(0, r_ch, True, act=AF.Sigmoid)
            P.dma("sync", lambda s, cc=cc: s.dma_start(
                out=r_out.ap()[:, cc], in_=r_ch[:]), reads=[r_ch[:]])
            proj(3, w_ch, False, act=AF.Sigmoid)
            for kt in range(KT):
                P.op("vector", lambda v, kt=kt: v.tensor_scalar_mul(
                    w_ch[:, kt, :], w_ch[:, kt, :], 0.99),
                    reads=[w_ch[:, kt, :]], writes=[w_ch[:, kt, :]])
            P.dma("sync", lambda s, cc=cc: s.dma_start(
                out=w_out.ap()[:, cc], in_=w_ch[:]), reads=[w_ch[:]])
            proj(1, k_bf, True, out_bf=True)
            proj(2, z_ch, False)
            for kt in range(KT):
                P.op("tensor", lambda t, kt=kt: t.matmul(
                    PSA[:, kt % 4, :], bdiag[:], k_bf[:, kt, :],
                    start=True, stop=True),
                    reads=[bdiag[:], k_bf[:, kt, :]],
                    writes=[PSA[:, kt % 4, :]])
                P.op("vector", lambda v, kt=kt: v.tensor_mul(
                    z_ch[:, kt, :], z_ch[:, kt, :], PSA[:, kt % 4, :]),
                    reads=[z_ch[:, kt, :], PSA[:, kt % 4, :]],
                    writes=[z_ch[:, kt, :]])
            P.dma("sync", lambda s, cc=cc: s.dma_start(
                out=z_out.ap()[:, cc], in_=z_ch[:]), reads=[z_ch[:]])
            for kt in range(KT):
                init_u = 0.0 if cc == 0 else carryU[:, kt:kt + 1]
                rd = [w_ch[:, kt, :], z_ch[:, kt, :]]
                if cc > 0:
                    rd.append(carryU[:, kt:kt + 1])
                P.op("vector", lambda v, kt=kt, init_u=init_u:
                     v.tensor_tensor_scan(
                         u_scr[:, kt, :], w_ch[:, kt, :], z_ch[:, kt, :],
                         init_u, ALU.mult, ALU.add),
                     reads=rd, writes=[u_scr[:, kt, :]])
                P.op("vector", lambda v, kt=kt: v.tensor_copy(
                    carryU[:, kt:kt + 1], u_scr[:, kt, TB - 1:TB]),
                    reads=[u_scr[:, kt, TB - 1:TB]],
                    writes=[carryU[:, kt:kt + 1]])
                init_p = 1.0 if cc == 0 else carryP[:, kt:kt + 1]
                rdp = [w_ch[:, kt, :]]
                if cc > 0:
                    rdp.append(carryP[:, kt:kt + 1])
                P.op("vector", lambda v, kt=kt, init_p=init_p:
                     v.tensor_tensor_scan(
                         u_scr[:, kt, :], w_ch[:, kt, :], w_ch[:, kt, :],
                         init_p, ALU.mult, ALU.bypass),
                     reads=rdp, writes=[u_scr[:, kt, :]])
                P.op("vector", lambda v, kt=kt: v.tensor_copy(
                    carryP[:, kt:kt + 1], u_scr[:, kt, TB - 1:TB]),
                    reads=[u_scr[:, kt, TB - 1:TB]],
                    writes=[carryP[:, kt:kt + 1]])
        P.op("vector", lambda v: v.tensor_copy(ex1[:, 0:8], carryU[:]),
             reads=[carryU[:]], writes=[ex1[:, 0:8]])
        P.op("vector", lambda v: v.tensor_copy(ex1[:, 8:16], carryP[:]),
             reads=[carryP[:]], writes=[ex1[:, 8:16]])
        tst = P.dma("sync", lambda s: s.dma_start(
            out=ex_out.ap(), in_=ex1[:]), reads=[ex1[:]])
        P.op("sync", lambda s: s.nop(), deps=P.queue_total_deps("sync"),
             inc=0)

        P.emit(nc, ctx)
    return nc


def build_launch2(T):
    TB = 512
    NTB = T // TB
    TH = 2 * TB

    nc = bass.Bass("TRN2", target_bir_lowering=False)
    P = Prog()
    dt = nc.dram_tensor
    x2sp = dt("x2sp", [128, NTB, KT, TB], F32, kind="ExternalInput")
    rsp = dt("rsp", [128, NTB, KT, TB], F32, kind="ExternalInput")
    wsp = dt("wsp", [128, NTB, KT, TB], F32, kind="ExternalInput")
    zsp = dt("zsp", [128, NTB, KT, TB], F32, kind="ExternalInput")
    exall = dt("exall", [128, 8, 16], F32, kind="ExternalInput")
    cmsk = dt("cmsk", [128, 8], F32, kind="ExternalInput")
    cm1m = dt("cm1m", [128, 8], F32, kind="ExternalInput")
    woP = dt("woP", [128, KT, D], F32, kind="ExternalInput")
    tw12P = dt("tw12P", [128, 8, 2 * KT * DFF], F32, kind="ExternalInput")
    th1P = dt("th1P", [128, 16, KH, 256], F32, kind="ExternalInput")
    th2P = dt("th2P", [128, 16, KH, 256], F32, kind="ExternalInput")
    tw3P = dt("tw3P", [128, 4, KH, 256], F32, kind="ExternalInput")
    s1p = dt("s1p", [128, KH], F32, kind="ExternalInput")
    s2p = dt("s2p", [128, KH], F32, kind="ExternalInput")
    sh1p = dt("sh1p", [128, KH], F32, kind="ExternalInput")
    sh2p = dt("sh2p", [128, KH], F32, kind="ExternalInput")
    s3p = dt("s3p", [128, KT], F32, kind="ExternalInput")
    x4_out = dt("x4_out", [128, NTB, KT, TB], F32, kind="ExternalOutput")
    x3_dram = dt("x3_dram", [128, NTB, KT, TB], F32)

    with ExitStack() as ctx:
        PB = Pool(nc, ctx, "PB", 176 * 1024)
        R1, R2, R3 = 0, 64 * 1024, 128 * 1024

        sb = lambda n, s, d: ctx.enter_context(nc.sbuf_tensor(n, s, d))
        ones_bf = sb("ones_bf", [128, 128], BF16)
        scs = sb("scs", [128, 2, TB], F32)
        exs = sb("exs", [128, 8, 16], F32)
        cms = sb("cms", [128, 8], F32)
        cm1 = sb("cm1", [128, 8], F32)
        acc = sb("acc", [128, 8], F32)
        tmp8 = sb("tmp8", [128, 2, 8], F32)
        carryD = sb("carryD", [128, KT], F32)
        wo_sb = sb("wo_sb", [128, KT, D], BF16)
        scps = sb("scps", [128, 5, KH], F32)
        s1h = sb("s1h", [128, 2, TB], BF16)
        t2f = sb("t2f", [128, 2, TB], F32)

        PSA = ctx.enter_context(nc.psum_tensor("psa", [128, 4, 512], F32))
        PSB = ctx.enter_context(nc.psum_tensor("psb", [128, 4, 512], F32))
        nrm = PB.view(R2, BF16, (KT, TB))
        sqbD = PB.view(R2 + 8 * 1024, BF16, (KT, TB))

        P.op("gpsimd", lambda g: g.memset(ones_bf[:], 1.0), writes=[ones_bf[:]])
        P.dma("sync", lambda s: s.dma_start(out=exs[:], in_=exall.ap()),
              writes=[exs[:]])
        P.dma("sync", lambda s: s.dma_start(out=cms[:], in_=cmsk.ap()),
              writes=[cms[:]])
        P.dma("sync", lambda s: s.dma_start(out=cm1[:], in_=cm1m.ap()),
              writes=[cm1[:]])
        P.dma("gpsimd", lambda g: g.dma_start(out=wo_sb[:], in_=woP.ap()),
              writes=[wo_sb[:]])
        for i, (src, dst) in enumerate(
                ((s1p, 0), (s2p, 1), (sh1p, 2), (sh2p, 3))):
            P.dma("sync", lambda s, src=src, dst=dst: s.dma_start(
                out=scps[:, dst, :], in_=src.ap()), writes=[scps[:, dst, :]])
        P.dma("sync", lambda s: s.dma_start(
            out=scps[:, 4, 0:KT], in_=s3p.ap()), writes=[scps[:, 4, 0:KT]])

        P.op("vector", lambda v: v.memset(acc[:], 0.0), writes=[acc[:]])
        for c in range(NCORE):
            P.op("vector", lambda v, c=c: v.tensor_scalar(
                tmp8[:, 0, :], exs[:, c, 8:16], cms[:, c:c + 1],
                cm1[:, c:c + 1], ALU.mult, ALU.add),
                reads=[exs[:, c, 8:16], cms[:, c:c + 1], cm1[:, c:c + 1]],
                writes=[tmp8[:, 0, :]])
            P.op("vector", lambda v, c=c: v.tensor_scalar(
                tmp8[:, 1, :], exs[:, c, 0:8], cms[:, c:c + 1], None, ALU.mult),
                reads=[exs[:, c, 0:8], cms[:, c:c + 1]], writes=[tmp8[:, 1, :]])
            P.op("vector", lambda v: v.tensor_mul(
                acc[:], acc[:], tmp8[:, 0, :]),
                reads=[acc[:], tmp8[:, 0, :]], writes=[acc[:]])
            P.op("vector", lambda v: v.tensor_add(
                acc[:], acc[:], tmp8[:, 1, :]),
                reads=[acc[:], tmp8[:, 1, :]], writes=[acc[:]])

        w_ch = PB.view(R1, F32, (KT, TB))
        z_ch = PB.view(R1 + 16 * 1024, F32, (KT, TB))
        r_ch = PB.view(R1 + 32 * 1024, F32, (KT, TB))
        x2_ch = PB.view(R1 + 48 * 1024, F32, (KT, TB))
        for cc in range(NTB):
            P.dma("sync", lambda s, cc=cc: s.dma_start(
                out=w_ch[:], in_=wsp.ap()[:, cc]), writes=[w_ch[:]])
            P.dma("sync", lambda s, cc=cc: s.dma_start(
                out=z_ch[:], in_=zsp.ap()[:, cc]), writes=[z_ch[:]])
            P.dma("sync", lambda s, cc=cc: s.dma_start(
                out=r_ch[:], in_=rsp.ap()[:, cc]), writes=[r_ch[:]])
            P.dma("sync", lambda s, cc=cc: s.dma_start(
                out=x2_ch[:], in_=x2sp.ap()[:, cc]), writes=[x2_ch[:]])
            for kt in range(KT):
                init = acc[:, kt:kt + 1] if cc == 0 else carryD[:, kt:kt + 1]
                P.op("vector", lambda v, kt=kt, init=init: v.tensor_tensor_scan(
                    z_ch[:, kt, :], w_ch[:, kt, :], z_ch[:, kt, :], init,
                    ALU.mult, ALU.add),
                    reads=[w_ch[:, kt, :], z_ch[:, kt, :], init],
                    writes=[z_ch[:, kt, :]])
                P.op("vector", lambda v, kt=kt: v.tensor_copy(
                    carryD[:, kt:kt + 1], z_ch[:, kt, TB - 1:TB]),
                    reads=[z_ch[:, kt, TB - 1:TB]], writes=[carryD[:, kt:kt + 1]])
                P.op("vector", lambda v, kt=kt: v.tensor_mul(
                    w_ch[:, kt, :], r_ch[:, kt, :], z_ch[:, kt, :]),
                    reads=[r_ch[:, kt, :], z_ch[:, kt, :]],
                    writes=[w_ch[:, kt, :]])
            P.op("scalar", lambda s: s.activation(sqbD[:], w_ch[:], AF.Square),
                 reads=[w_ch[:]], writes=[sqbD[:]])
            P.op("tensor", lambda t: [t.matmul(
                PSB[:, 2, :], ones_bf[:], sqbD[:, kt, :],
                start=(kt == 0), stop=(kt == KT - 1)) for kt in range(KT)],
                reads=[ones_bf[:], sqbD[:]], writes=[PSB[:, 2, :]],
                noself_tensor=True)
            P.op("vector", lambda v: v.tensor_scalar(
                scs[:, 1, :], PSB[:, 2, :], 1.0 / D, None, ALU.mult),
                reads=[PSB[:, 2, :]], writes=[scs[:, 1, :]])
            P.op("vector", lambda v: v.tensor_scalar_add(
                scs[:, 1, :], scs[:, 1, :], RMS_EPS),
                reads=[scs[:, 1, :]], writes=[scs[:, 1, :]])
            P.op("scalar", lambda s: s.activation(
                scs[:, 1, :], scs[:, 1, :], AF.Sqrt),
                reads=[scs[:, 1, :]], writes=[scs[:, 1, :]])
            P.op("vector", lambda v: v.reciprocal(scs[:, 1, :], scs[:, 1, :]),
                 reads=[scs[:, 1, :]], writes=[scs[:, 1, :]])
            for kt in range(KT):
                P.op("vector", lambda v, kt=kt: v.tensor_tensor(
                    nrm[:, kt, :], w_ch[:, kt, :], scs[:, 1, :], ALU.mult),
                    reads=[w_ch[:, kt, :], scs[:, 1, :]], writes=[nrm[:, kt, :]])
            for kto in range(KT):
                bank = kto % 2
                P.op("tensor", lambda t, kto=kto, bank=bank: [t.matmul(
                    PSA[:, bank, :], wo_sb[:, kin, kto * 128:(kto + 1) * 128],
                    nrm[:, kin, :], start=(kin == 0),
                    stop=(kin == KT - 1)) for kin in range(KT)],
                    reads=[wo_sb[:], nrm[:]],
                    writes=[PSA[:, bank, :]], noself_tensor=True)
                P.op("vector", lambda v, kto=kto, bank=bank:
                     v.scalar_tensor_tensor(
                         x2_ch[:, kto, :], PSA[:, bank, :], 1.0,
                         x2_ch[:, kto, :], ALU.mult, ALU.add),
                     reads=[PSA[:, bank, :], x2_ch[:, kto, :]],
                     writes=[x2_ch[:, kto, :]])
            _layer_norm(P, x2_ch, nrm, sqbD, ones_bf, scs, PSA, PSB, mu_i=0, rs_i=1)
            P.dma("sync", lambda s, cc=cc: s.dma_start(
                out=x3_dram.ap()[:, cc], in_=x2_ch[:]),
                reads=[x2_ch[:]], writes=[x3_dram.ap()[:, cc]])

        hB = PB.view(R1, BF16, (KH, TH))
        gB = PB.view(R2, BF16, (KH, TH))
        x3b = PB.view(R3, BF16, (2, KT, TB))
        slabs = [PB.view(R3 + 16 * 1024 + i * 16 * 1024, BF16, (KH, 256))
                 for i in range(2)]
        slabw = [PB.view(R3 + 16 * 1024 + i * 8 * 1024, BF16, (KT, DFF))
                 for i in range(4)]
        xr4 = PB.view(R2 + 16 * 1024, F32, (2, KT, TB))

        x4_toks = []
        for hf in range(NTB // 2):
            P.dma("gpsimd", lambda g, hf=hf: g.dma_start(
                out=x3b[:], in_=x3_dram.ap()[:, 2 * hf:2 * hf + 2]),
                reads=[x3_dram.ap()[:, 2 * hf:2 * hf + 2]], writes=[x3b[:]])
            slabw12 = [PB.view(R3 + 16 * 1024 + i * 16 * 1024, BF16,
                               (2 * KT * DFF,)) for i in range(2)]
            for dc in range(8):
                p1s = slabw[(dc % 2) * 2]
                p2s = slabw[(dc % 2) * 2 + 1]
                P.dma("gpsimd", lambda g, dc=dc, w12=slabw12[dc % 2]:
                      g.dma_start(out=w12, in_=tw12P.ap()[:, dc]),
                      writes=[slabw12[dc % 2]])
                for k4 in range(KF):
                    dk = dc * 4 + k4
                    for tbh in range(2):
                        P.op("tensor", lambda t, p1s=p1s, k4=k4,
                             tbh=tbh: [t.matmul(
                                 PSA[:, k4 % 2 + 2 * tbh, :],
                                 p1s[:, kin, k4 * 128:(k4 + 1) * 128],
                                 x3b[:, tbh, kin, :], start=(kin == 0),
                                 stop=(kin == KT - 1)) for kin in range(KT)],
                             reads=[p1s[:], x3b[:, tbh]],
                             writes=[PSA[:, k4 % 2 + 2 * tbh, :]],
                             noself_tensor=True)
                        P.op("tensor", lambda t, p2s=p2s, k4=k4,
                             tbh=tbh: [t.matmul(
                                 PSB[:, k4 % 2 + 2 * tbh, :],
                                 p2s[:, kin, k4 * 128:(k4 + 1) * 128],
                                 x3b[:, tbh, kin, :], start=(kin == 0),
                                 stop=(kin == KT - 1)) for kin in range(KT)],
                             reads=[p2s[:], x3b[:, tbh]],
                             writes=[PSB[:, k4 % 2 + 2 * tbh, :]],
                             noself_tensor=True)
                        bnk = k4 % 2 + 2 * tbh
                        P.op("scalar", lambda s, dk=dk, bnk=bnk, tbh=tbh:
                             s.activation(s1h[:, tbh, :], PSA[:, bnk, :],
                                          AF.Silu,
                                          scale=scps[:, 0, dk:dk + 1]),
                             reads=[PSA[:, bnk, :], scps[:, 0, dk:dk + 1]],
                             writes=[s1h[:, tbh, :]])
                        P.op("vector", lambda v, dk=dk, bnk=bnk, tbh=tbh:
                             v.tensor_scalar(
                                 t2f[:, tbh, :], PSB[:, bnk, :],
                                 scps[:, 1, dk:dk + 1], None, ALU.mult),
                             reads=[PSB[:, bnk, :], scps[:, 1, dk:dk + 1]],
                             writes=[t2f[:, tbh, :]])
                        P.op("vector", lambda v, dk=dk, tbh=tbh: v.tensor_mul(
                            hB[:, dk, tbh * TB:(tbh + 1) * TB],
                            s1h[:, tbh, :], t2f[:, tbh, :]),
                            reads=[s1h[:, tbh, :], t2f[:, tbh, :]],
                            writes=[hB[:, dk, tbh * TB:(tbh + 1) * TB]])

            def hidden(P_, src, dst, wsrc, prow):
                for dc2 in range(16):
                    sl_ = slabs[dc2 % 2]
                    P_.dma("gpsimd", lambda g, dc2=dc2, sl_=sl_: g.dma_start(
                        out=sl_, in_=wsrc.ap()[:, dc2]), writes=[sl_])
                    for sub in range(2):
                        dk = dc2 * 2 + sub
                        for tbh in range(2):
                            bnk = sub + 2 * tbh
                            P_.op("tensor", lambda t, sl_=sl_, sub=sub,
                                  tbh=tbh, bnk=bnk: [t.matmul(
                                      PSA[:, bnk, :],
                                      sl_[:, kin, sub * 128:(sub + 1) * 128],
                                      src[:, kin, tbh * TB:(tbh + 1) * TB],
                                      start=(kin == 0),
                                      stop=(kin == KH - 1))
                                      for kin in range(KH)],
                                  reads=[sl_[:],
                                         src[:, :, tbh * TB:(tbh + 1) * TB]],
                                  writes=[PSA[:, bnk, :]],
                                  noself_tensor=True)
                            P_.op("scalar", lambda s, dk=dk, bnk=bnk, tbh=tbh:
                                  s.activation(
                                      dst[:, dk, tbh * TB:(tbh + 1) * TB],
                                      PSA[:, bnk, :], AF.Gelu,
                                      scale=scps[:, prow, dk:dk + 1]),
                                  reads=[PSA[:, bnk, :],
                                         scps[:, prow, dk:dk + 1]],
                                  writes=[dst[:, dk,
                                              tbh * TB:(tbh + 1) * TB]])
            hidden(P, hB, gB, th1P, 2)
            hidden(P, gB, hB, th2P, 3)

            for tbh in range(2):
                P.dma("sync", lambda s, hf=hf, tbh=tbh: s.dma_start(
                    out=xr4[:, tbh], in_=x3_dram.ap()[:, 2 * hf + tbh]),
                    reads=[x3_dram.ap()[:, 2 * hf + tbh]],
                    writes=[xr4[:, tbh]])
            for dc3 in range(4):
                sl_ = slabs[dc3 % 2]
                P.dma("gpsimd", lambda g, dc3=dc3, sl_=sl_: g.dma_start(
                    out=sl_, in_=tw3P.ap()[:, dc3]), writes=[sl_])
                for sub in range(2):
                    dk = dc3 * 2 + sub
                    for tbh in range(2):
                        bnk = sub + 2 * tbh
                        P.op("tensor", lambda t, sl_=sl_, sub=sub,
                             tbh=tbh, bnk=bnk: [t.matmul(
                                 PSB[:, bnk, :],
                                 sl_[:, kin, sub * 128:(sub + 1) * 128],
                                 hB[:, kin, tbh * TB:(tbh + 1) * TB],
                                 start=(kin == 0), stop=(kin == KH - 1))
                                 for kin in range(KH)],
                             reads=[sl_[:],
                                    hB[:, :, tbh * TB:(tbh + 1) * TB]],
                             writes=[PSB[:, bnk, :]], noself_tensor=True)
                        P.op("vector", lambda v, dk=dk, bnk=bnk, tbh=tbh:
                             v.scalar_tensor_tensor(
                                 xr4[:, tbh, dk, :], PSB[:, bnk, :],
                                 scps[:, 4, dk:dk + 1], xr4[:, tbh, dk, :],
                                 ALU.mult, ALU.add),
                             reads=[PSB[:, bnk, :], scps[:, 4, dk:dk + 1],
                                    xr4[:, tbh, dk, :]],
                             writes=[xr4[:, tbh, dk, :]])
            for tbh in range(2):
                cc = 2 * hf + tbh
                _layer_norm(P, xr4[:, tbh], nrm, sqbD, ones_bf, scs, PSA, PSB, mu_i=0, rs_i=1)
                tk = P.dma("sync", lambda s, cc=cc, tbh=tbh: s.dma_start(
                    out=x4_out.ap()[:, cc], in_=xr4[:, tbh]),
                    reads=[xr4[:, tbh]])
                x4_toks.append(tk)
        P.op("sync", lambda s: s.nop(), deps=P.queue_total_deps("sync"),
             inc=0)

        P.emit(nc, ctx)
    return nc



def _img(w):
    K = w.shape[0] // 128
    return np.ascontiguousarray(
        w.reshape(K, 128, w.shape[1]).transpose(1, 0, 2))


def _img_chunks(w, nch):
    K = w.shape[0] // 128
    N = w.shape[1]
    return np.ascontiguousarray(
        w.reshape(K, 128, nch, N // nch).transpose(1, 2, 0, 3))


_NC_CACHE = {}


def kernel(L0_attn_x, L0_attn_Wq, L0_attn_Wk, L0_attn_Wv, L0_ffn_gate_w,
           L0_ffn_expert_bias, L0_ffn_w1, L0_ffn_b1, L0_ffn_w2, L0_ffn_b2,
           L0_ffn_w3, L0_ffn_b3, L0_ffn_sw1, L0_ffn_sb1, L0_ffn_sw2,
           L0_ffn_sb2, L0_ffn_sw3, L0_ffn_sb3, L1_attn_Wr, L1_attn_Wk,
           L1_attn_Wv, L1_attn_Ww, L1_attn_Wo, L1_ffn_w1_w, L1_ffn_w1_s,
           L1_ffn_w2_w, L1_ffn_w2_s, L1_ffn_h1_w, L1_ffn_h1_s, L1_ffn_h2_w,
           L1_ffn_h2_s, L1_ffn_w3_w, L1_ffn_w3_s):
    TFULL = L0_attn_x.shape[1]
    T = TFULL // NCORE
    TB = 512
    NTB = T // TB
    NCH = TFULL // T

    f = np.float32
    x_cm = np.ascontiguousarray(L0_attn_x[0].T.astype(f, copy=False))
    xfull_img = np.ascontiguousarray(
        x_cm.reshape(KT, 128, NCH, T).transpose(1, 2, 0, 3))

    wqP = _img(L0_attn_Wq.astype(f, copy=False))
    wkP = _img(L0_attn_Wk.astype(f, copy=False))
    wvP = _img(L0_attn_Wv.astype(f, copy=False))
    gateP = _img(np.ascontiguousarray(L0_ffn_gate_w.T.astype(f, copy=False)))
    ebias = np.ascontiguousarray(
        np.broadcast_to(L0_ffn_expert_bias.astype(f), (128, E_MOE)))

    w1T = L0_ffn_w1.transpose(0, 2, 1).astype(f, copy=False)
    w3T = L0_ffn_w3.transpose(0, 2, 1).astype(f, copy=False)
    w2T = L0_ffn_w2.transpose(0, 2, 1).astype(f, copy=False)
    sw1T = L0_ffn_sw1.T.astype(f, copy=False)
    sw3T = L0_ffn_sw3.T.astype(f, copy=False)
    sw2T = L0_ffn_sw2.T.astype(f, copy=False)
    we1T = np.concatenate(
        [w1T, sw1T[None, :, 0:DFF], sw1T[None, :, DFF:2 * DFF]], 0)
    we3T = np.concatenate(
        [w3T, sw3T[None, :, 0:DFF], sw3T[None, :, DFF:2 * DFF]], 0)
    we2T = np.concatenate(
        [w2T, sw2T[None, 0:DFF, :], sw2T[None, DFF:2 * DFF, :]], 0)
    weP = np.ascontiguousarray(np.stack(
        [np.concatenate([_img(we1T[e]).reshape(128, -1),
                         _img(we3T[e]).reshape(128, -1),
                         _img(we2T[e]).reshape(128, -1)], axis=1)
         for e in range(NEXP)]))

    def packb(b):
        n = b.shape[0] // 128
        return np.ascontiguousarray(b.reshape(n, 128).T.astype(f, copy=False))

    b1cat = np.concatenate([L0_ffn_b1.reshape(-1), L0_ffn_sb1], 0)
    b3cat = np.concatenate([L0_ffn_b3.reshape(-1), L0_ffn_sb3], 0)
    b1p = packb(b1cat)
    b3p = packb(b3cat)
    b2xT = np.ascontiguousarray(
        np.concatenate([L0_ffn_b2, L0_ffn_sb2[None]], 0).astype(f, copy=False))

    wrP = _img(np.ascontiguousarray(L1_attn_Wr.T.astype(f, copy=False)))
    wk2P = _img(np.ascontiguousarray(L1_attn_Wk.T.astype(f, copy=False)))
    wv2P = _img(np.ascontiguousarray(L1_attn_Wv.T.astype(f, copy=False)))
    wwP = _img(np.ascontiguousarray(L1_attn_Ww.T.astype(f, copy=False)))
    woP = _img(np.ascontiguousarray(L1_attn_Wo.T.astype(f, copy=False)))
    _t1 = _img_chunks(np.ascontiguousarray(L1_ffn_w1_w.T.astype(f, copy=False)), 8)
    _t2 = _img_chunks(np.ascontiguousarray(L1_ffn_w2_w.T.astype(f, copy=False)), 8)
    tw12P = np.ascontiguousarray(np.concatenate(
        [_t1.reshape(128, 8, -1), _t2.reshape(128, 8, -1)], axis=2))
    th1P = _img_chunks(np.ascontiguousarray(L1_ffn_h1_w.T.astype(f, copy=False)), 16)
    th2P = _img_chunks(np.ascontiguousarray(L1_ffn_h2_w.T.astype(f, copy=False)), 16)
    tw3P = _img_chunks(np.ascontiguousarray(L1_ffn_w3_w.T.astype(f, copy=False)), 4)
    s1p = packb(L1_ffn_w1_s)
    s2p = packb(L1_ffn_w2_s)
    sh1p = packb(L1_ffn_h1_s)
    sh2p = packb(L1_ffn_h2_s)
    s3p = packb(L1_ffn_w3_s)

    key = ("l1", T)
    if key not in _NC_CACHE:
        _NC_CACHE[key] = build_launch1(T)
    nc1 = _NC_CACHE[key]

    shared1 = dict(wqP=wqP, wkP=wkP, wvP=wvP, gateP=gateP, ebias=ebias,
                   weP=weP, b1p=b1p, b3p=b3p,
                   b2xT=b2xT, wrP=wrP, wk2P=wk2P, wv2P=wv2P, wwP=wwP,
                   xfull=xfull_img)
    in_maps1 = []
    for c in range(NCORE):
        xl = np.ascontiguousarray(
            x_cm[:, c * T:(c + 1) * T].reshape(KT, 128, NTB, TB)
            .transpose(1, 2, 0, 3))
        m = dict(shared1)
        m["xloc"] = xl
        in_maps1.append(m)
    res1 = run_bass_kernel_spmd(nc1, in_maps1, core_ids=list(range(NCORE)))

    exall = np.ascontiguousarray(
        np.stack([res1.results[c]["ex_out"] for c in range(NCORE)], axis=1))

    key2 = ("l2", T)
    if key2 not in _NC_CACHE:
        _NC_CACHE[key2] = build_launch2(T)
    nc2 = _NC_CACHE[key2]

    shared2 = dict(exall=exall, woP=woP, tw12P=tw12P, th1P=th1P,
                   th2P=th2P, tw3P=tw3P, s1p=s1p, s2p=s2p, sh1p=sh1p,
                   sh2p=sh2p, s3p=s3p)
    in_maps2 = []
    for c in range(NCORE):
        cm = np.zeros((128, 8), f)
        cm[:, :c] = 1.0
        m = dict(shared2)
        m["x2sp"] = res1.results[c]["x2_out"]
        m["rsp"] = res1.results[c]["r_out"]
        m["wsp"] = res1.results[c]["w_out"]
        m["zsp"] = res1.results[c]["z_out"]
        m["cmsk"] = cm
        m["cm1m"] = np.ascontiguousarray(1.0 - cm)
        in_maps2.append(m)
    res2 = run_bass_kernel_spmd(nc2, in_maps2, core_ids=list(range(NCORE)))

    out = np.empty((1, TFULL, D), np.float32)
    for c in range(NCORE):
        img = res2.results[c]["x4_out"]
        xc = img.transpose(2, 0, 1, 3).reshape(D, T)
        out[0, c * T:(c + 1) * T, :] = xc.T
    return out
